# version 26
# baseline (speedup 1.0000x reference)
import contextlib
import numpy as np
import ml_dtypes
import concourse.bass as bass
import concourse.mybir as mybir
from concourse.bass_utils import run_bass_kernel_spmd
from concourse.bass import IndirectOffsetOnAxis

F32 = mybir.dt.float32
F8 = mybir.dt.float8e4
BF16 = mybir.dt.bfloat16
I32 = mybir.dt.int32
AF = mybir.ActivationFunctionType
ALU = mybir.AluOpType
AX = mybir.AxisListType

N_CORES = 8
D = 1024
NH = 8
C = 128
I_FFN = 4096
IC = 512
NIC = I_FFN // IC
SHW = D * 512
LN2C = float(np.log(2.0))
GSW_CONST = 448.0 * 6.0

T_FULL = 8192
T_HALF = 4096
NCH = 64
NSL = 32
NTT = 32
BQ = 4
NQ = 4
FTB = 4
NTB = 8


def set_dims(t_full):
    global T_FULL, T_HALF, NCH, NSL, NTT, BQ, FTB, NTB
    T_FULL = t_full
    T_HALF = t_full // 2
    NCH = T_FULL // C
    NSL = NCH // 2
    NTT = T_HALF // C
    BQ = min(4, NTT)
    FTB = max(1, min(4, NTT // 2))
    NTB = NTT // FTB
    assert NTB % 2 == 0
    global NQ
    NQ = 4 if (T_HALF // 4) % 128 == 0 and NSL % 4 == 0 else 2


set_dims(8192)
DEBUG = False
DBG_CI = 0

ENGINES = ("t", "v", "s", "sy", "g")


class Prog:

    def __init__(self):
        self.streams = {e: [] for e in ENGINES}
        self.cnt = {}
        self.marks = {}
        self.sems = {}

    def sem(self, name, handle):
        self.sems[name] = handle
        self.cnt[name] = 0

    def bare(self, eng, fn):
        self.streams[eng].append(fn)

    def inc(self, eng, fn, sem, n):
        s = self.sems[sem]
        self.cnt[sem] += n
        self.streams[eng].append(lambda h: fn(h).then_inc(s, n))

    def mark(self, name, sem):
        self.marks[name] = (sem, self.cnt[sem])

    def wait(self, eng, name):
        sem_name, val = self.marks[name]
        s = self.sems[sem_name]
        if val > 0:
            self.streams[eng].append(lambda h: h.wait_ge(s, val))

    def drain(self, eng):
        self.streams[eng].append(lambda h: h.drain())

    def wait_now(self, eng, sem):
        s = self.sems[sem]
        val = self.cnt[sem]
        if val > 0:
            self.streams[eng].append(lambda h: h.wait_ge(s, val))

    def fence_inc(self, eng, sem, n=1):
        s = self.sems[sem]
        self.cnt[sem] += n
        fn = self.streams[eng].pop()
        self.streams[eng].append(lambda h, fn=fn: fn(h).then_inc(s, n))

    def fence_nop(self, eng, sem, n=1):
        s = self.sems[sem]
        self.cnt[sem] += n
        self.streams[eng].append(lambda h: h.drain())
        self.streams[eng].append(lambda h: h.nop().then_inc(s, n))


def emit_quant_multi_a(P, streams):
    par = getattr(P, "q8c", 0) % 2
    P.q8c = getattr(P, "q8c", 0) + 1
    st = []
    for s_ in streams:
        nblk = s_["nel"] // 16
        sml = s_["sml"]
        d = dict(s_)
        d["nblk"] = nblk
        d["am"] = sml[:, 0 * nblk:1 * nblk]
        d["scb"] = sml[:, 1 * nblk:2 * nblk]
        d["sch"] = sml[:, 2 * nblk:3 * nblk]
        d["sa"] = sml[:, 3 * nblk:4 * nblk]
        d["sb"] = sml[:, 4 * nblk:5 * nblk]
        d["sel"] = sml[:, 5 * nblk:6 * nblk]
        d["u"] = sml[:, 6 * nblk:7 * nblk]
        d["dq"] = sml[:, 7 * nblk:8 * nblk]
        d["c8a"] = s_["s8"][:, 0:nblk]
        d["c8b"] = s_["s8"][:, nblk:2 * nblk]
        d["x3"] = s_["xin"].rearrange("p (b s) -> p b s", s=16)
        st.append(d)

    def each(fn):
        for d in st:
            fn(d)

    tag0 = st[0]["tag"]
    P.drain("v")
    for d in st:
        if "pre_amax" in d:
            lt = d["pre_amax"]
            P.bare("v", lambda v, d=d, lt=lt: v.tensor_reduce(
                d["am"][:, lt[0]:lt[1]], lt[2], AX.X, ALU.max, apply_absolute_value=True))
        else:
            P.bare("v", lambda v, d=d: v.tensor_reduce(
                d["am"], d["x3"], AX.X, ALU.max, apply_absolute_value=True))
    P.drain("v")
    each(lambda d: P.bare("v", lambda v, d=d: v.tensor_scalar(
        d["scb"], d["am"], d["gs6"], 192.0, ALU.mult, ALU.min)))
    each(lambda d: P.bare("v", lambda v, d=d: v.tensor_scalar(
        d["sch"], d["am"], d["gs6h"], 224.0, ALU.mult, ALU.min)))
    P.fence_inc("v", "v")
    P.mark(f"{tag0}_sc", "v")
    P.wait("g", f"{tag0}_sc")
    each(lambda d: P.inc("g", lambda g, d=d: g.dma_start(out=d["c8a"], in_=d["sch"]),
                         f"f8d{par}", 16))
    each(lambda d: P.inc("g", lambda g, d=d: g.dma_start(out=d["c8b"], in_=d["scb"]),
                         f"f8d{par}", 16))
    P.mark(f"{tag0}_8d", f"f8d{par}")
    P.wait("g", f"{tag0}_8d")
    each(lambda d: P.inc("g", lambda g, d=d: g.dma_start(out=d["sa"], in_=d["c8a"]),
                         f"f8u{par}", 16))
    each(lambda d: P.inc("g", lambda g, d=d: g.dma_start(out=d["sb"], in_=d["c8b"]),
                         f"f8u{par}", 16))
    P.mark(f"{tag0}_8u", f"f8u{par}")
    return st


def emit_quant_multi_b(P, st):
    def each(fn):
        for d in st:
            fn(d)

    tag0 = st[0]["tag"]
    P.wait("v", f"{tag0}_8u")
    P.drain("v")
    each(lambda d: P.bare("v", lambda v, d=d: v.tensor_scalar(
        d["sa"], d["sa"], d["invgs2"], None, ALU.mult)))
    each(lambda d: P.bare("v", lambda v, d=d: v.tensor_scalar(
        d["sb"], d["sb"], d["invgs"], None, ALU.mult)))
    each(lambda d: P.bare("v", lambda v, d=d: v.tensor_scalar(
        d["sel"], d["scb"], 1.0, None, ALU.is_ge)))
    P.drain("v")
    each(lambda d: P.bare("v", lambda v, d=d: v.tensor_tensor(d["u"], d["sa"], d["sb"],
                                                              ALU.subtract)))
    P.drain("v")
    each(lambda d: P.bare("v", lambda v, d=d: v.tensor_tensor(d["u"], d["u"], d["sel"],
                                                              ALU.mult)))
    P.drain("v")
    each(lambda d: P.bare("v", lambda v, d=d: v.tensor_tensor(d["dq"], d["u"], d["sb"],
                                                              ALU.add)))
    P.drain("v")
    each(lambda d: P.bare("v", lambda v, d=d: v.tensor_scalar(
        d["am"], d["dq"], 1e-30, None, ALU.max)))
    P.drain("v")
    each(lambda d: P.bare("v", lambda v, d=d: v.reciprocal(d["sel"], d["am"])))
    P.drain("v")
    each(lambda d: P.bare("v", lambda v, d=d: v.tensor_tensor(
        d["x3"], d["x3"],
        d["sel"].unsqueeze(2).broadcast_to([128, d["nblk"], 16]), ALU.mult)))
    P.fence_inc("v", "v")
    P.mark(f"{tag0}_t", "v")
    P.wait("s", f"{tag0}_t")
    each(lambda d: P.bare("s", lambda s, d=d: s.activation(d["tA"], d["xin"], AF.Abs)))
    P.fence_inc("s", "s")
    P.mark(f"{tag0}_abs", "s")
    P.wait("v", f"{tag0}_abs")
    each(lambda d: P.bare("v", lambda v, d=d: v.tensor_scalar(
        d["tC"], d["tA"], 0.25, None, ALU.is_ge)))
    each(lambda d: P.bare("v", lambda v, d=d: v.tensor_scalar(
        d["xq_out"], d["tA"], 0.75, None, ALU.is_ge)))
    each(lambda d: P.bare("v", lambda v, d=d: v.tensor_scalar(
        d["tD"], d["tA"], 5.0, 2.0, ALU.is_ge, ALU.mult)))
    each(lambda d: P.bare("v", lambda v, d=d: v.tensor_scalar(
        d["tE"], d["tA"], 3.5, None, ALU.is_ge)))
    P.drain("v")
    each(lambda d: P.bare("v", lambda v, d=d: v.scalar_tensor_tensor(
        d["tC"], d["tA"], 1.25, d["tC"], ALU.is_ge, ALU.add)))
    each(lambda d: P.bare("v", lambda v, d=d: v.scalar_tensor_tensor(
        d["xq_out"], d["tA"], 1.75, d["xq_out"], ALU.is_ge, ALU.add)))
    each(lambda d: P.bare("v", lambda v, d=d: v.scalar_tensor_tensor(
        d["tD"], d["tA"], 2.5, d["tD"], ALU.is_ge, ALU.add)))
    P.fence_inc("v", "v")
    P.mark(f"{tag0}_ch", "v")
    P.wait("s", f"{tag0}_ch")
    each(lambda d: P.bare("s", lambda s, d=d: s.activation(d["tA"], d["xin"], AF.Sign)))
    P.fence_inc("s", "s")
    P.mark(f"{tag0}_sgn", "s")
    P.drain("v")
    each(lambda d: P.bare("v", lambda v, d=d: v.tensor_tensor(d["tC"], d["tC"], d["xq_out"],
                                                              ALU.add)))
    each(lambda d: P.bare("v", lambda v, d=d: v.tensor_tensor(d["tD"], d["tD"], d["tE"],
                                                              ALU.add)))
    P.wait("v", f"{tag0}_sgn")
    P.drain("v")
    each(lambda d: P.bare("v", lambda v, d=d: v.scalar_tensor_tensor(
        d["tC"], d["tC"], 0.5, d["tD"], ALU.mult, ALU.add)))
    P.drain("v")
    each(lambda d: P.bare("v", lambda v, d=d: v.tensor_tensor(d["tC"], d["tC"], d["tA"],
                                                              ALU.mult)))
    P.drain("v")
    each(lambda d: P.bare("v", lambda v, d=d: v.tensor_tensor(
        d["xq_out"].rearrange("p (b s) -> p b s", s=16),
        d["tC"].rearrange("p (b s) -> p b s", s=16),
        d["dq"].unsqueeze(2).broadcast_to([128, d["nblk"], 16]), ALU.mult)))
    P.drain("v")


def emit_quant_multi(P, streams):
    emit_quant_multi_b(P, emit_quant_multi_a(P, streams))


def qstream(pp, tag, xin, xq_out, nel, ui, tA, tC, tD, tE, sml, s8):
    return dict(tag=tag, xin=xin, xq_out=xq_out, nel=nel,
                gs6=pp[:, 2 * ui:2 * ui + 1], invgs=pp[:, 2 * ui + 1:2 * ui + 2],
                gs6h=pp[:, 18 + 2 * ui:19 + 2 * ui], invgs2=pp[:, 19 + 2 * ui:20 + 2 * ui],
                tA=tA, tC=tC, tD=tD, tE=tE, sml=sml, s8=s8)


def emit_quant(P, pp, tag, xin, xq_out, nel, ui, tA, tC, tD, tE, sml, s8):
    emit_quant_multi(P, [qstream(pp, tag, xin, xq_out, nel, ui, tA, tC, tD, tE, sml, s8)])


def emit_ln(P, tag, x_ap, out_ap, bns, mv, stdv, inv, negm, eps_pp, prev_tag):
    if prev_tag is not None:
        P.wait("v", prev_tag)
    P.drain("v")
    P.bare("v", lambda v: v.bn_stats(bns[:, 0:6], x_ap[:, 0:512]))
    P.drain("v")
    P.bare("v", lambda v: v.bn_stats(bns[:, 6:12], x_ap[:, 512:1024]))
    P.drain("v")
    P.bare("v", lambda v: v.bn_aggr(mv, bns))
    P.fence_inc("v", "v")
    P.mark(f"{tag}_bns", "v")
    P.wait("s", f"{tag}_bns")
    P.inc("s", lambda s: s.activation(stdv, mv[:, 1:2], AF.Sqrt, bias=eps_pp, scale=1.0), "s", 1)
    P.mark(f"{tag}_std", "s")
    P.wait("v", f"{tag}_std")
    P.bare("v", lambda v: v.reciprocal(inv, stdv))
    P.drain("v")
    P.bare("v", lambda v: v.tensor_scalar(negm, mv[:, 0:1], inv[:, 0:1], -1.0, ALU.mult, ALU.mult))
    P.drain("v")
    P.bare("v", lambda v: v.tensor_scalar(out_ap, x_ap, inv[:, 0:1], negm[:, 0:1],
                                          ALU.mult, ALU.add))
    P.fence_inc("v", "v")
    P.mark(f"{tag}_ln", "v")


def emit_ln2(P, tagA, xA, outA, tagB, xB, outB, st2, eps_pp, prev_tag):
    if prev_tag is not None:
        P.wait("v", prev_tag)
    P.drain("v")
    P.bare("v", lambda v: v.bn_stats(st2["bnsA"][:, 0:6], xA[:, 0:512]))
    P.bare("v", lambda v: v.bn_stats(st2["bnsB"][:, 0:6], xB[:, 0:512]))
    P.drain("v")
    P.bare("v", lambda v: v.bn_stats(st2["bnsA"][:, 6:12], xA[:, 512:1024]))
    P.bare("v", lambda v: v.bn_stats(st2["bnsB"][:, 6:12], xB[:, 512:1024]))
    P.drain("v")
    P.bare("v", lambda v: v.bn_aggr(st2["mvA"][:, :], st2["bnsA"][:, :]))
    P.bare("v", lambda v: v.bn_aggr(st2["mvB"][:, :], st2["bnsB"][:, :]))
    P.fence_inc("v", "v")
    P.mark(f"{tagA}_bns", "v")
    P.wait("s", f"{tagA}_bns")
    P.bare("s", lambda s: s.activation(st2["stdA"][:, :], st2["mvA"][:, 1:2], AF.Sqrt,
                                       bias=eps_pp, scale=1.0))
    P.drain("s")
    P.inc("s", lambda s: s.activation(st2["stdB"][:, :], st2["mvB"][:, 1:2], AF.Sqrt,
                                      bias=eps_pp, scale=1.0), "s", 1)
    P.mark(f"{tagA}_std", "s")
    P.wait("v", f"{tagA}_std")
    P.bare("v", lambda v: v.reciprocal(st2["invA"][:, :], st2["stdA"][:, :]))
    P.bare("v", lambda v: v.reciprocal(st2["invB"][:, :], st2["stdB"][:, :]))
    P.drain("v")
    P.bare("v", lambda v: v.tensor_scalar(st2["nmA"][:, :], st2["mvA"][:, 0:1], st2["invA"][:, 0:1],
                                          -1.0, ALU.mult, ALU.mult))
    P.bare("v", lambda v: v.tensor_scalar(st2["nmB"][:, :], st2["mvB"][:, 0:1], st2["invB"][:, 0:1],
                                          -1.0, ALU.mult, ALU.mult))
    P.drain("v")
    P.bare("v", lambda v: v.tensor_scalar(outA, xA, st2["invA"][:, 0:1], st2["nmA"][:, 0:1],
                                          ALU.mult, ALU.add))
    P.bare("v", lambda v: v.tensor_scalar(outB, xB, st2["invB"][:, 0:1], st2["nmB"][:, 0:1],
                                          ALU.mult, ALU.add))
    P.fence_inc("v", "v")
    P.mark(f"{tagA}_ln", "v")
    P.mark(f"{tagB}_ln", "v")


def build_nc():
    nc = bass.Bass("TRN2", target_bir_lowering=False)
    P = Prog()

    x_full = nc.dram_tensor("x_full", [T_FULL, D], F32, kind="ExternalInput").ap()
    x_half = nc.dram_tensor("x_half", [T_HALF, D], F32, kind="ExternalInput").ap()
    wqi = nc.dram_tensor("wqi", [D, 512], F32, kind="ExternalInput").ap()
    wki = nc.dram_tensor("wki", [D, 512], F32, kind="ExternalInput").ap()
    wvi = nc.dram_tensor("wvi", [D, 512], F32, kind="ExternalInput").ap()
    woi = nc.dram_tensor("woi", [D, D], F32, kind="ExternalInput").ap()
    gshi = nc.dram_tensor("gshi", [512, D], F32, kind="ExternalInput").ap()
    ushi = nc.dram_tensor("ushi", [512, D], F32, kind="ExternalInput").ap()
    dshi = nc.dram_tensor("dshi", [D, 512], F32, kind="ExternalInput").ap()
    maski = nc.dram_tensor("maski", [128, 512], F32, kind="ExternalInput").ap()
    scali = nc.dram_tensor("scali", [128, 8], F32, kind="ExternalInput").ap()
    gidxi = nc.dram_tensor("gidxi", [128, NTT], I32, kind="ExternalInput").ap()
    out_half = nc.dram_tensor("out_half", [T_HALF, D], F32, kind="ExternalOutput").ap()

    cc_am_in = nc.dram_tensor("cc_am_in", [1, 3], F32).ap()
    cc_am_out = nc.dram_tensor("cc_am_out", [1, 3], F32).ap()
    cc_wq_in = nc.dram_tensor("cc_wq_in", [3 * SHW], BF16).ap()
    cc_wq_out = nc.dram_tensor("cc_wq_out", [N_CORES * 3 * SHW], BF16).ap()
    cc_o_in = nc.dram_tensor("cc_o_in", [T_HALF, 512], BF16).ap()
    cc_o_out_q = [nc.dram_tensor(f"cc_o_out{k}", [N_CORES * T_HALF // NQ, 512], BF16).ap()
                  for k in range(NQ)]
    r1_sp = nc.dram_tensor("r1_sp", [T_HALF, D], F32).ap()
    dbg_d = nc.dram_tensor("dbg_d", [128, 7680], F32).ap()
    dbg2_d = nc.dram_tensor("dbg2_d", [128, NTT * 1024], BF16).ap()
    dbg3_d = nc.dram_tensor("dbg3_d", [128, BQ * 1024 + 8 * BQ * 64], F32).ap()
    hq_sp = nc.dram_tensor("hq_sp", [NTT, 128, 32, 128], BF16).ap()

    T = dict(dbg_d=dbg_d, dbg2_d=dbg2_d, dbg3_d=dbg3_d, x_full=x_full, x_half=x_half, wqi=wqi, wki=wki, wvi=wvi, woi=woi,
             gshi=gshi, ushi=ushi, dshi=dshi, maski=maski, scali=scali, gidxi=gidxi,
             out_half=out_half, cc_am_in=cc_am_in, cc_am_out=cc_am_out,
             cc_wq_in=cc_wq_in, cc_wq_out=cc_wq_out, cc_o_in=cc_o_in,
             cc_o_out_q=cc_o_out_q, r1_sp=r1_sp, hq_sp=hq_sp)

    with contextlib.ExitStack() as stk:
        def sbp(name, shape, dt, **kw):
            return stk.enter_context(nc.sbuf_tensor(name, shape, dt, **kw))

        consts = sbp("consts", [128, 8], F32)
        maskt = sbp("maskt", [128, 512], F32)
        sk_sb = [sbp(f"sk_sb{i}", [64, 512], BF16) for i in range(2)]
        sk_f32 = sbp("sk_f32", [64, 512], F32)
        z_sb = sbp("z_sb", [64, 512], BF16)
        z_f32 = sbp("z_f32", [64, 512], F32)
        pp = sbp("pp", [128, 28], F32)
        gidx = sbp("gidx", [128, NTT], I32)
        T.update(consts=consts, maskt=maskt, sk_sb=sk_sb, sk_f32=sk_f32, z_sb=z_sb,
                 z_f32=z_f32, pp=pp, gidx=gidx)

        semnames = (["sy", "sct", "g", "gc", "cc", "t", "v", "s", "wtr", "wst", "xqT"]
                    + [f"{k}{i}" for k in ("xld", "lnT", "qkT", "wT", "ost", "og", "oT",
                                           "r1s", "fw", "hqT", "hqs", "fbl") for i in (0, 1)]
                    + ["qkT2", "f8d0", "f8d1", "f8u0", "f8u1"])
        for nm in semnames:
            h = stk.enter_context(nc.semaphore("s_" + nm))
            P.sem(nm, h)

        with nc.sbuf_tensor("o_my", [128, NSL * 512], BF16) as o_my:
            T["o_my"] = o_my
            qstk = contextlib.ExitStack()
            for nm in ("wq_sb", "wk_sb", "wv_sb"):
                T[nm] = qstk.enter_context(nc.sbuf_tensor(nm, [128, 8, 512], BF16))
            wstk = contextlib.ExitStack()
            build_w_head(nc, P, wstk, T)
            with contextlib.ExitStack() as astk:
                build_attn(nc, P, astk, T)
            wstk.close()
            qstk.close()
            with nc.sbuf_tensor("xq_res", [128, NTT * 8 * 128], BF16,
                                is_transpose_dest=True) as xq_res:
                T["xq_res"] = xq_res
                with contextlib.ExitStack() as bstk:
                    build_b(nc, P, bstk, T)
                with contextlib.ExitStack() as fstk:
                    build_fa(nc, P, fstk, T)
        with contextlib.ExitStack() as f2stk:
            build_fb(nc, P, f2stk, T)

        with nc.Block() as blk:
            @blk.tensor
            def _(t):
                for fn in P.streams["t"]:
                    fn(t)

            @blk.vector
            def _(v):
                for fn in P.streams["v"]:
                    fn(v)

            @blk.scalar
            def _(s):
                for fn in P.streams["s"]:
                    fn(s)

            @blk.sync
            def _(sy):
                for fn in P.streams["sy"]:
                    fn(sy)

            @blk.gpsimd
            def _(g):
                for fn in P.streams["g"]:
                    fn(g)

    return nc


def build_w_head(nc, P, wstk, T):
    consts, pp = T["consts"], T["pp"]
    o_my = T["o_my"]
    wsh = [wstk.enter_context(nc.sbuf_tensor(f"wsh{i}", [128, 4096], F32)) for i in range(3)]
    wqt = wstk.enter_context(nc.sbuf_tensor("wqt", [128, 4096], BF16))
    scr1 = wstk.enter_context(nc.sbuf_tensor("scr1", [128, 2048], BF16))
    scr2 = wstk.enter_context(nc.sbuf_tensor("scr2", [128, 2048], BF16))
    scr3 = wstk.enter_context(nc.sbuf_tensor("scr3", [128, 2048], BF16))
    if NSL * 512 - 23 * 512 >= 4096:
        scr4 = o_my[:, 23 * 512:23 * 512 + 4096]
    else:
        scr4 = wstk.enter_context(nc.sbuf_tensor("scr4", [128, 4096], BF16))[:, :]
    sml = wstk.enter_context(nc.sbuf_tensor("smlW", [128, 8 * 128], F32))
    s8W = wstk.enter_context(nc.sbuf_tensor("s8W", [128, 256], F8))
    wfm = wstk.enter_context(nc.sbuf_tensor("wfm", [128, 8, 128], BF16, is_transpose_dest=True))

    P.inc("sy", lambda h: h.dma_start(
        out=wsh[0][:, :].rearrange("p (j f) -> p j f", f=D),
        in_=T["gshi"].rearrange("(j p) f -> p j f", p=128)), "sy", 16)
    P.inc("sy", lambda h: h.dma_start(
        out=wsh[1][:, :].rearrange("p (j f) -> p j f", f=D),
        in_=T["ushi"].rearrange("(j p) f -> p j f", p=128)), "sy", 16)
    P.inc("sy", lambda h: h.dma_start(
        out=wsh[2][:, :].rearrange("p (j f) -> p j f", f=512),
        in_=T["dshi"].rearrange("(j p) f -> p j f", p=128)), "sy", 16)
    P.inc("sy", lambda h: h.dma_start(out=T["consts"][:, :], in_=T["scali"][:, :]), "sy", 16)
    P.inc("sy", lambda h: h.dma_start(out=T["maskt"][:, :], in_=T["maski"][:, :]), "sy", 16)
    P.inc("sy", lambda h: h.dma_start(out=T["gidx"][:, :], in_=T["gidxi"][:, :]), "sy", 16)
    P.mark("w_ld", "sy")
    for wsrc, wdst in ((T["wqi"], T["wq_sb"]), (T["wki"], T["wk_sb"]),
                       (T["wvi"], T["wv_sb"])):
        P.inc("g", lambda g, wsrc=wsrc, wdst=wdst: g.dma_start(
            out=wdst[:, :, :], in_=wsrc.rearrange("(m p) c -> p m c", p=128)), "sct", 16)
    P.mark("a_wld", "sct")

    P.bare("v", lambda v: v.memset(pp[:, 16:17], -6.0 * LN2C))
    P.bare("v", lambda v: v.memset(pp[:, 17:18], 6.0 * LN2C))
    P.drain("v")
    P.wait("v", "w_ld")
    for i in range(3):
        P.bare("v", lambda v, i=i: v.tensor_reduce(
            pp[:, 10 + i:11 + i], wsh[i][:, :], AX.X, ALU.max,
            apply_absolute_value=True))
    P.fence_inc("v", "v")
    P.mark("w_amax", "v")
    P.wait("g", "w_amax")
    P.inc("g", lambda g: g.tensor_reduce(pp[0:1, 13:16], pp[:, 10:13], AX.C, ALU.max), "gc", 1)
    P.wait_now("g", "gc")
    P.inc("g", lambda g: g.dma_start(out=T["cc_am_in"][0:1, 0:3], in_=pp[0:1, 13:16]), "g", 16)
    P.wait_now("g", "g")
    P.inc("g", lambda g: g.collective_compute(
        "AllReduce", ALU.max, replica_groups=[list(range(N_CORES))],
        ins=[T["cc_am_in"].opt()], outs=[T["cc_am_out"].opt()]), "cc", 1)
    P.wait_now("g", "cc")
    P.inc("g", lambda g: g.dma_start(
        out=pp[:, 10:13], in_=T["cc_am_out"][0:1, :].broadcast_to([128, 3])), "g", 16)
    P.mark("w_gam", "g")

    tr_i = [0]

    def tail(i):
        if i == 0:
            P.wait("v", "w_gam")
            for j in range(3):
                P.bare("v", lambda v, j=j: v.tensor_scalar(
                    pp[:, 1 + 2 * j:2 + 2 * j], pp[:, 10 + j:11 + j], 1.0 / GSW_CONST,
                    None, ALU.mult))
                P.bare("v", lambda v, j=j: v.reciprocal(pp[:, 13:14], pp[:, 10 + j:11 + j]))
                P.drain("v")
                P.bare("v", lambda v, j=j: v.tensor_scalar(
                    pp[:, 0 + 2 * j:1 + 2 * j], pp[:, 13:14], 448.0, None, ALU.mult))
                P.drain("v")
            P.bare("v", lambda v: v.tensor_scalar(pp[:, 6:7], consts[:, 1:2], 1.0 / 6.0,
                                                  None, ALU.mult))
            P.bare("v", lambda v: v.reciprocal(pp[:, 7:8], consts[:, 1:2]))
            P.bare("v", lambda v: v.tensor_scalar(pp[:, 8:9], consts[:, 2:3], 1.0 / 6.0,
                                                  None, ALU.mult))
            P.bare("v", lambda v: v.reciprocal(pp[:, 9:10], consts[:, 2:3]))
            P.drain("v")
            for j in range(5):
                P.bare("v", lambda v, j=j: v.tensor_scalar(
                    pp[:, 18 + 2 * j:19 + 2 * j], pp[:, 2 * j:2 * j + 1], 0.5, None, ALU.mult))
                P.bare("v", lambda v, j=j: v.tensor_scalar(
                    pp[:, 19 + 2 * j:20 + 2 * j], pp[:, 2 * j + 1:2 * j + 2], 2.0, None, ALU.mult))
            P.drain("v")
        if i < 3:
            if i > 0:
                P.wait("v", f"w_tr_{tr_i[0] - 1}")
            for jh in range(2):
                emit_quant(P, pp, f"wq{i}_{jh}", wsh[i][:, jh * 2048:(jh + 1) * 2048],
                           wqt[:, jh * 2048:(jh + 1) * 2048], 2048, i,
                           scr1[:, :], scr2[:, :], scr3[:, :],
                           scr4[:, jh * 2048:(jh + 1) * 2048], sml[:, :],
                           s8W[:, :])
            P.fence_inc("v", "v")
            P.mark(f"w_q_{i}", "v")
            P.wait("sy", f"w_q_{i}")
            njt, freew, rowtot = [(4, 1024, 512), (4, 1024, 512), (8, 512, 1024)][i]
            off_dr = i * SHW
            nmid = freew // 128
            for j in range(njt):
                if tr_i[0] > 0:
                    P.wait("sy", f"w_st_{tr_i[0] - 1}")
                srcw = wqt[:, j * freew:(j + 1) * freew]
                P.inc("sy", lambda h, srcw=srcw, nmid=nmid: h.dma_start_transpose(
                    wfm[:, 0:nmid, :], srcw), "wtr", 16)
                P.mark(f"w_tr_{tr_i[0]}", "wtr")
                P.wait("g", f"w_tr_{tr_i[0]}")

                def mk_store(h, nmid=nmid, off_dr=off_dr, j=j, rowtot=rowtot):
                    dst3 = T["cc_wq_in"][off_dr:off_dr + SHW].rearrange(
                        "(m p r) -> p m r", p=128, r=rowtot)
                    return h.dma_start(out=dst3[:, 0:nmid, j * 128:(j + 1) * 128],
                                       in_=wfm[:, 0:nmid, :])
                P.inc("g", mk_store, "wst", 16)
                P.mark(f"w_st_{tr_i[0]}", "wst")
                tr_i[0] += 1
        else:
            P.wait_now("g", "g")
            P.wait_now("g", "wst")
            P.fence_nop("g", "gc")
            P.mark("w_g_done", "gc")
            P.inc("g", lambda g: g.collective_compute(
                "AllGather", ALU.bypass, replica_groups=[list(range(N_CORES))],
                ins=[T["cc_wq_in"].opt()], outs=[T["cc_wq_out"].opt()]), "cc", 1)
            P.mark("w_ag", "cc")

    T["w_tails"] = [lambda i=i: tail(i) for i in range(4)]


def build_attn(nc, P, astk, T):
    consts, maskt, o_my = T["consts"], T["maskt"], T["o_my"]
    sk_sb, z_sb, z_f32 = T["sk_sb"], T["z_sb"], T["z_f32"]
    sk_f32 = T["sk_f32"]
    NB = 2

    def asb(name, shape, dt, **kw):
        return astk.enter_context(nc.sbuf_tensor(name, shape, dt, **kw))

    def aps(name, shape):
        return astk.enter_context(nc.psum_tensor(name, shape, F32))

    x_t = [asb(f"x_t{i}", [128, D], F32) for i in range(NB)]
    st2 = dict(bnsA=asb("a_bnsA", [128, 12], F32), bnsB=asb("a_bnsB", [128, 12], F32),
               mvA=asb("a_mvA", [128, 2], F32), mvB=asb("a_mvB", [128, 2], F32),
               stdA=asb("a_stdA", [128, 1], F32), stdB=asb("a_stdB", [128, 1], F32),
               invA=asb("a_invA", [128, 1], F32), invB=asb("a_invB", [128, 1], F32),
               nmA=asb("a_nmA", [128, 1], F32), nmB=asb("a_nmB", [128, 1], F32))
    ln_tm = [asb(f"ln_tm{i}", [128, D], BF16) for i in range(NB)]
    ln_fm = [asb(f"ln_fm{i}", [128, 8, 128], BF16, is_transpose_dest=True) for i in range(NB)]
    NBQ = 3
    qk_sp = [asb(f"qk_sp{i}", [128, 2048], BF16) for i in range(NBQ)]
    q_sp = [t[:, 0:1024] for t in qk_sp]
    k_sp = [t[:, 1024:2048] for t in qk_sp]
    qk_fm = [asb(f"qk_fm{i}", [128, 16, 128], BF16, is_transpose_dest=True) for i in range(NBQ)]
    q_fm = [t[:, 0:8, :] for t in qk_fm]
    k_fm = [t[:, 8:16, :] for t in qk_fm]
    k_tm = [asb(f"k_tm{i}", [128, 512], BF16) for i in range(NBQ)]
    v_tm = [asb(f"v_tm{i}", [128, 512], BF16) for i in range(NBQ)]
    a_sb = [asb(f"a_sb{i}", [128, 1024], BF16) for i in range(NB)]
    w_all = [asb(f"w_all{i}", [64, 1024], BF16) for i in range(NB)]
    w_tm = [asb(f"w_tm{i}", [128, 8, 64], BF16, is_transpose_dest=True) for i in range(NB)]
    b_sb = [asb(f"b_sb{i}", [128, 1024], BF16) for i in range(NB)]
    NBO = 6
    o_st = [asb(f"o_st{i}", [128, 512], BF16) for i in range(NBO)]
    wq_sb, wk_sb, wv_sb = T["wq_sb"], T["wk_sb"], T["wv_sb"]
    dbg_sb = asb("dbg_sb", [128, 7680], F32) if DEBUG else None

    psQ = aps("psQ", [128, 512])
    psK = aps("psK", [128, 512])
    psVO = aps("psVO", [128, 512])
    psAB = aps("psAB", [128, 1024])
    psW = aps("psW", [64, 1024])
    psSK = aps("psSK", [64, 512])

    sel1 = consts[:, 3:4]
    sel2 = consts[:, 4:5]
    selx1 = consts[:, 5:6]
    selx2 = consts[:, 6:7]

    if DEBUG:
        P.bare("v", lambda v: v.memset(dbg_sb[:, :], 0.0))
    P.bare("v", lambda v: v.memset(z_sb[:, :], 0.0))
    P.bare("v", lambda v: v.memset(sk_sb[0][:, :], 0.0))
    P.bare("v", lambda v: v.memset(sk_sb[1][:, :], 0.0))
    P.bare("v", lambda v: v.memset(sk_f32[:, :], 0.0))
    P.bare("v", lambda v: v.memset(z_f32[:, :], 0.0))
    for i in range(NBQ):
        P.bare("v", lambda v, i=i: v.memset(qk_sp[i][:, :], 0.0))
    P.drain("v")
    P.fence_inc("v", "v")
    P.mark("a_zsb_-1", "v")
    P.mark("a_sk0_0", "v")
    P.mark("a_state0", "v")

    def ok(ci):
        return 0 <= ci < NCH

    def st_front2(c0):
        pair = [c for c in (c0, c0 + 1) if ok(c)]
        if not pair:
            return
        for ci in pair:
            bi = ci % NB
            if ci >= NB:
                P.wait("sy", f"a_ln_{ci - NB}")
            P.inc("sy", lambda h, ci=ci, bi=bi: h.dma_start(
                out=x_t[bi][:, :], in_=T["x_full"][ci * 128:(ci + 1) * 128, :]), f"xld{bi}", 16)
            P.mark(f"a_xld_{ci}", f"xld{bi}")
        for ci in pair:
            P.wait("v", f"a_xld_{ci}")
        prev = f"aln{c0 - 1}_ln" if c0 > 0 else None
        if len(pair) == 2:
            emit_ln2(P, f"aln{c0}", x_t[c0 % NB][:, :], ln_tm[c0 % NB][:, :],
                     f"aln{c0 + 1}", x_t[(c0 + 1) % NB][:, :], ln_tm[(c0 + 1) % NB][:, :],
                     st2, consts[:, 0:1], prev)
        else:
            ci = pair[0]
            emit_ln(P, f"aln{ci}", x_t[ci % NB][:, :], ln_tm[ci % NB][:, :],
                    st2["bnsA"][:, :], st2["mvA"][:, :], st2["stdA"][:, :],
                    st2["invA"][:, :], st2["nmA"][:, :], consts[:, 0:1], prev)
        for ci in pair:
            P.mark(f"a_ln_{ci}", "v")
        for ci in pair:
            bi = ci % NB
            P.wait("sy", f"a_ln_{ci}")
            if ci >= NB:
                P.wait("sy", f"a_proj_{ci - NB}")
            P.inc("sy", lambda h, bi=bi: h.dma_start_transpose(
                ln_fm[bi][:, :, :], ln_tm[bi][:, :]), f"lnT{bi}", 16)
            P.mark(f"a_lnT_{ci}", f"lnT{bi}")

    def st_B1(ci):
        if not ok(ci):
            return
        bi = ci % NB
        P.wait("t", f"a_lnT_{ci}")
        if ci == 0:
            P.wait("t", "a_wld")
        if ci > 0:
            P.wait("t", f"a_qkvev_{ci - 1}")
        if ci >= 3:
            P.wait("t", f"a_oev_{ci - 3}")
        for m in range(8):
            P.bare("t", lambda t, bi=bi, m=m: t.matmul(
                psQ[:, :], ln_fm[bi][:, m, :], wq_sb[:, m, :], start=(m == 0), stop=(m == 7)))
        for m in range(8):
            P.bare("t", lambda t, bi=bi, m=m: t.matmul(
                psK[:, :], ln_fm[bi][:, m, :], wk_sb[:, m, :], start=(m == 0), stop=(m == 7)))
        for m in range(8):
            P.bare("t", lambda t, bi=bi, m=m: t.matmul(
                psVO[:, :], ln_fm[bi][:, m, :], wv_sb[:, m, :], start=(m == 0), stop=(m == 7)))
        P.fence_inc("t", "t")
        P.mark(f"a_proj_{ci}", "t")

    def st_qkvev(ci):
        if not ok(ci):
            return
        bq = ci % NBQ
        P.wait("v", f"a_proj_{ci}")
        if ci >= NBQ:
            P.wait("v", f"a_qkT_{ci - NBQ}")
            P.wait("v", f"a_skz_{ci - NBQ}")
        P.bare("v", lambda v, bq=bq: v.tensor_copy(
            q_sp[bq].rearrange("p (h r) -> p h r", r=128)[:, :, 0:64],
            psQ[:, :].rearrange("p (h r) -> p h r", r=64)))
        P.bare("v", lambda v, bq=bq: v.tensor_copy(
            k_sp[bq].rearrange("p (h r) -> p h r", r=128)[:, :, 0:64],
            psK[:, :].rearrange("p (h r) -> p h r", r=64)))
        P.bare("v", lambda v, bq=bq: v.tensor_copy(k_tm[bq][:, :], psK[:, :]))
        P.bare("v", lambda v, bq=bq: v.tensor_copy(v_tm[bq][:, :], psVO[:, :]))
        P.fence_inc("v", "v")
        P.mark(f"a_qkvev_{ci}", "v")

    def st_qkT(ci):
        if not ok(ci):
            return
        bq = ci % NBQ
        sem = f"qkT{bq}" if bq < 2 else "qkT2"
        P.wait("sy", f"a_qkvev_{ci}")
        if ci >= NBQ:
            P.wait("sy", f"a_skz_{ci - NBQ}")
        P.inc("sy", lambda h, bq=bq: h.dma_start_transpose(qk_fm[bq][:, :, :], qk_sp[bq][:, :]),
              sem, 16)
        P.mark(f"a_qkT_{ci}", sem)

    def st_B2(ci):
        if not ok(ci):
            return
        bq = ci % NBQ
        P.wait("t", f"a_qkT_{ci}")
        if ci > 0:
            P.wait("t", f"a_bmask_{ci - 1}")
            P.wait("t", f"a_skadd_{ci - 1}")
        for h_ in range(8):
            P.bare("t", lambda t, bq=bq, h_=h_: t.matmul(
                psAB[:, h_ * 128:(h_ + 1) * 128], k_fm[bq][0:64, h_, :], q_fm[bq][0:64, h_, :],
                start=True, stop=True))
        for h_ in range(8):
            P.bare("t", lambda t, bq=bq, h_=h_: t.matmul(
                psSK[0:64, h_ * 64:(h_ + 1) * 64], k_tm[bq][:, h_ * 64:(h_ + 1) * 64],
                k_tm[bq][:, h_ * 64:(h_ + 1) * 64], start=True, stop=True))
        P.fence_inc("t", "t")
        P.mark(f"a_A_{ci}", "t")

    def st_amask_skadd(ci):
        if not ok(ci):
            return
        bi = ci % NB
        P.wait("v", f"a_A_{ci}")
        if ci >= NB:
            P.wait("v", f"a_W_{ci - NB}")
        P.bare("v", lambda v, bi=bi: v.tensor_tensor(a_sb[bi][:, 0:512], psAB[:, 0:512],
                                                     maskt[:, :], ALU.mult))
        P.bare("v", lambda v, bi=bi: v.tensor_tensor(a_sb[bi][:, 512:1024], psAB[:, 512:1024],
                                                     maskt[:, :], ALU.mult))
        P.fence_inc("v", "v")
        P.mark(f"a_amask_{ci}", "v")
        P.drain("v")
        P.bare("v", lambda v: v.tensor_tensor(sk_f32[:, :], sk_f32[:, :], psSK[:, :], ALU.add))
        P.drain("v")
        P.fence_inc("v", "v")
        P.mark(f"a_skadd_{ci}", "v")
        if ci + 1 < NCH:
            P.wait("v", f"a_W_{ci - 1}") if ci >= 1 else None
            P.bare("v", lambda v, ci=ci: v.tensor_copy(sk_sb[(ci + 1) % 2][:, :], sk_f32[:, :]))
            P.drain("v")
            P.fence_inc("v", "v")
            P.mark(f"a_sk0_{ci + 1}", "v")

    def st_B3(ci):
        if not ok(ci):
            return
        bi = ci % NB
        P.wait("t", f"a_amask_{ci}")
        P.wait("t", f"a_sk0_{ci}")
        if ci == 0:
            P.wait("t", "a_state0")
        if ci > 0:
            P.wait("t", f"a_zsb_{ci - 1}")
        bq = ci % NBQ
        for h_ in range(8):
            P.bare("t", lambda t, bi=bi, bq=bq, h_=h_: t.matmul(
                psW[0:64, h_ * 128:(h_ + 1) * 128], k_tm[bq][:, h_ * 64:(h_ + 1) * 64],
                a_sb[bi][:, h_ * 128:(h_ + 1) * 128], start=True, stop=False))
            P.bare("t", lambda t, bq=bq, h_=h_, ci=ci: t.matmul(
                psW[0:64, h_ * 128:(h_ + 1) * 128], sk_sb[ci % 2][0:64, h_ * 64:(h_ + 1) * 64],
                q_fm[bq][0:64, h_, :], start=False, stop=True))
        P.fence_inc("t", "t")
        P.mark(f"a_W_{ci}", "t")

    def st_wev(ci):
        if not ok(ci):
            return
        bi = ci % NB
        P.wait("v", f"a_W_{ci}")
        if ci >= NB:
            P.wait("v", f"a_wT_{ci - NB}")
            P.wait("v", f"a_B_{ci - NB}")
        P.bare("v", lambda v, bi=bi: v.tensor_copy(w_all[bi][:, 0:512], psW[:, 0:512]))
        P.bare("v", lambda v, bi=bi: v.tensor_copy(w_all[bi][:, 512:1024], psW[:, 512:1024]))
        if DEBUG and ci == DBG_CI:
            P.drain("v")
            P.bare("v", lambda v, bi=bi: v.tensor_copy(dbg_sb[0:64, 3584:4608], w_all[bi][:, :]))
        P.fence_inc("v", "v")
        P.mark(f"a_wev_{ci}", "v")

    def st_wT(ci):
        if not ok(ci):
            return
        bi = ci % NB
        P.wait("sy", f"a_wev_{ci}")
        if ci >= NB:
            P.wait("sy", f"a_skz_{ci - NB}")
        P.inc("sy", lambda h, bi=bi: h.dma_start_transpose(w_tm[bi][:, :, :], w_all[bi][:, :]),
              f"wT{bi}", 16)
        P.mark(f"a_wT_{ci}", f"wT{bi}")

    def st_B4(ci):
        if not ok(ci):
            return
        bi = ci % NB
        P.wait("t", f"a_wev_{ci}")
        bq = ci % NBQ
        for h_ in range(8):
            P.bare("t", lambda t, bi=bi, bq=bq, h_=h_: t.matmul(
                psAB[:, h_ * 128:(h_ + 1) * 128], w_all[bi][0:64, h_ * 128:(h_ + 1) * 128],
                q_fm[bq][0:64, h_, :], start=True, stop=True))
        P.fence_inc("t", "t")
        P.mark(f"a_B_{ci}", "t")

    def st_bmask(ci):
        if not ok(ci):
            return
        bi = ci % NB
        P.wait("v", f"a_B_{ci}")
        if ci >= NB:
            P.wait("v", f"a_O_{ci - NB}")
        P.bare("v", lambda v, bi=bi: v.tensor_tensor(b_sb[bi][:, 0:512], psAB[:, 0:512],
                                                     maskt[:, :], ALU.mult))
        P.bare("v", lambda v, bi=bi: v.tensor_tensor(b_sb[bi][:, 512:1024], psAB[:, 512:1024],
                                                     maskt[:, :], ALU.mult))
        P.fence_inc("v", "v")
        P.mark(f"a_bmask_{ci}", "v")

    def st_B5(ci):
        if not ok(ci):
            return
        bi = ci % NB
        P.wait("t", f"a_bmask_{ci}")
        P.wait("t", f"a_zsb_{ci - 1}")
        P.wait("t", f"a_wT_{ci}")
        if ci + 2 < NCH:
            P.wait("t", f"a_qkvev_{ci + 2}")
        bq = ci % NBQ
        for h_ in range(8):
            P.bare("t", lambda t, bi=bi, bq=bq, h_=h_: t.matmul(
                psVO[:, h_ * 64:(h_ + 1) * 64], b_sb[bi][:, h_ * 128:(h_ + 1) * 128],
                v_tm[bq][:, h_ * 64:(h_ + 1) * 64], start=True, stop=False))
            P.bare("t", lambda t, bq=bq, h_=h_: t.matmul(
                psVO[:, h_ * 64:(h_ + 1) * 64], q_fm[bq][0:64, h_, :],
                z_sb[0:64, h_ * 64:(h_ + 1) * 64], start=False, stop=True))
        for h_ in range(8):
            P.bare("t", lambda t, bi=bi, bq=bq, h_=h_: t.matmul(
                psW[0:64, h_ * 64:(h_ + 1) * 64], w_tm[bi][:, h_, :],
                v_tm[bq][:, h_ * 64:(h_ + 1) * 64], start=True, stop=True))
        P.fence_inc("t", "t")
        P.mark(f"a_O_{ci}", "t")
        P.mark(f"a_skz_{ci}", "t")

    def st_oev_z(ci):
        if not ok(ci):
            return
        bi = ci % NBO
        sp = ci % 2
        sl = ci % NSL
        first_half = ci < NSL
        P.wait("v", f"a_O_{ci}")
        if ci >= NBO:
            P.wait("v", f"a_ost_{ci - NBO}")
        if first_half:
            P.bare("v", lambda v, sl=sl: v.tensor_scalar(
                o_my[:, sl * 512:(sl + 1) * 512], psVO[:, :], sel1, None, ALU.mult))
            P.bare("v", lambda v, bi=bi: v.tensor_scalar(
                o_st[bi][:, :], psVO[:, :], selx1, None, ALU.mult))
        else:
            P.bare("v", lambda v, sl=sl: v.scalar_tensor_tensor(
                o_my[:, sl * 512:(sl + 1) * 512], psVO[:, :], sel2,
                o_my[:, sl * 512:(sl + 1) * 512], ALU.mult, ALU.add))
            P.bare("v", lambda v, bi=bi: v.tensor_scalar(
                o_st[bi][:, :], psVO[:, :], selx2, None, ALU.mult))
        if DEBUG and ci == DBG_CI:
            P.drain("v")
            P.bare("v", lambda v: v.tensor_copy(dbg_sb[:, 4608:5120], psVO[:, :]))
            if DBG_CI > 0:
                P.bare("v", lambda v, ci=ci: v.tensor_copy(dbg_sb[0:64, 5120:5632], sk_sb[ci % 2][:, :]))
                P.bare("v", lambda v: v.tensor_copy(dbg_sb[0:64, 6144:6656], z_sb[:, :]))
                P.bare("v", lambda v: v.tensor_copy(dbg_sb[0:64, 6656:7168], z_f32[:, :]))
        P.fence_inc("v", "v")
        P.mark(f"a_oev_{ci}", "v")
        P.wait("g", f"a_oev_{ci}")
        if first_half:
            P.inc("g", lambda g, bi=bi, sl=sl: g.dma_start(
                out=T["cc_o_in"][sl * 128:(sl + 1) * 128, :], in_=o_st[bi][:, :]),
                f"ost{sp}", 16)
        else:
            P.wait("g", f"a_ost_{ci - NSL}")
            P.inc("g", lambda g, bi=bi, sl=sl: g.dma_start(
                out=T["cc_o_in"][sl * 128:(sl + 1) * 128, :], in_=o_st[bi][:, :],
                accum_op=ALU.add), f"ost{sp}", 16)
        P.mark(f"a_ost_{ci}", f"ost{sp}")
        P.wait("v", f"a_skz_{ci}")
        if ci == 0:
            P.bare("v", lambda v: v.tensor_copy(z_f32[:, :], psW[0:64, 0:512]))
        else:
            P.bare("v", lambda v: v.tensor_tensor(z_f32[:, :], z_f32[:, :],
                                                  psW[0:64, 0:512], ALU.add))
        P.drain("v")
        P.bare("v", lambda v: v.tensor_copy(z_sb[:, :], z_f32[:, :]))
        P.fence_inc("v", "v")
        P.mark(f"a_zsb_{ci}", "v")

    st_front2(0)
    st_B1(0)
    st_qkvev(0)
    st_B1(1)
    st_qkvev(1)
    st_qkT(0)
    st_B2(0)
    st_amask_skadd(0)
    st_qkT(1)
    w_ip = [min(10 + 4 * j, NCH - 1) for j in range(4)]
    w_tp = [0]
    for i in range(NCH):
        if (i + 2) % 2 == 0:
            st_front2(i + 2)
        st_B3(i)
        st_wev(i)
        st_B1(i + 2)
        st_qkvev(i + 2)
        st_B4(i)
        st_bmask(i)
        st_B2(i + 1)
        st_amask_skadd(i + 1)
        st_wT(i)
        st_B5(i)
        st_oev_z(i)
        while w_tp[0] < 4 and i >= w_ip[w_tp[0]]:
            T["w_tails"][w_tp[0]]()
            w_tp[0] += 1
        for k in range(NQ - 1):
            if i == NSL + (NSL // NQ) * (k + 1) - 1:
                P.wait("g", f"a_ost_{i - 1}")
                P.wait("g", f"a_ost_{i}")
                qr = T_HALF // NQ
                P.inc("g", lambda g, k=k, qr=qr: g.collective_compute(
                    "AllGather", ALU.bypass, replica_groups=[list(range(N_CORES))],
                    ins=[T["cc_o_in"][k * qr:(k + 1) * qr, :].opt()],
                    outs=[T["cc_o_out_q"][k].opt()]), "cc", 1)
                P.mark(f"o_ag{k}", "cc")
        st_qkT(i + 2)

    if DEBUG:
        P.wait_now("g", "ost0")
        P.wait_now("g", "ost1")
        P.inc("g", lambda g: g.dma_start(out=T["dbg_d"][:, :], in_=dbg_sb[:, :]), "g", 16)
        P.wait_now("g", "g")
    P.fence_nop("t", "t")
    P.mark("a_end_t", "t")
    P.fence_nop("v", "v")
    P.mark("a_end_v", "v")
    P.fence_nop("s", "s")
    P.mark("a_end_s", "s")

    P.wait_now("g", "ost0")
    P.wait_now("g", "ost1")
    qr = T_HALF // NQ
    k = NQ - 1
    P.inc("g", lambda g, k=k: g.collective_compute(
        "AllGather", ALU.bypass, replica_groups=[list(range(N_CORES))],
        ins=[T["cc_o_in"][k * qr:(k + 1) * qr, :].opt()],
        outs=[T["cc_o_out_q"][k].opt()]), "cc", 1)
    P.mark(f"o_ag{k}", "cc")


def build_b(nc, P, bstk, T):
    consts, o_my, xq_res = T["consts"], T["o_my"], T["xq_res"]
    pp = T["pp"]
    NB = 2

    def bsb(name, shape, dt, **kw):
        return bstk.enter_context(nc.sbuf_tensor(name, shape, dt, **kw))

    wo_sb = bsb("wo_sb", [128, 8, 1024], BF16)
    o_pt = [bsb(f"o_pt{i}", [128, 512], BF16) for i in range(NB)]
    om_fm = [bsb(f"om_fm{i}", [128, 4, 128], BF16, is_transpose_dest=True) for i in range(NB)]
    op_fm = [bsb(f"op_fm{i}", [128, 4, 128], BF16, is_transpose_dest=True) for i in range(NB)]
    r1t = [bsb(f"r1t{i}", [128, D], F32) for i in range(NB)]
    ln2a = [bsb(f"ln2a{i}", [128, BQ * 1024], BF16) for i in range(2)]
    xq_tm = bsb("xq_tm", [128, BQ * 1024], BF16)
    BH = BQ * 512
    if NSL * 512 >= 2 * BH * 4:
        tA = o_my[:, 0:2 * BH]
        tA_in_omy = True
    else:
        tA = bsb("tA_b", [128, 2 * BH], BF16)[:, :]
        tA_in_omy = False
    tC = bsb("tC_b", [128, 2 * BH], BF16)
    tD = bsb("tD_b", [128, 2 * BH], BF16)
    tE = bsb("tE_b", [128, 2 * BH], BF16)
    smlB = [bsb(f"smlB{i}", [128, 8 * BQ * 64], F32) for i in range(2)]
    s8B = [bsb(f"s8B{i}", [128, 2 * BQ * 64], F8) for i in range(2)]
    bnsB = bsb("bnsB", [128, 12], F32)
    mvB = bsb("mvB", [128, 2], F32)
    stdB = bsb("stdB", [128, 1], F32)
    invB = bsb("invB", [128, 1], F32)
    negmB = bsb("negmB", [128, 1], F32)
    psWo = [bstk.enter_context(nc.psum_tensor(f"psWo{i}", [128, 1024], F32)) for i in range(2)]

    for eng in ("sy", "g", "s"):
        P.wait(eng, "a_end_t")
        P.wait(eng, "a_end_v")
        P.wait(eng, "a_end_s")
        for sem in ("xld0", "xld1", "lnT0", "lnT1", "qkT0", "qkT1", "qkT2", "wT0", "wT1",
                    "ost0", "ost1", "g"):
            P.wait_now(eng, sem)
    P.inc("g", lambda g: g.dma_start(
        out=wo_sb[:, :, :], in_=T["woi"].rearrange("(m p) c -> p m c", p=128)), "g", 16)
    P.mark("b_wold", "g")

    pendB = [None]

    def flushb():
        if pendB[0] is None:
            return
        stB, tt0 = pendB[0]
        pendB[0] = None
        if tA_in_omy:
            nsl_used = (2 * BH) // 512
            gt = min(nsl_used - 1, tt0)
            P.wait("v", f"b_oT_{gt}")
            if gt >= 1:
                P.wait("v", f"b_oT_{gt - 1}")
        if tt0 >= BQ:
            P.wait("v", f"b_xqT_{tt0 - BQ}")
        emit_quant_multi_b(P, stB)
        P.fence_inc("v", "v")
        P.mark(f"b_q_{tt0}", "v")
        P.wait("sy", f"b_q_{tt0}")
        for i4 in range(BQ):
            tq = tt0 - (BQ - 1) + i4
            P.inc("sy", lambda h, i4=i4, tq=tq: h.dma_start_transpose(
                xq_res[:, tq * 1024:(tq + 1) * 1024].rearrange(
                    "p (m r) -> p m r", r=128),
                xq_tm[:, i4 * 1024:(i4 + 1) * 1024]), "xqT", 16)
        P.mark(f"b_xqT_{tt0}", "xqT")

    for pt in range(NTT // 2):
        tpair = (2 * pt, 2 * pt + 1)
        for tt in tpair:
            bi = tt % NB
            if tt >= NB:
                P.wait("sy", f"b_r1sp_{tt - NB}")
                P.wait("sy", f"bln{tt - NB}_ln")
            P.inc("sy", lambda h, tt=tt, bi=bi: h.dma_start(
                out=r1t[bi][:, :], in_=T["x_half"][tt * 128:(tt + 1) * 128, :]), f"xld{bi}", 16)
            P.mark(f"b_xld_{tt}", f"xld{bi}")

        for tt in tpair:
            bi = tt % NB
            qk = tt * NQ // NTT
            P.wait("g", f"o_ag{qk}")
            if tt >= NB:
                P.wait("g", f"b_oT_{tt - NB}")
            P.inc("g", lambda g, tt=tt, bi=bi, qk=qk: g.indirect_dma_start(
                o_pt[bi][:, :], None, T["cc_o_out_q"][qk][:, :],
                IndirectOffsetOnAxis(T["gidx"][:, tt:tt + 1], 0)), f"og{bi}", 16)
            P.mark(f"b_og_{tt}", f"og{bi}")

        for tt in tpair:
            bi = tt % NB
            P.wait("sy", f"b_og_{tt}")
            if tt >= NB:
                P.wait("sy", f"b_wo_{tt - NB}")
            P.inc("sy", lambda h, tt=tt, bi=bi: h.dma_start_transpose(
                om_fm[bi][:, :, :], o_my[:, tt * 512:(tt + 1) * 512]), f"oT{bi}", 16)
            P.inc("sy", lambda h, bi=bi: h.dma_start_transpose(op_fm[bi][:, :, :], o_pt[bi][:, :]),
                  f"oT{bi}", 16)
            P.mark(f"b_oT_{tt}", f"oT{bi}")

        for tt in tpair:
            bi = tt % NB
            P.wait("t", f"b_oT_{tt}")
            if tt == 0:
                P.wait("t", "b_wold")
            if tt >= 2:
                P.wait("t", f"b_r1_{tt - 2}")
            for dh in range(2):
                for j in range(4):
                    P.bare("t", lambda t, bi=bi, j=j, dh=dh: t.matmul(
                        psWo[bi][:, dh * 512:(dh + 1) * 512], om_fm[bi][:, j, :],
                        wo_sb[:, j, dh * 512:(dh + 1) * 512], start=(j == 0), stop=False))
                for j in range(4):
                    P.bare("t", lambda t, bi=bi, j=j, dh=dh: t.matmul(
                        psWo[bi][:, dh * 512:(dh + 1) * 512], op_fm[bi][:, j, :],
                        wo_sb[:, 4 + j, dh * 512:(dh + 1) * 512], start=False, stop=(j == 3)))
            P.fence_inc("t", "t")
            P.mark(f"b_wo_{tt}", "t")

        for tt in tpair:
            bi = tt % NB
            P.wait("v", f"b_wo_{tt}")
            P.wait("v", f"b_xld_{tt}")
            P.bare("v", lambda v, bi=bi: v.tensor_tensor(r1t[bi][:, 0:512], psWo[bi][:, 0:512],
                                                         r1t[bi][:, 0:512], ALU.add))
            P.bare("v", lambda v, bi=bi: v.tensor_tensor(r1t[bi][:, 512:1024], psWo[bi][:, 512:1024],
                                                         r1t[bi][:, 512:1024], ALU.add))
            P.fence_inc("v", "v")
            P.mark(f"b_r1_{tt}", "v")

        for tt in tpair:
            bi = tt % NB
            P.wait("g", f"b_r1_{tt}")
            P.inc("g", lambda g, tt=tt, bi=bi: g.dma_start(
                out=T["r1_sp"][tt * 128:(tt + 1) * 128, :], in_=r1t[bi][:, :]), f"r1s{bi}", 16)
            P.mark(f"b_r1sp_{tt}", f"r1s{bi}")

        for tt in tpair:
            bi = tt % NB
            P.wait("v", f"b_r1_{tt}")
            P.drain("v")
            sl4 = tt % BQ
            emit_ln(P, f"bln{tt}", r1t[bi][:, :],
                    ln2a[(tt // BQ) % 2][:, sl4 * 1024:(sl4 + 1) * 1024],
                    bnsB[:, :], mvB[:, :], stdB[:, :], invB[:, :], negmB[:, :], consts[:, 0:1],
                    f"bln{tt - 1}_ln" if tt > 0 else None)
            P.mark(f"b_ln_{tt}", "v")

            if sl4 == BQ - 1:
                kq = tt // BQ
                P.wait("v", f"b_ln_{tt}")
                stB = emit_quant_multi_a(P, [qstream(
                    pp, f"bq{tt}_{si}", ln2a[kq % 2][:, si * BH:(si + 1) * BH],
                    xq_tm[:, si * BH:(si + 1) * BH], BH, 3,
                    tA[:, si * BH:(si + 1) * BH], tC[:, si * BH:(si + 1) * BH],
                    tD[:, si * BH:(si + 1) * BH], tE[:, si * BH:(si + 1) * BH],
                    smlB[kq % 2][:, si * 8 * BQ * 32:(si + 1) * 8 * BQ * 32],
                    s8B[kq % 2][:, si * 2 * BQ * 32:(si + 1) * 2 * BQ * 32])
                    for si in range(2)])
                flushb()
                pendB[0] = (stB, tt)
    flushb()
    P.mark("b_done", "xqT")
    if DEBUG:
        P.wait("g", "b_done")
        P.inc("g", lambda g: g.dma_start(out=T["dbg2_d"][:, :], in_=xq_res[:, :]), "g", 16)
        P.wait_now("g", "g")
    P.fence_nop("t", "t")
    P.mark("b_end_t", "t")
    P.fence_nop("v", "v")
    P.mark("b_end_v", "v")
    P.fence_nop("s", "s")
    P.mark("b_end_s", "s")


def build_fa(nc, P, fstk, T):
    pp, xq_res = T["pp"], T["xq_res"]

    def fsb(name, shape, dt, **kw):
        return fstk.enter_context(nc.sbuf_tensor(name, shape, dt, **kw))

    wqg = [fsb(f"wqg{i}", [128, 8, 512], BF16) for i in range(2)]
    wqu = [fsb(f"wqu{i}", [128, 8, 512], BF16) for i in range(2)]
    h_b = [fsb(f"h_b{i}", [128, FTB * 512], BF16) for i in range(4)]
    hq_b = [fsb(f"hq_b{i}", [128, FTB * 512], BF16) for i in range(4)]
    sg_t = [fsb(f"sg_t{i}", [128, 512], BF16) for i in range(2)]
    up_t = [fsb(f"up_t{i}", [128, 512], BF16) for i in range(2)]
    tA = [fsb(f"tA_f{i}", [128, FTB * 512], BF16) for i in range(2)]
    if NSL * 512 >= 6 * FTB * 512:
        o_my = T["o_my"]
        tD = [o_my[:, i * FTB * 512:(i + 1) * FTB * 512] for i in range(2)]
        tE = [o_my[:, (2 + i) * FTB * 512:(3 + i) * FTB * 512] for i in range(2)]
        tC = [o_my[:, (4 + i) * FTB * 512:(5 + i) * FTB * 512] for i in range(2)]
    else:
        tD = [fsb(f"tD_f{i}", [128, FTB * 512], BF16)[:, :] for i in range(2)]
        tE = [fsb(f"tE_f{i}", [128, FTB * 512], BF16)[:, :] for i in range(2)]
        tC = [fsb(f"tC_fx{i}", [128, FTB * 512], BF16)[:, :] for i in range(2)]
    smlF = [fsb(f"smlF{i}", [128, 8 * FTB * 32], F32) for i in range(4)]
    s8F = [fsb(f"s8F{i}", [128, 2 * FTB * 32], F8) for i in range(4)]
    hq_fm = [fsb(f"hq_fm{i}", [128, 4, 128], BF16, is_transpose_dest=True) for i in range(2)]
    psG = [fstk.enter_context(nc.psum_tensor(f"psG{i}", [128, 512], F32)) for i in range(2)]
    psU = [fstk.enter_context(nc.psum_tensor(f"psU{i}", [128, 512], F32)) for i in range(2)]

    P.wait("sy", "w_ag")
    P.wait("sy", "b_done")
    P.wait("t", "b_done")
    for eng in ("sy", "g", "s"):
        P.wait(eng, "b_end_t")
        P.wait(eng, "b_end_v")
        P.wait(eng, "b_end_s")
        for sem in ("xld0", "xld1", "og0", "og1", "oT0", "oT1", "r1s0", "r1s1",
                    "xqT", "g"):
            P.wait_now(eng, sem)
    tr_g = [0]
    pend = [None]

    def flushq():
        if pend[0] is None:
            return
        stq, ic0, pb0 = pend[0]
        pend[0] = None
        for tb0 in (pb0 * 2, pb0 * 2 + 1):
            gb0 = ic0 * NTB + tb0
            if gb0 >= 4:
                gq = gb0 - 4
                P.wait("v", f"f_hqT_{gq // NTB}_{gq % NTB}")
        emit_quant_multi_b(P, stq)
        P.fence_inc("v", "v")
        P.mark(f"f_q_{ic0}_{pb0}", "v")
        P.wait("sy", f"f_q_{ic0}_{pb0}")
        for tb0 in (pb0 * 2, pb0 * 2 + 1):
            qb = (ic0 * NTB + tb0) % 4
            for ti in range(FTB):
                tt = tb0 * FTB + ti
                fp = tr_g[0] % 2
                if tr_g[0] >= 2:
                    P.wait("sy", f"f_hqst_g{tr_g[0] - 2}")
                P.inc("sy", lambda h, qb=qb, ti=ti, fp=fp: h.dma_start_transpose(
                    hq_fm[fp][:, :, :], hq_b[qb][:, ti * 512:(ti + 1) * 512]), f"hqT{fp}", 16)
                P.mark(f"f_hqTr_g{tr_g[0]}", f"hqT{fp}")
                P.wait("g", f"f_hqTr_g{tr_g[0]}")
                def mk_st(g, tt=tt, ic0=ic0, fp=fp):
                    dst = T["hq_sp"][tt, :, ic0 * 4:(ic0 + 1) * 4, :]
                    return g.dma_start(out=dst, in_=hq_fm[fp][:, :, :])
                P.inc("g", mk_st, f"hqs{fp}", 16)
                P.mark(f"f_hqst_g{tr_g[0]}", f"hqs{fp}")
                tr_g[0] += 1
            P.mark(f"f_hqT_{ic0}_{tb0}", f"hqs{(tr_g[0] - 1) % 2}")

    for ic in range(NIC):
        wb = ic % 2
        if ic >= 2:
            P.wait("sy", f"f_gu_{ic - 2}")
        for wsb_t, woff in ((wqg[wb], 0), (wqu[wb], SHW)):
            def mk_ld(h, wsb_t=wsb_t, woff=woff, ic=ic):
                src = T["cc_wq_out"][ic * 3 * SHW + woff: ic * 3 * SHW + woff + SHW].rearrange(
                    "(m p r) -> p m r", p=128, r=512)
                return h.dma_start(out=wsb_t[:, :, :], in_=src)
            P.inc("sy", mk_ld, f"fw{ic % 2}", 16)
        P.mark(f"f_wld_{ic}", f"fw{ic % 2}")

        for pb in range(NTB // 2):
            streams = []
            gp2 = (ic * (NTB // 2) + pb) % 2
            for si in range(2):
                tb = pb * 2 + si
                gb = ic * NTB + tb
                hb = gb % 4
                for ti in range(FTB):
                    tt = tb * FTB + ti
                    tp = tt % 2
                    P.wait("t", f"f_wld_{ic}")
                    if not (ic == 0 and tt < 2):
                        prev = (ic, tt - 2) if tt >= 2 else (ic - 1, NTT - 2 + tt)
                        P.wait("t", f"f_h_{prev[0]}_{prev[1]}")
                    for m in range(8):
                        P.bare("t", lambda t, tt=tt, m=m, wb=wb, tp=tp: t.matmul(
                            psG[tp][:, :], xq_res[:, tt * 1024 + m * 128: tt * 1024 + (m + 1) * 128],
                            wqg[wb][:, m, :], start=(m == 0), stop=(m == 7)))
                    for m in range(8):
                        P.bare("t", lambda t, tt=tt, m=m, wb=wb, tp=tp: t.matmul(
                            psU[tp][:, :], xq_res[:, tt * 1024 + m * 128: tt * 1024 + (m + 1) * 128],
                            wqu[wb][:, m, :], start=(m == 0), stop=(m == 7)))
                    P.fence_inc("t", "t")
                    P.mark(f"f_gu_t_{ic}_{tt}", "t")
                    if tt == NTT - 1:
                        P.mark(f"f_gu_{ic}", "t")
                    P.wait("s", f"f_gu_t_{ic}_{tt}")
                    if not (ic == 0 and tt < 2):
                        prev = (ic, tt - 2) if tt >= 2 else (ic - 1, NTT - 2 + tt)
                        P.wait("s", f"f_h_{prev[0]}_{prev[1]}")
                    P.bare("s", lambda s, tp=tp: s.activation(sg_t[tp][:, :], psG[tp][:, :], AF.Silu))
                    P.drain("s")
                    P.inc("s", lambda s, tp=tp: s.copy(up_t[tp][:, :], psU[tp][:, :]), "s", 1)
                    P.mark(f"f_sg_{ic}_{tt}", "s")
                    P.wait("v", f"f_sg_{ic}_{tt}")
                    P.bare("v", lambda v, tp=tp, hb=hb, ti=ti: v.tensor_tensor(
                        h_b[hb][:, ti * 512:(ti + 1) * 512], sg_t[tp][:, :], up_t[tp][:, :], ALU.mult))
                    P.fence_inc("v", "v")
                    P.mark(f"f_h_{ic}_{tt}", "v")
                    if ti > 0:
                        am_sl = smlF[2 * gp2 + si][:, (ti - 1) * 32:ti * 32]
                        h3 = h_b[hb][:, (ti - 1) * 512:ti * 512].rearrange(
                            "p (b s) -> p b s", s=16)
                        P.bare("v", lambda v, am_sl=am_sl, h3=h3: v.tensor_reduce(
                            am_sl, h3, AX.X, ALU.max, apply_absolute_value=True))
                sd = qstream(pp, f"fq{ic}_{tb}", h_b[hb][:, :],
                             hq_b[gb % 4][:, :], FTB * 512, 4,
                             tA[si][:, :], tC[si], tD[si], tE[si],
                             smlF[2 * gp2 + si][:, :], s8F[2 * gp2 + si][:, :])
                sd["pre_amax"] = ((FTB - 1) * 32, FTB * 32,
                                  h_b[hb][:, (FTB - 1) * 512:FTB * 512].rearrange(
                                      "p (b s) -> p b s", s=16))
                streams.append(sd)
            stq = emit_quant_multi_a(P, streams)
            flushq()
            pend[0] = (stq, ic, pb)
    flushq()
    P.fence_nop("g", "gc")
    P.mark("fa_done", "gc")
    P.fence_nop("t", "t")
    P.mark("fa_end_t", "t")
    P.fence_nop("v", "v")
    P.mark("fa_end_v", "v")
    P.fence_nop("s", "s")
    P.mark("fa_end_s", "s")


def build_fb(nc, P, f2stk, T):
    def fsb(name, shape, dt, **kw):
        return f2stk.enter_context(nc.sbuf_tensor(name, shape, dt, **kw))

    wqd = fsb("wqd", [128, 4, NIC, 1024], BF16)
    hq_i = [fsb(f"hq_i{i}", [128, 32, 128], BF16) for i in range(2)]
    r1t = [fsb(f"r1tf{i}", [128, D], F32) for i in range(2)]
    o_t = [fsb(f"o_tf{i}", [128, D], F32) for i in range(2)]
    psD = [f2stk.enter_context(nc.psum_tensor(f"psD{i}", [128, 1024], F32)) for i in range(2)]

    P.wait("sy", "fa_done")
    P.wait_now("sy", "hqs0")
    P.wait_now("sy", "hqs1")
    for eng in ("sy", "g", "s"):
        P.wait(eng, "fa_end_t")
        P.wait(eng, "fa_end_v")
        P.wait(eng, "fa_end_s")
        for sem in ("fw0", "fw1", "hqT0", "hqT1", "hqs0", "hqs1", "g"):
            P.wait_now(eng, sem)
    for ic in range(NIC):
        def mk_ld(h, ic=ic):
            src = T["cc_wq_out"][ic * 3 * SHW + 2 * SHW: ic * 3 * SHW + 3 * SHW].rearrange(
                "(su p r) -> p su r", p=128, r=1024)
            return h.dma_start(out=wqd[:, :, ic, :], in_=src)
        P.inc("sy", mk_ld, "sy", 16)
    P.mark("fb_wld", "sy")

    for tt in range(NTT):
        bi = tt % 2
        if tt >= 2:
            P.wait("sy", f"fb_mm_{tt - 2}")
        P.inc("sy", lambda h, tt=tt, bi=bi: h.dma_start(
            out=hq_i[bi][:, :, :], in_=T["hq_sp"][tt, :, :, :]), f"fbl{bi}", 16)
        P.inc("sy", lambda h, tt=tt, bi=bi: h.dma_start(
            out=r1t[bi][:, :], in_=T["r1_sp"][tt * 128:(tt + 1) * 128, :]), f"fbl{bi}", 16)
        P.mark(f"fb_ld_{tt}", f"fbl{bi}")

        P.wait("t", f"fb_ld_{tt}")
        if tt == 0:
            P.wait("t", "fb_wld")
        if tt >= 2:
            P.wait("t", f"fb_oev_{tt - 2}")
        for dh in range(2):
            n = 0
            for ic in range(NIC):
                for su in range(4):
                    n += 1
                    P.bare("t", lambda t, bi=bi, ic=ic, su=su, dh=dh, n=n: t.matmul(
                        psD[bi][:, dh * 512:(dh + 1) * 512], hq_i[bi][:, ic * 4 + su, :],
                        wqd[:, su, ic, dh * 512:(dh + 1) * 512],
                        start=(n == 1), stop=(n == 32)))
        P.fence_inc("t", "t")
        P.mark(f"fb_mm_{tt}", "t")

        P.wait("v", f"fb_mm_{tt}")
        P.bare("v", lambda v, bi=bi: v.tensor_tensor(o_t[bi][:, 0:512], psD[bi][:, 0:512],
                                                     r1t[bi][:, 0:512], ALU.add))
        P.bare("v", lambda v, bi=bi: v.tensor_tensor(o_t[bi][:, 512:1024], psD[bi][:, 512:1024],
                                                     r1t[bi][:, 512:1024], ALU.add))
        P.fence_inc("v", "v")
        P.mark(f"fb_oev_{tt}", "v")

        P.wait("sy", f"fb_oev_{tt}")
        P.inc("sy", lambda h, tt=tt, bi=bi: h.dma_start(
            out=T["out_half"][tt * 128:(tt + 1) * 128, :], in_=o_t[bi][:, :]), "sy", 16)
        P.mark(f"fb_out_{tt}", "sy")

    P.wait_now("sy", "sy")


def _host_prepare(inputs):
    x = np.ascontiguousarray(inputs["L0_attn_x"])
    Wq, Wk, Wv, Wo = (inputs["L0_attn_W_q"], inputs["L0_attn_W_k"],
                      inputs["L0_attn_W_v"], inputs["L0_attn_W_o"])
    gw, uw, dw = inputs["L0_ffn_gate_w"], inputs["L0_ffn_up_w"], inputs["L0_ffn_down_w"]
    s_in = float(np.asarray(inputs["L0_ffn_s_in"]).reshape(-1)[0])
    s_ind = float(np.asarray(inputs["L0_ffn_s_in_down"]).reshape(-1)[0])

    triu = np.triu(np.ones((128, 128), np.float32))
    mask512 = np.tile(triu, (1, 4)).astype(np.float32)

    in_maps = []
    for c in range(N_CORES):
        b, p = c // 2, c % 2
        hs = 512 * p
        scal = np.zeros((128, 8), np.float32)
        scal[:, 0] = 1e-5
        scal[:, 1] = s_in
        scal[:, 2] = s_ind
        scal[:, 3] = 1.0 - p
        scal[:, 4] = float(p)
        scal[:, 5] = float(p)
        scal[:, 6] = 1.0 - p
        partner = c ^ 1
        ntt = T_HALF // 128
        qr = T_HALF // NQ
        s_idx = np.arange(T_HALF)
        local = s_idx % qr
        gidx_v = (partner * qr + local).astype(np.int32).reshape(ntt, 128).T
        gidx_v = np.ascontiguousarray(gidx_v)
        wo_arr = np.concatenate([Wo[hs:hs + 512, :], Wo[512 - hs:512 - hs + 512, :]], axis=0)
        in_maps.append({
            "x_full": np.ascontiguousarray(x[b]),
            "x_half": np.ascontiguousarray(x[b, p * T_HALF:(p + 1) * T_HALF]),
            "wqi": np.ascontiguousarray(Wq[:, hs:hs + 512]),
            "wki": np.ascontiguousarray(Wk[:, hs:hs + 512]),
            "wvi": np.ascontiguousarray(Wv[:, hs:hs + 512]),
            "woi": np.ascontiguousarray(wo_arr),
            "gshi": np.ascontiguousarray(gw[c * 512:(c + 1) * 512, :]),
            "ushi": np.ascontiguousarray(uw[c * 512:(c + 1) * 512, :]),
            "dshi": np.ascontiguousarray(dw[:, c * 512:(c + 1) * 512]),
            "maski": mask512,
            "scali": scal,
            "gidxi": gidx_v,
        })
    return in_maps


def kernel(**inputs):
    in_maps = _host_prepare(inputs)
    nc = build_nc()
    res = run_bass_kernel_spmd(nc, in_maps, core_ids=list(range(N_CORES)))
    out = np.zeros((4, T_FULL, D), np.float32)
    for c in range(N_CORES):
        b, p = c // 2, c % 2
        out[b, p * T_HALF:(p + 1) * T_HALF, :] = np.asarray(res.results[c]["out_half"])
    return out


# version 27
# speedup vs baseline: 1.0261x; 1.0261x over previous
import contextlib
import numpy as np
import ml_dtypes
import concourse.bass as bass
import concourse.mybir as mybir
from concourse.bass_utils import run_bass_kernel_spmd
from concourse.bass import IndirectOffsetOnAxis

F32 = mybir.dt.float32
F8 = mybir.dt.float8e4
BF16 = mybir.dt.bfloat16
I32 = mybir.dt.int32
AF = mybir.ActivationFunctionType
ALU = mybir.AluOpType
AX = mybir.AxisListType

N_CORES = 8
D = 1024
NH = 8
C = 128
I_FFN = 4096
IC = 512
NIC = I_FFN // IC
SHW = D * 512
LN2C = float(np.log(2.0))
GSW_CONST = 448.0 * 6.0

T_FULL = 8192
T_HALF = 4096
NCH = 64
NSL = 32
NTT = 32
BQ = 4
NQ = 4
FTB = 4
NTB = 8


def set_dims(t_full):
    global T_FULL, T_HALF, NCH, NSL, NTT, BQ, FTB, NTB
    T_FULL = t_full
    T_HALF = t_full // 2
    NCH = T_FULL // C
    NSL = NCH // 2
    NTT = T_HALF // C
    BQ = min(4, NTT)
    FTB = max(1, min(4, NTT // 2))
    NTB = NTT // FTB
    assert NTB % 2 == 0
    global NQ
    NQ = 4 if (T_HALF // 4) % 128 == 0 and NSL % 4 == 0 else 2


set_dims(8192)
DEBUG = False
DBG_CI = 0

ENGINES = ("t", "v", "s", "sy", "g")


class Prog:

    def __init__(self):
        self.streams = {e: [] for e in ENGINES}
        self.cnt = {}
        self.marks = {}
        self.sems = {}

    def sem(self, name, handle):
        self.sems[name] = handle
        self.cnt[name] = 0

    def bare(self, eng, fn):
        self.streams[eng].append(fn)

    def inc(self, eng, fn, sem, n):
        s = self.sems[sem]
        self.cnt[sem] += n
        self.streams[eng].append(lambda h: fn(h).then_inc(s, n))

    def mark(self, name, sem):
        self.marks[name] = (sem, self.cnt[sem])

    def wait(self, eng, name):
        sem_name, val = self.marks[name]
        s = self.sems[sem_name]
        if val > 0:
            self.streams[eng].append(lambda h: h.wait_ge(s, val))

    def drain(self, eng):
        self.streams[eng].append(lambda h: h.drain())

    def wait_now(self, eng, sem):
        s = self.sems[sem]
        val = self.cnt[sem]
        if val > 0:
            self.streams[eng].append(lambda h: h.wait_ge(s, val))

    def fence_inc(self, eng, sem, n=1):
        s = self.sems[sem]
        self.cnt[sem] += n
        fn = self.streams[eng].pop()
        self.streams[eng].append(lambda h, fn=fn: fn(h).then_inc(s, n))

    def fence_nop(self, eng, sem, n=1):
        s = self.sems[sem]
        self.cnt[sem] += n
        self.streams[eng].append(lambda h: h.drain())
        self.streams[eng].append(lambda h: h.nop().then_inc(s, n))


def emit_quant_multi_a(P, streams):
    par = getattr(P, "q8c", 0) % 2
    P.q8c = getattr(P, "q8c", 0) + 1
    st = []
    for s_ in streams:
        nblk = s_["nel"] // 16
        sml = s_["sml"]
        d = dict(s_)
        d["nblk"] = nblk
        d["am"] = sml[:, 0 * nblk:1 * nblk]
        d["scb"] = sml[:, 1 * nblk:2 * nblk]
        d["sch"] = sml[:, 2 * nblk:3 * nblk]
        d["sa"] = sml[:, 3 * nblk:4 * nblk]
        d["sb"] = sml[:, 4 * nblk:5 * nblk]
        d["sel"] = sml[:, 5 * nblk:6 * nblk]
        d["u"] = sml[:, 6 * nblk:7 * nblk]
        d["dq"] = sml[:, 7 * nblk:8 * nblk]
        d["c8a"] = s_["s8"][:, 0:nblk]
        d["c8b"] = s_["s8"][:, nblk:2 * nblk]
        d["x3"] = s_["xin"].rearrange("p (b s) -> p b s", s=16)
        st.append(d)

    def each(fn):
        for d in st:
            fn(d)

    tag0 = st[0]["tag"]
    P.drain("v")
    for d in st:
        if "pre_amax" in d:
            lt = d["pre_amax"]
            P.bare("v", lambda v, d=d, lt=lt: v.tensor_reduce(
                d["am"][:, lt[0]:lt[1]], lt[2], AX.X, ALU.max, apply_absolute_value=True))
        else:
            P.bare("v", lambda v, d=d: v.tensor_reduce(
                d["am"], d["x3"], AX.X, ALU.max, apply_absolute_value=True))
    P.drain("v")
    each(lambda d: P.bare("v", lambda v, d=d: v.tensor_scalar(
        d["scb"], d["am"], d["gs6"], 192.0, ALU.mult, ALU.min)))
    each(lambda d: P.bare("v", lambda v, d=d: v.tensor_scalar(
        d["sch"], d["am"], d["gs6h"], 224.0, ALU.mult, ALU.min)))
    P.fence_inc("v", "v")
    P.mark(f"{tag0}_sc", "v")
    P.wait("g", f"{tag0}_sc")
    each(lambda d: P.inc("g", lambda g, d=d: g.dma_start(out=d["c8a"], in_=d["sch"]),
                         f"f8d{par}", 16))
    each(lambda d: P.inc("g", lambda g, d=d: g.dma_start(out=d["c8b"], in_=d["scb"]),
                         f"f8d{par}", 16))
    P.mark(f"{tag0}_8d", f"f8d{par}")
    P.wait("g", f"{tag0}_8d")
    each(lambda d: P.inc("g", lambda g, d=d: g.dma_start(out=d["sa"], in_=d["c8a"]),
                         f"f8u{par}", 16))
    each(lambda d: P.inc("g", lambda g, d=d: g.dma_start(out=d["sb"], in_=d["c8b"]),
                         f"f8u{par}", 16))
    P.mark(f"{tag0}_8u", f"f8u{par}")
    return st


def emit_quant_multi_b(P, st):
    def each(fn):
        for d in st:
            fn(d)

    tag0 = st[0]["tag"]
    P.wait("v", f"{tag0}_8u")
    P.drain("v")
    each(lambda d: P.bare("v", lambda v, d=d: v.tensor_scalar(
        d["sa"], d["sa"], d["invgs2"], None, ALU.mult)))
    each(lambda d: P.bare("v", lambda v, d=d: v.tensor_scalar(
        d["sb"], d["sb"], d["invgs"], None, ALU.mult)))
    each(lambda d: P.bare("v", lambda v, d=d: v.tensor_scalar(
        d["sel"], d["scb"], 1.0, None, ALU.is_ge)))
    P.drain("v")
    each(lambda d: P.bare("v", lambda v, d=d: v.tensor_tensor(d["u"], d["sa"], d["sb"],
                                                              ALU.subtract)))
    P.drain("v")
    each(lambda d: P.bare("v", lambda v, d=d: v.tensor_tensor(d["u"], d["u"], d["sel"],
                                                              ALU.mult)))
    P.drain("v")
    each(lambda d: P.bare("v", lambda v, d=d: v.tensor_tensor(d["dq"], d["u"], d["sb"],
                                                              ALU.add)))
    P.drain("v")
    each(lambda d: P.bare("v", lambda v, d=d: v.tensor_scalar(
        d["am"], d["dq"], 1e-30, None, ALU.max)))
    P.drain("v")
    each(lambda d: P.bare("v", lambda v, d=d: v.reciprocal(d["sel"], d["am"])))
    P.drain("v")
    each(lambda d: P.bare("v", lambda v, d=d: v.tensor_tensor(
        d["x3"], d["x3"],
        d["sel"].unsqueeze(2).broadcast_to([128, d["nblk"], 16]), ALU.mult)))
    P.fence_inc("v", "v")
    P.mark(f"{tag0}_t", "v")
    P.wait("s", f"{tag0}_t")
    each(lambda d: P.bare("s", lambda s, d=d: s.activation(d["tA"], d["xin"], AF.Abs)))
    P.fence_inc("s", "s")
    P.mark(f"{tag0}_abs", "s")
    P.wait("v", f"{tag0}_abs")
    each(lambda d: P.bare("v", lambda v, d=d: v.tensor_scalar(
        d["tC"], d["tA"], 0.25, None, ALU.is_ge)))
    each(lambda d: P.bare("v", lambda v, d=d: v.tensor_scalar(
        d["xq_out"], d["tA"], 0.75, None, ALU.is_ge)))
    each(lambda d: P.bare("v", lambda v, d=d: v.tensor_scalar(
        d["tD"], d["tA"], 5.0, 2.0, ALU.is_ge, ALU.mult)))
    each(lambda d: P.bare("v", lambda v, d=d: v.tensor_scalar(
        d["tE"], d["tA"], 3.5, None, ALU.is_ge)))
    P.drain("v")
    each(lambda d: P.bare("v", lambda v, d=d: v.scalar_tensor_tensor(
        d["tC"], d["tA"], 1.25, d["tC"], ALU.is_ge, ALU.add)))
    each(lambda d: P.bare("v", lambda v, d=d: v.scalar_tensor_tensor(
        d["xq_out"], d["tA"], 1.75, d["xq_out"], ALU.is_ge, ALU.add)))
    each(lambda d: P.bare("v", lambda v, d=d: v.scalar_tensor_tensor(
        d["tD"], d["tA"], 2.5, d["tD"], ALU.is_ge, ALU.add)))
    P.fence_inc("v", "v")
    P.mark(f"{tag0}_ch", "v")
    P.wait("s", f"{tag0}_ch")
    each(lambda d: P.bare("s", lambda s, d=d: s.activation(d["tA"], d["xin"], AF.Sign)))
    P.fence_inc("s", "s")
    P.mark(f"{tag0}_sgn", "s")
    P.drain("v")
    each(lambda d: P.bare("v", lambda v, d=d: v.tensor_tensor(d["tC"], d["tC"], d["xq_out"],
                                                              ALU.add)))
    each(lambda d: P.bare("v", lambda v, d=d: v.tensor_tensor(d["tD"], d["tD"], d["tE"],
                                                              ALU.add)))
    P.wait("v", f"{tag0}_sgn")
    P.drain("v")
    each(lambda d: P.bare("v", lambda v, d=d: v.scalar_tensor_tensor(
        d["tC"], d["tC"], 0.5, d["tD"], ALU.mult, ALU.add)))
    P.drain("v")
    each(lambda d: P.bare("v", lambda v, d=d: v.tensor_tensor(d["tC"], d["tC"], d["tA"],
                                                              ALU.mult)))
    P.drain("v")
    each(lambda d: P.bare("v", lambda v, d=d: v.tensor_tensor(
        d["xq_out"].rearrange("p (b s) -> p b s", s=16),
        d["tC"].rearrange("p (b s) -> p b s", s=16),
        d["dq"].unsqueeze(2).broadcast_to([128, d["nblk"], 16]), ALU.mult)))
    P.drain("v")


def emit_quant_multi(P, streams):
    emit_quant_multi_b(P, emit_quant_multi_a(P, streams))


def qstream(pp, tag, xin, xq_out, nel, ui, tA, tC, tD, tE, sml, s8):
    return dict(tag=tag, xin=xin, xq_out=xq_out, nel=nel,
                gs6=pp[:, 2 * ui:2 * ui + 1], invgs=pp[:, 2 * ui + 1:2 * ui + 2],
                gs6h=pp[:, 18 + 2 * ui:19 + 2 * ui], invgs2=pp[:, 19 + 2 * ui:20 + 2 * ui],
                tA=tA, tC=tC, tD=tD, tE=tE, sml=sml, s8=s8)


def emit_quant(P, pp, tag, xin, xq_out, nel, ui, tA, tC, tD, tE, sml, s8):
    emit_quant_multi(P, [qstream(pp, tag, xin, xq_out, nel, ui, tA, tC, tD, tE, sml, s8)])


def emit_ln(P, tag, x_ap, out_ap, bns, mv, stdv, inv, negm, eps_pp, prev_tag):
    if prev_tag is not None:
        P.wait("v", prev_tag)
    P.drain("v")
    P.bare("v", lambda v: v.bn_stats(bns[:, 0:6], x_ap[:, 0:512]))
    P.drain("v")
    P.bare("v", lambda v: v.bn_stats(bns[:, 6:12], x_ap[:, 512:1024]))
    P.drain("v")
    P.bare("v", lambda v: v.bn_aggr(mv, bns))
    P.fence_inc("v", "v")
    P.mark(f"{tag}_bns", "v")
    P.wait("s", f"{tag}_bns")
    P.inc("s", lambda s: s.activation(stdv, mv[:, 1:2], AF.Sqrt, bias=eps_pp, scale=1.0), "s", 1)
    P.mark(f"{tag}_std", "s")
    P.wait("v", f"{tag}_std")
    P.bare("v", lambda v: v.reciprocal(inv, stdv))
    P.drain("v")
    P.bare("v", lambda v: v.tensor_scalar(negm, mv[:, 0:1], inv[:, 0:1], -1.0, ALU.mult, ALU.mult))
    P.drain("v")
    P.bare("v", lambda v: v.tensor_scalar(out_ap, x_ap, inv[:, 0:1], negm[:, 0:1],
                                          ALU.mult, ALU.add))
    P.fence_inc("v", "v")
    P.mark(f"{tag}_ln", "v")


def emit_ln2(P, tagA, xA, outA, tagB, xB, outB, st2, eps_pp, prev_tag):
    if prev_tag is not None:
        P.wait("v", prev_tag)
    P.drain("v")
    P.bare("v", lambda v: v.bn_stats(st2["bnsA"][:, 0:6], xA[:, 0:512]))
    P.bare("v", lambda v: v.bn_stats(st2["bnsB"][:, 0:6], xB[:, 0:512]))
    P.drain("v")
    P.bare("v", lambda v: v.bn_stats(st2["bnsA"][:, 6:12], xA[:, 512:1024]))
    P.bare("v", lambda v: v.bn_stats(st2["bnsB"][:, 6:12], xB[:, 512:1024]))
    P.drain("v")
    P.bare("v", lambda v: v.bn_aggr(st2["mvA"][:, :], st2["bnsA"][:, :]))
    P.bare("v", lambda v: v.bn_aggr(st2["mvB"][:, :], st2["bnsB"][:, :]))
    P.fence_inc("v", "v")
    P.mark(f"{tagA}_bns", "v")
    P.wait("s", f"{tagA}_bns")
    P.bare("s", lambda s: s.activation(st2["stdA"][:, :], st2["mvA"][:, 1:2], AF.Sqrt,
                                       bias=eps_pp, scale=1.0))
    P.drain("s")
    P.inc("s", lambda s: s.activation(st2["stdB"][:, :], st2["mvB"][:, 1:2], AF.Sqrt,
                                      bias=eps_pp, scale=1.0), "s", 1)
    P.mark(f"{tagA}_std", "s")
    P.wait("v", f"{tagA}_std")
    P.bare("v", lambda v: v.reciprocal(st2["invA"][:, :], st2["stdA"][:, :]))
    P.bare("v", lambda v: v.reciprocal(st2["invB"][:, :], st2["stdB"][:, :]))
    P.drain("v")
    P.bare("v", lambda v: v.tensor_scalar(st2["nmA"][:, :], st2["mvA"][:, 0:1], st2["invA"][:, 0:1],
                                          -1.0, ALU.mult, ALU.mult))
    P.bare("v", lambda v: v.tensor_scalar(st2["nmB"][:, :], st2["mvB"][:, 0:1], st2["invB"][:, 0:1],
                                          -1.0, ALU.mult, ALU.mult))
    P.drain("v")
    P.bare("v", lambda v: v.tensor_scalar(outA, xA, st2["invA"][:, 0:1], st2["nmA"][:, 0:1],
                                          ALU.mult, ALU.add))
    P.bare("v", lambda v: v.tensor_scalar(outB, xB, st2["invB"][:, 0:1], st2["nmB"][:, 0:1],
                                          ALU.mult, ALU.add))
    P.fence_inc("v", "v")
    P.mark(f"{tagA}_ln", "v")
    P.mark(f"{tagB}_ln", "v")


def build_nc():
    nc = bass.Bass("TRN2", target_bir_lowering=False)
    P = Prog()

    x_full = nc.dram_tensor("x_full", [T_FULL, D], F32, kind="ExternalInput").ap()
    x_half = nc.dram_tensor("x_half", [T_HALF, D], F32, kind="ExternalInput").ap()
    wqi = nc.dram_tensor("wqi", [D, 512], F32, kind="ExternalInput").ap()
    wki = nc.dram_tensor("wki", [D, 512], F32, kind="ExternalInput").ap()
    wvi = nc.dram_tensor("wvi", [D, 512], F32, kind="ExternalInput").ap()
    woi = nc.dram_tensor("woi", [D, D], F32, kind="ExternalInput").ap()
    gshi = nc.dram_tensor("gshi", [512, D], F32, kind="ExternalInput").ap()
    ushi = nc.dram_tensor("ushi", [512, D], F32, kind="ExternalInput").ap()
    dshi = nc.dram_tensor("dshi", [D, 512], F32, kind="ExternalInput").ap()
    maski = nc.dram_tensor("maski", [128, 512], F32, kind="ExternalInput").ap()
    scali = nc.dram_tensor("scali", [128, 8], F32, kind="ExternalInput").ap()
    gidxi = nc.dram_tensor("gidxi", [128, NTT], I32, kind="ExternalInput").ap()
    out_half = nc.dram_tensor("out_half", [T_HALF, D], F32, kind="ExternalOutput").ap()

    cc_am_in = nc.dram_tensor("cc_am_in", [1, 3], F32).ap()
    cc_am_out = nc.dram_tensor("cc_am_out", [1, 3], F32).ap()
    cc_wq_in = nc.dram_tensor("cc_wq_in", [3 * SHW], BF16).ap()
    cc_wq_out = nc.dram_tensor("cc_wq_out", [N_CORES * 3 * SHW], BF16).ap()
    cc_o_in = nc.dram_tensor("cc_o_in", [T_HALF, 512], BF16).ap()
    cc_o_out_q = [nc.dram_tensor(f"cc_o_out{k}", [N_CORES * T_HALF // NQ, 512], BF16).ap()
                  for k in range(NQ)]
    r1_sp = nc.dram_tensor("r1_sp", [T_HALF, D], F32).ap()
    dbg_d = nc.dram_tensor("dbg_d", [128, 7680], F32).ap()
    dbg2_d = nc.dram_tensor("dbg2_d", [128, NTT * 1024], BF16).ap()
    dbg3_d = nc.dram_tensor("dbg3_d", [128, BQ * 1024 + 8 * BQ * 64], F32).ap()
    hq_sp = nc.dram_tensor("hq_sp", [NTT, 128, 32, 128], BF16).ap()

    T = dict(dbg_d=dbg_d, dbg2_d=dbg2_d, dbg3_d=dbg3_d, x_full=x_full, x_half=x_half, wqi=wqi, wki=wki, wvi=wvi, woi=woi,
             gshi=gshi, ushi=ushi, dshi=dshi, maski=maski, scali=scali, gidxi=gidxi,
             out_half=out_half, cc_am_in=cc_am_in, cc_am_out=cc_am_out,
             cc_wq_in=cc_wq_in, cc_wq_out=cc_wq_out, cc_o_in=cc_o_in,
             cc_o_out_q=cc_o_out_q, r1_sp=r1_sp, hq_sp=hq_sp)

    with contextlib.ExitStack() as stk:
        def sbp(name, shape, dt, **kw):
            return stk.enter_context(nc.sbuf_tensor(name, shape, dt, **kw))

        consts = sbp("consts", [128, 8], F32)
        maskt = sbp("maskt", [128, 512], F32)
        sk_sb = [sbp(f"sk_sb{i}", [64, 512], BF16) for i in range(2)]
        sk_f32 = sbp("sk_f32", [64, 512], F32)
        z_sb = sbp("z_sb", [64, 512], BF16)
        z_f32 = sbp("z_f32", [64, 512], F32)
        pp = sbp("pp", [128, 28], F32)
        gidx = sbp("gidx", [128, NTT], I32)
        T.update(consts=consts, maskt=maskt, sk_sb=sk_sb, sk_f32=sk_f32, z_sb=z_sb,
                 z_f32=z_f32, pp=pp, gidx=gidx)

        semnames = (["sy", "sct", "g", "gc", "cc", "t", "v", "s", "wtr", "wst", "xqT"]
                    + [f"{k}{i}" for k in ("xld", "lnT", "qkT", "wT", "ost", "og", "oT",
                                           "r1s", "fw", "hqT", "hqs", "fbl") for i in (0, 1)]
                    + ["qkT2", "f8d0", "f8d1", "f8u0", "f8u1"])
        for nm in semnames:
            h = stk.enter_context(nc.semaphore("s_" + nm))
            P.sem(nm, h)

        with nc.sbuf_tensor("o_my", [128, NSL * 512], BF16) as o_my:
            T["o_my"] = o_my
            qstk = contextlib.ExitStack()
            for nm in ("wq_sb", "wk_sb", "wv_sb"):
                T[nm] = qstk.enter_context(nc.sbuf_tensor(nm, [128, 8, 512], BF16))
            wstk = contextlib.ExitStack()
            build_w_head(nc, P, wstk, T)
            with contextlib.ExitStack() as astk:
                build_attn(nc, P, astk, T)
            wstk.close()
            qstk.close()
            with nc.sbuf_tensor("xq_res", [128, NTT * 8 * 128], BF16,
                                is_transpose_dest=True) as xq_res:
                T["xq_res"] = xq_res
                with contextlib.ExitStack() as bstk:
                    build_b(nc, P, bstk, T)
                with contextlib.ExitStack() as fstk:
                    build_fa(nc, P, fstk, T)
        with contextlib.ExitStack() as f2stk:
            build_fb(nc, P, f2stk, T)

        with nc.Block() as blk:
            @blk.tensor
            def _(t):
                for fn in P.streams["t"]:
                    fn(t)

            @blk.vector
            def _(v):
                for fn in P.streams["v"]:
                    fn(v)

            @blk.scalar
            def _(s):
                for fn in P.streams["s"]:
                    fn(s)

            @blk.sync
            def _(sy):
                for fn in P.streams["sy"]:
                    fn(sy)

            @blk.gpsimd
            def _(g):
                for fn in P.streams["g"]:
                    fn(g)

    return nc


def build_w_head(nc, P, wstk, T):
    consts, pp = T["consts"], T["pp"]
    o_my = T["o_my"]
    wsh = [wstk.enter_context(nc.sbuf_tensor(f"wsh{i}", [128, 4096], F32)) for i in range(3)]
    wqt = wstk.enter_context(nc.sbuf_tensor("wqt", [128, 4096], BF16))
    scr1 = wstk.enter_context(nc.sbuf_tensor("scr1", [128, 2048], BF16))
    scr2 = wstk.enter_context(nc.sbuf_tensor("scr2", [128, 2048], BF16))
    scr3 = wstk.enter_context(nc.sbuf_tensor("scr3", [128, 2048], BF16))
    if NSL * 512 - 23 * 512 >= 4096:
        scr4 = o_my[:, 23 * 512:23 * 512 + 4096]
    else:
        scr4 = wstk.enter_context(nc.sbuf_tensor("scr4", [128, 4096], BF16))[:, :]
    sml = wstk.enter_context(nc.sbuf_tensor("smlW", [128, 8 * 128], F32))
    s8W = wstk.enter_context(nc.sbuf_tensor("s8W", [128, 256], F8))
    wfm = wstk.enter_context(nc.sbuf_tensor("wfm", [128, 8, 128], BF16, is_transpose_dest=True))

    P.inc("sy", lambda h: h.dma_start(
        out=wsh[0][:, :].rearrange("p (j f) -> p j f", f=D),
        in_=T["gshi"].rearrange("(j p) f -> p j f", p=128)), "sy", 16)
    P.inc("sy", lambda h: h.dma_start(
        out=wsh[1][:, :].rearrange("p (j f) -> p j f", f=D),
        in_=T["ushi"].rearrange("(j p) f -> p j f", p=128)), "sy", 16)
    P.inc("sy", lambda h: h.dma_start(
        out=wsh[2][:, :].rearrange("p (j f) -> p j f", f=512),
        in_=T["dshi"].rearrange("(j p) f -> p j f", p=128)), "sy", 16)
    P.inc("sy", lambda h: h.dma_start(out=T["consts"][:, :], in_=T["scali"][:, :]), "sy", 16)
    P.inc("sy", lambda h: h.dma_start(out=T["maskt"][:, :], in_=T["maski"][:, :]), "sy", 16)
    P.inc("sy", lambda h: h.dma_start(out=T["gidx"][:, :], in_=T["gidxi"][:, :]), "sy", 16)
    P.mark("w_ld", "sy")
    for wsrc, wdst in ((T["wqi"], T["wq_sb"]), (T["wki"], T["wk_sb"]),
                       (T["wvi"], T["wv_sb"])):
        P.inc("g", lambda g, wsrc=wsrc, wdst=wdst: g.dma_start(
            out=wdst[:, :, :], in_=wsrc.rearrange("(m p) c -> p m c", p=128)), "sct", 16)
    P.mark("a_wld", "sct")

    P.bare("v", lambda v: v.memset(pp[:, 16:17], -6.0 * LN2C))
    P.bare("v", lambda v: v.memset(pp[:, 17:18], 6.0 * LN2C))
    P.drain("v")
    P.wait("v", "w_ld")
    for i in range(3):
        P.bare("v", lambda v, i=i: v.tensor_reduce(
            pp[:, 10 + i:11 + i], wsh[i][:, :], AX.X, ALU.max,
            apply_absolute_value=True))
    P.fence_inc("v", "v")
    P.mark("w_amax", "v")
    P.wait("g", "w_amax")
    P.inc("g", lambda g: g.tensor_reduce(pp[0:1, 13:16], pp[:, 10:13], AX.C, ALU.max), "gc", 1)
    P.wait_now("g", "gc")
    P.inc("g", lambda g: g.dma_start(out=T["cc_am_in"][0:1, 0:3], in_=pp[0:1, 13:16]), "g", 16)
    P.wait_now("g", "g")
    P.inc("g", lambda g: g.collective_compute(
        "AllReduce", ALU.max, replica_groups=[list(range(N_CORES))],
        ins=[T["cc_am_in"].opt()], outs=[T["cc_am_out"].opt()]), "cc", 1)
    P.wait_now("g", "cc")
    P.inc("g", lambda g: g.dma_start(
        out=pp[:, 10:13], in_=T["cc_am_out"][0:1, :].broadcast_to([128, 3])), "g", 16)
    P.mark("w_gam", "g")

    tr_i = [0]

    def tail(i):
        if i == 0:
            P.wait("v", "w_gam")
            for j in range(3):
                P.bare("v", lambda v, j=j: v.tensor_scalar(
                    pp[:, 1 + 2 * j:2 + 2 * j], pp[:, 10 + j:11 + j], 1.0 / GSW_CONST,
                    None, ALU.mult))
                P.bare("v", lambda v, j=j: v.reciprocal(pp[:, 13:14], pp[:, 10 + j:11 + j]))
                P.drain("v")
                P.bare("v", lambda v, j=j: v.tensor_scalar(
                    pp[:, 0 + 2 * j:1 + 2 * j], pp[:, 13:14], 448.0, None, ALU.mult))
                P.drain("v")
            P.bare("v", lambda v: v.tensor_scalar(pp[:, 6:7], consts[:, 1:2], 1.0 / 6.0,
                                                  None, ALU.mult))
            P.bare("v", lambda v: v.reciprocal(pp[:, 7:8], consts[:, 1:2]))
            P.bare("v", lambda v: v.tensor_scalar(pp[:, 8:9], consts[:, 2:3], 1.0 / 6.0,
                                                  None, ALU.mult))
            P.bare("v", lambda v: v.reciprocal(pp[:, 9:10], consts[:, 2:3]))
            P.drain("v")
            for j in range(5):
                P.bare("v", lambda v, j=j: v.tensor_scalar(
                    pp[:, 18 + 2 * j:19 + 2 * j], pp[:, 2 * j:2 * j + 1], 0.5, None, ALU.mult))
                P.bare("v", lambda v, j=j: v.tensor_scalar(
                    pp[:, 19 + 2 * j:20 + 2 * j], pp[:, 2 * j + 1:2 * j + 2], 2.0, None, ALU.mult))
            P.drain("v")
        if i < 3:
            if i > 0:
                P.wait("v", f"w_tr_{tr_i[0] - 1}")
            for jh in range(2):
                emit_quant(P, pp, f"wq{i}_{jh}", wsh[i][:, jh * 2048:(jh + 1) * 2048],
                           wqt[:, jh * 2048:(jh + 1) * 2048], 2048, i,
                           scr1[:, :], scr2[:, :], scr3[:, :],
                           scr4[:, jh * 2048:(jh + 1) * 2048], sml[:, :],
                           s8W[:, :])
            P.fence_inc("v", "v")
            P.mark(f"w_q_{i}", "v")
            P.wait("sy", f"w_q_{i}")
            njt, freew, rowtot = [(4, 1024, 512), (4, 1024, 512), (8, 512, 1024)][i]
            off_dr = i * SHW
            nmid = freew // 128
            for j in range(njt):
                if tr_i[0] > 0:
                    P.wait("sy", f"w_st_{tr_i[0] - 1}")
                srcw = wqt[:, j * freew:(j + 1) * freew]
                P.inc("sy", lambda h, srcw=srcw, nmid=nmid: h.dma_start_transpose(
                    wfm[:, 0:nmid, :], srcw), "wtr", 16)
                P.mark(f"w_tr_{tr_i[0]}", "wtr")
                P.wait("g", f"w_tr_{tr_i[0]}")

                def mk_store(h, nmid=nmid, off_dr=off_dr, j=j, rowtot=rowtot):
                    dst3 = T["cc_wq_in"][off_dr:off_dr + SHW].rearrange(
                        "(m p r) -> p m r", p=128, r=rowtot)
                    return h.dma_start(out=dst3[:, 0:nmid, j * 128:(j + 1) * 128],
                                       in_=wfm[:, 0:nmid, :])
                P.inc("g", mk_store, "wst", 16)
                P.mark(f"w_st_{tr_i[0]}", "wst")
                tr_i[0] += 1
        else:
            P.wait_now("g", "g")
            P.wait_now("g", "wst")
            P.fence_nop("g", "gc")
            P.mark("w_g_done", "gc")
            P.inc("g", lambda g: g.collective_compute(
                "AllGather", ALU.bypass, replica_groups=[list(range(N_CORES))],
                ins=[T["cc_wq_in"].opt()], outs=[T["cc_wq_out"].opt()]), "cc", 1)
            P.mark("w_ag", "cc")

    T["w_tails"] = [lambda i=i: tail(i) for i in range(4)]


def build_attn(nc, P, astk, T):
    consts, maskt, o_my = T["consts"], T["maskt"], T["o_my"]
    sk_sb, z_sb, z_f32 = T["sk_sb"], T["z_sb"], T["z_f32"]
    sk_f32 = T["sk_f32"]
    NB = 2

    def asb(name, shape, dt, **kw):
        return astk.enter_context(nc.sbuf_tensor(name, shape, dt, **kw))

    def aps(name, shape):
        return astk.enter_context(nc.psum_tensor(name, shape, F32))

    x_t = [asb(f"x_t{i}", [128, D], F32) for i in range(NB)]
    st2 = dict(bnsA=asb("a_bnsA", [128, 12], F32), bnsB=asb("a_bnsB", [128, 12], F32),
               mvA=asb("a_mvA", [128, 2], F32), mvB=asb("a_mvB", [128, 2], F32),
               stdA=asb("a_stdA", [128, 1], F32), stdB=asb("a_stdB", [128, 1], F32),
               invA=asb("a_invA", [128, 1], F32), invB=asb("a_invB", [128, 1], F32),
               nmA=asb("a_nmA", [128, 1], F32), nmB=asb("a_nmB", [128, 1], F32))
    ln_tm = [asb(f"ln_tm{i}", [128, D], BF16) for i in range(NB)]
    ln_fm = [asb(f"ln_fm{i}", [128, 8, 128], BF16, is_transpose_dest=True) for i in range(NB)]
    NBQ = 3
    q_sp = [asb(f"q_sp{i}", [128, 1024], BF16) for i in range(NBQ)]
    k_sp = [asb(f"k_sp{i}", [128, 1024], BF16) for i in range(NBQ)]
    q_fm = [asb(f"q_fm{i}", [128, 8, 128], BF16, is_transpose_dest=True) for i in range(NBQ)]
    k_fm = [asb(f"k_fm{i}", [128, 8, 128], BF16, is_transpose_dest=True) for i in range(NBQ)]
    k_tm = [asb(f"k_tm{i}", [128, 512], BF16) for i in range(NBQ)]
    v_tm = [asb(f"v_tm{i}", [128, 512], BF16) for i in range(NBQ)]
    a_sb = [asb(f"a_sb{i}", [128, 1024], BF16) for i in range(NB)]
    w_all = [asb(f"w_all{i}", [64, 1024], BF16) for i in range(NB)]
    w_tm = [asb(f"w_tm{i}", [128, 8, 64], BF16, is_transpose_dest=True) for i in range(NB)]
    b_sb = [asb(f"b_sb{i}", [128, 1024], BF16) for i in range(NB)]
    NBO = 6
    o_st = [asb(f"o_st{i}", [128, 512], BF16) for i in range(NBO)]
    wq_sb, wk_sb, wv_sb = T["wq_sb"], T["wk_sb"], T["wv_sb"]
    dbg_sb = asb("dbg_sb", [128, 7680], F32) if DEBUG else None

    psQ = aps("psQ", [128, 512])
    psK = aps("psK", [128, 512])
    psVO = aps("psVO", [128, 512])
    psAB = aps("psAB", [128, 1024])
    psW = aps("psW", [64, 1024])
    psSK = aps("psSK", [64, 512])

    sel1 = consts[:, 3:4]
    sel2 = consts[:, 4:5]
    selx1 = consts[:, 5:6]
    selx2 = consts[:, 6:7]

    if DEBUG:
        P.bare("v", lambda v: v.memset(dbg_sb[:, :], 0.0))
    P.bare("v", lambda v: v.memset(z_sb[:, :], 0.0))
    P.bare("v", lambda v: v.memset(sk_sb[0][:, :], 0.0))
    P.bare("v", lambda v: v.memset(sk_sb[1][:, :], 0.0))
    P.bare("v", lambda v: v.memset(sk_f32[:, :], 0.0))
    P.bare("v", lambda v: v.memset(z_f32[:, :], 0.0))
    for i in range(NBQ):
        P.bare("v", lambda v, i=i: v.memset(q_sp[i][:, :], 0.0))
        P.bare("v", lambda v, i=i: v.memset(k_sp[i][:, :], 0.0))
    P.drain("v")
    P.fence_inc("v", "v")
    P.mark("a_zsb_-1", "v")
    P.mark("a_sk0_0", "v")
    P.mark("a_state0", "v")

    def ok(ci):
        return 0 <= ci < NCH

    def st_front2(c0):
        pair = [c for c in (c0, c0 + 1) if ok(c)]
        if not pair:
            return
        for ci in pair:
            bi = ci % NB
            if ci >= NB:
                P.wait("sy", f"a_ln_{ci - NB}")
            P.inc("sy", lambda h, ci=ci, bi=bi: h.dma_start(
                out=x_t[bi][:, :], in_=T["x_full"][ci * 128:(ci + 1) * 128, :]), f"xld{bi}", 16)
            P.mark(f"a_xld_{ci}", f"xld{bi}")
        for ci in pair:
            P.wait("v", f"a_xld_{ci}")
        prev = f"aln{c0 - 1}_ln" if c0 > 0 else None
        if len(pair) == 2:
            emit_ln2(P, f"aln{c0}", x_t[c0 % NB][:, :], ln_tm[c0 % NB][:, :],
                     f"aln{c0 + 1}", x_t[(c0 + 1) % NB][:, :], ln_tm[(c0 + 1) % NB][:, :],
                     st2, consts[:, 0:1], prev)
        else:
            ci = pair[0]
            emit_ln(P, f"aln{ci}", x_t[ci % NB][:, :], ln_tm[ci % NB][:, :],
                    st2["bnsA"][:, :], st2["mvA"][:, :], st2["stdA"][:, :],
                    st2["invA"][:, :], st2["nmA"][:, :], consts[:, 0:1], prev)
        for ci in pair:
            P.mark(f"a_ln_{ci}", "v")
        for ci in pair:
            bi = ci % NB
            P.wait("sy", f"a_ln_{ci}")
            if ci >= NB:
                P.wait("sy", f"a_proj_{ci - NB}")
            P.inc("sy", lambda h, bi=bi: h.dma_start_transpose(
                ln_fm[bi][:, :, :], ln_tm[bi][:, :]), f"lnT{bi}", 16)
            P.mark(f"a_lnT_{ci}", f"lnT{bi}")

    def st_B1(ci):
        if not ok(ci):
            return
        bi = ci % NB
        P.wait("t", f"a_lnT_{ci}")
        if ci == 0:
            P.wait("t", "a_wld")
        if ci > 0:
            P.wait("t", f"a_qkvev_{ci - 1}")
        if ci >= 3:
            P.wait("t", f"a_oev_{ci - 3}")
        for m in range(8):
            P.bare("t", lambda t, bi=bi, m=m: t.matmul(
                psQ[:, :], ln_fm[bi][:, m, :], wq_sb[:, m, :], start=(m == 0), stop=(m == 7)))
        for m in range(8):
            P.bare("t", lambda t, bi=bi, m=m: t.matmul(
                psK[:, :], ln_fm[bi][:, m, :], wk_sb[:, m, :], start=(m == 0), stop=(m == 7)))
        for m in range(8):
            P.bare("t", lambda t, bi=bi, m=m: t.matmul(
                psVO[:, :], ln_fm[bi][:, m, :], wv_sb[:, m, :], start=(m == 0), stop=(m == 7)))
        P.fence_inc("t", "t")
        P.mark(f"a_proj_{ci}", "t")

    def st_qkvev(ci):
        if not ok(ci):
            return
        bq = ci % NBQ
        P.wait("v", f"a_proj_{ci}")
        if ci >= NBQ:
            P.wait("v", f"a_qkT_{ci - NBQ}")
            P.wait("v", f"a_skz_{ci - NBQ}")
        P.bare("v", lambda v, bq=bq: v.tensor_copy(
            q_sp[bq][:, :].rearrange("p (h r) -> p h r", r=128)[:, :, 0:64],
            psQ[:, :].rearrange("p (h r) -> p h r", r=64)))
        P.bare("v", lambda v, bq=bq: v.tensor_copy(
            k_sp[bq][:, :].rearrange("p (h r) -> p h r", r=128)[:, :, 0:64],
            psK[:, :].rearrange("p (h r) -> p h r", r=64)))
        P.bare("v", lambda v, bq=bq: v.tensor_copy(k_tm[bq][:, :], psK[:, :]))
        P.bare("v", lambda v, bq=bq: v.tensor_copy(v_tm[bq][:, :], psVO[:, :]))
        P.fence_inc("v", "v")
        P.mark(f"a_qkvev_{ci}", "v")

    def st_qkT(ci):
        if not ok(ci):
            return
        bq = ci % NBQ
        sem = f"qkT{bq}" if bq < 2 else "qkT2"
        P.wait("sy", f"a_qkvev_{ci}")
        if ci >= NBQ:
            P.wait("sy", f"a_skz_{ci - NBQ}")
        P.inc("sy", lambda h, bq=bq: h.dma_start_transpose(q_fm[bq][:, :, :], q_sp[bq][:, :]),
              sem, 16)
        P.inc("sy", lambda h, bq=bq: h.dma_start_transpose(k_fm[bq][:, :, :], k_sp[bq][:, :]),
              sem, 16)
        P.mark(f"a_qkT_{ci}", sem)

    def st_B2(ci):
        if not ok(ci):
            return
        bq = ci % NBQ
        P.wait("t", f"a_qkT_{ci}")
        if ci > 0:
            P.wait("t", f"a_bmask_{ci - 1}")
            P.wait("t", f"a_skadd_{ci - 1}")
        for h_ in range(8):
            P.bare("t", lambda t, bq=bq, h_=h_: t.matmul(
                psAB[:, h_ * 128:(h_ + 1) * 128], k_fm[bq][0:64, h_, :], q_fm[bq][0:64, h_, :],
                start=True, stop=True))
        for h_ in range(8):
            P.bare("t", lambda t, bq=bq, h_=h_: t.matmul(
                psSK[0:64, h_ * 64:(h_ + 1) * 64], k_tm[bq][:, h_ * 64:(h_ + 1) * 64],
                k_tm[bq][:, h_ * 64:(h_ + 1) * 64], start=True, stop=True))
        P.fence_inc("t", "t")
        P.mark(f"a_A_{ci}", "t")

    def st_amask_skadd(ci):
        if not ok(ci):
            return
        bi = ci % NB
        P.wait("v", f"a_A_{ci}")
        if ci >= NB:
            P.wait("v", f"a_W_{ci - NB}")
        P.bare("v", lambda v, bi=bi: v.tensor_tensor(a_sb[bi][:, 0:512], psAB[:, 0:512],
                                                     maskt[:, :], ALU.mult))
        P.bare("v", lambda v, bi=bi: v.tensor_tensor(a_sb[bi][:, 512:1024], psAB[:, 512:1024],
                                                     maskt[:, :], ALU.mult))
        P.fence_inc("v", "v")
        P.mark(f"a_amask_{ci}", "v")
        P.drain("v")
        P.bare("v", lambda v: v.tensor_tensor(sk_f32[:, :], sk_f32[:, :], psSK[:, :], ALU.add))
        P.drain("v")
        P.fence_inc("v", "v")
        P.mark(f"a_skadd_{ci}", "v")
        if ci + 1 < NCH:
            P.wait("v", f"a_W_{ci - 1}") if ci >= 1 else None
            P.bare("v", lambda v, ci=ci: v.tensor_copy(sk_sb[(ci + 1) % 2][:, :], sk_f32[:, :]))
            P.drain("v")
            P.fence_inc("v", "v")
            P.mark(f"a_sk0_{ci + 1}", "v")

    def st_B3(ci):
        if not ok(ci):
            return
        bi = ci % NB
        P.wait("t", f"a_amask_{ci}")
        P.wait("t", f"a_sk0_{ci}")
        if ci == 0:
            P.wait("t", "a_state0")
        if ci > 0:
            P.wait("t", f"a_zsb_{ci - 1}")
        bq = ci % NBQ
        for h_ in range(8):
            P.bare("t", lambda t, bi=bi, bq=bq, h_=h_: t.matmul(
                psW[0:64, h_ * 128:(h_ + 1) * 128], k_tm[bq][:, h_ * 64:(h_ + 1) * 64],
                a_sb[bi][:, h_ * 128:(h_ + 1) * 128], start=True, stop=False))
            P.bare("t", lambda t, bq=bq, h_=h_, ci=ci: t.matmul(
                psW[0:64, h_ * 128:(h_ + 1) * 128], sk_sb[ci % 2][0:64, h_ * 64:(h_ + 1) * 64],
                q_fm[bq][0:64, h_, :], start=False, stop=True))
        P.fence_inc("t", "t")
        P.mark(f"a_W_{ci}", "t")

    def st_wev(ci):
        if not ok(ci):
            return
        bi = ci % NB
        P.wait("v", f"a_W_{ci}")
        if ci >= NB:
            P.wait("v", f"a_wT_{ci - NB}")
            P.wait("v", f"a_B_{ci - NB}")
        P.bare("v", lambda v, bi=bi: v.tensor_copy(w_all[bi][:, 0:512], psW[:, 0:512]))
        P.bare("v", lambda v, bi=bi: v.tensor_copy(w_all[bi][:, 512:1024], psW[:, 512:1024]))
        if DEBUG and ci == DBG_CI:
            P.drain("v")
            P.bare("v", lambda v, bi=bi: v.tensor_copy(dbg_sb[0:64, 3584:4608], w_all[bi][:, :]))
        P.fence_inc("v", "v")
        P.mark(f"a_wev_{ci}", "v")

    def st_wT(ci):
        if not ok(ci):
            return
        bi = ci % NB
        P.wait("sy", f"a_wev_{ci}")
        if ci >= NB:
            P.wait("sy", f"a_skz_{ci - NB}")
        P.inc("sy", lambda h, bi=bi: h.dma_start_transpose(w_tm[bi][:, :, :], w_all[bi][:, :]),
              f"wT{bi}", 16)
        P.mark(f"a_wT_{ci}", f"wT{bi}")

    def st_B4(ci):
        if not ok(ci):
            return
        bi = ci % NB
        P.wait("t", f"a_wev_{ci}")
        bq = ci % NBQ
        for h_ in range(8):
            P.bare("t", lambda t, bi=bi, bq=bq, h_=h_: t.matmul(
                psAB[:, h_ * 128:(h_ + 1) * 128], w_all[bi][0:64, h_ * 128:(h_ + 1) * 128],
                q_fm[bq][0:64, h_, :], start=True, stop=True))
        P.fence_inc("t", "t")
        P.mark(f"a_B_{ci}", "t")

    def st_bmask(ci):
        if not ok(ci):
            return
        bi = ci % NB
        P.wait("v", f"a_B_{ci}")
        if ci >= NB:
            P.wait("v", f"a_O_{ci - NB}")
        P.bare("v", lambda v, bi=bi: v.tensor_tensor(b_sb[bi][:, 0:512], psAB[:, 0:512],
                                                     maskt[:, :], ALU.mult))
        P.bare("v", lambda v, bi=bi: v.tensor_tensor(b_sb[bi][:, 512:1024], psAB[:, 512:1024],
                                                     maskt[:, :], ALU.mult))
        P.fence_inc("v", "v")
        P.mark(f"a_bmask_{ci}", "v")

    def st_B5(ci):
        if not ok(ci):
            return
        bi = ci % NB
        P.wait("t", f"a_bmask_{ci}")
        P.wait("t", f"a_zsb_{ci - 1}")
        P.wait("t", f"a_wT_{ci}")
        if ci + 2 < NCH:
            P.wait("t", f"a_qkvev_{ci + 2}")
        bq = ci % NBQ
        for h_ in range(8):
            P.bare("t", lambda t, bi=bi, bq=bq, h_=h_: t.matmul(
                psVO[:, h_ * 64:(h_ + 1) * 64], b_sb[bi][:, h_ * 128:(h_ + 1) * 128],
                v_tm[bq][:, h_ * 64:(h_ + 1) * 64], start=True, stop=False))
            P.bare("t", lambda t, bq=bq, h_=h_: t.matmul(
                psVO[:, h_ * 64:(h_ + 1) * 64], q_fm[bq][0:64, h_, :],
                z_sb[0:64, h_ * 64:(h_ + 1) * 64], start=False, stop=True))
        for h_ in range(8):
            P.bare("t", lambda t, bi=bi, bq=bq, h_=h_: t.matmul(
                psW[0:64, h_ * 64:(h_ + 1) * 64], w_tm[bi][:, h_, :],
                v_tm[bq][:, h_ * 64:(h_ + 1) * 64], start=True, stop=True))
        P.fence_inc("t", "t")
        P.mark(f"a_O_{ci}", "t")
        P.mark(f"a_skz_{ci}", "t")

    def st_oev_z(ci):
        if not ok(ci):
            return
        bi = ci % NBO
        sp = ci % 2
        sl = ci % NSL
        first_half = ci < NSL
        P.wait("v", f"a_O_{ci}")
        if ci >= NBO:
            P.wait("v", f"a_ost_{ci - NBO}")
        if first_half:
            P.bare("v", lambda v, sl=sl: v.tensor_scalar(
                o_my[:, sl * 512:(sl + 1) * 512], psVO[:, :], sel1, None, ALU.mult))
            P.bare("v", lambda v, bi=bi: v.tensor_scalar(
                o_st[bi][:, :], psVO[:, :], selx1, None, ALU.mult))
        else:
            P.bare("v", lambda v, sl=sl: v.scalar_tensor_tensor(
                o_my[:, sl * 512:(sl + 1) * 512], psVO[:, :], sel2,
                o_my[:, sl * 512:(sl + 1) * 512], ALU.mult, ALU.add))
            P.bare("v", lambda v, bi=bi: v.tensor_scalar(
                o_st[bi][:, :], psVO[:, :], selx2, None, ALU.mult))
        if DEBUG and ci == DBG_CI:
            P.drain("v")
            P.bare("v", lambda v: v.tensor_copy(dbg_sb[:, 4608:5120], psVO[:, :]))
            if DBG_CI > 0:
                P.bare("v", lambda v, ci=ci: v.tensor_copy(dbg_sb[0:64, 5120:5632], sk_sb[ci % 2][:, :]))
                P.bare("v", lambda v: v.tensor_copy(dbg_sb[0:64, 6144:6656], z_sb[:, :]))
                P.bare("v", lambda v: v.tensor_copy(dbg_sb[0:64, 6656:7168], z_f32[:, :]))
        P.fence_inc("v", "v")
        P.mark(f"a_oev_{ci}", "v")
        P.wait("g", f"a_oev_{ci}")
        if first_half:
            P.inc("g", lambda g, bi=bi, sl=sl: g.dma_start(
                out=T["cc_o_in"][sl * 128:(sl + 1) * 128, :], in_=o_st[bi][:, :]),
                f"ost{sp}", 16)
        else:
            P.wait("g", f"a_ost_{ci - NSL}")
            P.inc("g", lambda g, bi=bi, sl=sl: g.dma_start(
                out=T["cc_o_in"][sl * 128:(sl + 1) * 128, :], in_=o_st[bi][:, :],
                accum_op=ALU.add), f"ost{sp}", 16)
        P.mark(f"a_ost_{ci}", f"ost{sp}")
        P.wait("v", f"a_skz_{ci}")
        if ci == 0:
            P.bare("v", lambda v: v.tensor_copy(z_f32[:, :], psW[0:64, 0:512]))
        else:
            P.bare("v", lambda v: v.tensor_tensor(z_f32[:, :], z_f32[:, :],
                                                  psW[0:64, 0:512], ALU.add))
        P.drain("v")
        P.bare("v", lambda v: v.tensor_copy(z_sb[:, :], z_f32[:, :]))
        P.fence_inc("v", "v")
        P.mark(f"a_zsb_{ci}", "v")

    st_front2(0)
    st_B1(0)
    st_qkvev(0)
    st_B1(1)
    st_qkvev(1)
    st_qkT(0)
    st_B2(0)
    st_amask_skadd(0)
    st_qkT(1)
    w_ip = [min(10 + 4 * j, NCH - 1) for j in range(4)]
    w_tp = [0]
    for i in range(NCH):
        if (i + 2) % 2 == 0:
            st_front2(i + 2)
        st_B3(i)
        st_wev(i)
        st_B1(i + 2)
        st_qkvev(i + 2)
        st_B4(i)
        st_bmask(i)
        st_B2(i + 1)
        st_amask_skadd(i + 1)
        st_wT(i)
        st_B5(i)
        st_oev_z(i)
        while w_tp[0] < 4 and i >= w_ip[w_tp[0]]:
            T["w_tails"][w_tp[0]]()
            w_tp[0] += 1
        for k in range(NQ - 1):
            if i == NSL + (NSL // NQ) * (k + 1) - 1:
                P.wait("g", f"a_ost_{i - 1}")
                P.wait("g", f"a_ost_{i}")
                qr = T_HALF // NQ
                P.inc("g", lambda g, k=k, qr=qr: g.collective_compute(
                    "AllGather", ALU.bypass, replica_groups=[list(range(N_CORES))],
                    ins=[T["cc_o_in"][k * qr:(k + 1) * qr, :].opt()],
                    outs=[T["cc_o_out_q"][k].opt()]), "cc", 1)
                P.mark(f"o_ag{k}", "cc")
        st_qkT(i + 2)

    if DEBUG:
        P.wait_now("g", "ost0")
        P.wait_now("g", "ost1")
        P.inc("g", lambda g: g.dma_start(out=T["dbg_d"][:, :], in_=dbg_sb[:, :]), "g", 16)
        P.wait_now("g", "g")
    P.fence_nop("t", "t")
    P.mark("a_end_t", "t")
    P.fence_nop("v", "v")
    P.mark("a_end_v", "v")
    P.fence_nop("s", "s")
    P.mark("a_end_s", "s")

    P.wait_now("g", "ost0")
    P.wait_now("g", "ost1")
    qr = T_HALF // NQ
    k = NQ - 1
    P.inc("g", lambda g, k=k: g.collective_compute(
        "AllGather", ALU.bypass, replica_groups=[list(range(N_CORES))],
        ins=[T["cc_o_in"][k * qr:(k + 1) * qr, :].opt()],
        outs=[T["cc_o_out_q"][k].opt()]), "cc", 1)
    P.mark(f"o_ag{k}", "cc")


def build_b(nc, P, bstk, T):
    consts, o_my, xq_res = T["consts"], T["o_my"], T["xq_res"]
    pp = T["pp"]
    NB = 2

    def bsb(name, shape, dt, **kw):
        return bstk.enter_context(nc.sbuf_tensor(name, shape, dt, **kw))

    wo_sb = bsb("wo_sb", [128, 8, 1024], BF16)
    o_pt = [bsb(f"o_pt{i}", [128, 512], BF16) for i in range(NB)]
    om_fm = [bsb(f"om_fm{i}", [128, 4, 128], BF16, is_transpose_dest=True) for i in range(NB)]
    op_fm = [bsb(f"op_fm{i}", [128, 4, 128], BF16, is_transpose_dest=True) for i in range(NB)]
    r1t = [bsb(f"r1t{i}", [128, D], F32) for i in range(NB)]
    ln2a = [bsb(f"ln2a{i}", [128, BQ * 1024], BF16) for i in range(2)]
    xq_tm = bsb("xq_tm", [128, BQ * 1024], BF16)
    BH = BQ * 512
    if NSL * 512 >= 2 * BH * 4:
        tA = o_my[:, 0:2 * BH]
        tA_in_omy = True
    else:
        tA = bsb("tA_b", [128, 2 * BH], BF16)[:, :]
        tA_in_omy = False
    tC = bsb("tC_b", [128, 2 * BH], BF16)
    tD = bsb("tD_b", [128, 2 * BH], BF16)
    tE = bsb("tE_b", [128, 2 * BH], BF16)
    smlB = [bsb(f"smlB{i}", [128, 8 * BQ * 64], F32) for i in range(2)]
    s8B = [bsb(f"s8B{i}", [128, 2 * BQ * 64], F8) for i in range(2)]
    bnsB = bsb("bnsB", [128, 12], F32)
    mvB = bsb("mvB", [128, 2], F32)
    stdB = bsb("stdB", [128, 1], F32)
    invB = bsb("invB", [128, 1], F32)
    negmB = bsb("negmB", [128, 1], F32)
    psWo = [bstk.enter_context(nc.psum_tensor(f"psWo{i}", [128, 1024], F32)) for i in range(2)]

    for eng in ("sy", "g", "s"):
        P.wait(eng, "a_end_t")
        P.wait(eng, "a_end_v")
        P.wait(eng, "a_end_s")
        for sem in ("xld0", "xld1", "lnT0", "lnT1", "qkT0", "qkT1", "qkT2", "wT0", "wT1",
                    "ost0", "ost1", "g"):
            P.wait_now(eng, sem)
    P.inc("g", lambda g: g.dma_start(
        out=wo_sb[:, :, :], in_=T["woi"].rearrange("(m p) c -> p m c", p=128)), "g", 16)
    P.mark("b_wold", "g")

    pendB = [None]

    def flushb():
        if pendB[0] is None:
            return
        stB, tt0 = pendB[0]
        pendB[0] = None
        if tA_in_omy:
            nsl_used = (2 * BH) // 512
            gt = min(nsl_used - 1, tt0)
            P.wait("v", f"b_oT_{gt}")
            if gt >= 1:
                P.wait("v", f"b_oT_{gt - 1}")
        if tt0 >= BQ:
            P.wait("v", f"b_xqT_{tt0 - BQ}")
        emit_quant_multi_b(P, stB)
        P.fence_inc("v", "v")
        P.mark(f"b_q_{tt0}", "v")
        P.wait("sy", f"b_q_{tt0}")
        for i4 in range(BQ):
            tq = tt0 - (BQ - 1) + i4
            P.inc("sy", lambda h, i4=i4, tq=tq: h.dma_start_transpose(
                xq_res[:, tq * 1024:(tq + 1) * 1024].rearrange(
                    "p (m r) -> p m r", r=128),
                xq_tm[:, i4 * 1024:(i4 + 1) * 1024]), "xqT", 16)
        P.mark(f"b_xqT_{tt0}", "xqT")

    for pt in range(NTT // 2):
        tpair = (2 * pt, 2 * pt + 1)
        for tt in tpair:
            bi = tt % NB
            if tt >= NB:
                P.wait("sy", f"b_r1sp_{tt - NB}")
                P.wait("sy", f"bln{tt - NB}_ln")
            P.inc("sy", lambda h, tt=tt, bi=bi: h.dma_start(
                out=r1t[bi][:, :], in_=T["x_half"][tt * 128:(tt + 1) * 128, :]), f"xld{bi}", 16)
            P.mark(f"b_xld_{tt}", f"xld{bi}")

        for tt in tpair:
            bi = tt % NB
            qk = tt * NQ // NTT
            P.wait("g", f"o_ag{qk}")
            if tt >= NB:
                P.wait("g", f"b_oT_{tt - NB}")
            P.inc("g", lambda g, tt=tt, bi=bi, qk=qk: g.indirect_dma_start(
                o_pt[bi][:, :], None, T["cc_o_out_q"][qk][:, :],
                IndirectOffsetOnAxis(T["gidx"][:, tt:tt + 1], 0)), f"og{bi}", 16)
            P.mark(f"b_og_{tt}", f"og{bi}")

        for tt in tpair:
            bi = tt % NB
            P.wait("sy", f"b_og_{tt}")
            if tt >= NB:
                P.wait("sy", f"b_wo_{tt - NB}")
            P.inc("sy", lambda h, tt=tt, bi=bi: h.dma_start_transpose(
                om_fm[bi][:, :, :], o_my[:, tt * 512:(tt + 1) * 512]), f"oT{bi}", 16)
            P.inc("sy", lambda h, bi=bi: h.dma_start_transpose(op_fm[bi][:, :, :], o_pt[bi][:, :]),
                  f"oT{bi}", 16)
            P.mark(f"b_oT_{tt}", f"oT{bi}")

        for tt in tpair:
            bi = tt % NB
            P.wait("t", f"b_oT_{tt}")
            if tt == 0:
                P.wait("t", "b_wold")
            if tt >= 2:
                P.wait("t", f"b_r1_{tt - 2}")
            for dh in range(2):
                for j in range(4):
                    P.bare("t", lambda t, bi=bi, j=j, dh=dh: t.matmul(
                        psWo[bi][:, dh * 512:(dh + 1) * 512], om_fm[bi][:, j, :],
                        wo_sb[:, j, dh * 512:(dh + 1) * 512], start=(j == 0), stop=False))
                for j in range(4):
                    P.bare("t", lambda t, bi=bi, j=j, dh=dh: t.matmul(
                        psWo[bi][:, dh * 512:(dh + 1) * 512], op_fm[bi][:, j, :],
                        wo_sb[:, 4 + j, dh * 512:(dh + 1) * 512], start=False, stop=(j == 3)))
            P.fence_inc("t", "t")
            P.mark(f"b_wo_{tt}", "t")

        for tt in tpair:
            bi = tt % NB
            P.wait("v", f"b_wo_{tt}")
            P.wait("v", f"b_xld_{tt}")
            P.bare("v", lambda v, bi=bi: v.tensor_tensor(r1t[bi][:, 0:512], psWo[bi][:, 0:512],
                                                         r1t[bi][:, 0:512], ALU.add))
            P.bare("v", lambda v, bi=bi: v.tensor_tensor(r1t[bi][:, 512:1024], psWo[bi][:, 512:1024],
                                                         r1t[bi][:, 512:1024], ALU.add))
            P.fence_inc("v", "v")
            P.mark(f"b_r1_{tt}", "v")

        for tt in tpair:
            bi = tt % NB
            P.wait("g", f"b_r1_{tt}")
            P.inc("g", lambda g, tt=tt, bi=bi: g.dma_start(
                out=T["r1_sp"][tt * 128:(tt + 1) * 128, :], in_=r1t[bi][:, :]), f"r1s{bi}", 16)
            P.mark(f"b_r1sp_{tt}", f"r1s{bi}")

        for tt in tpair:
            bi = tt % NB
            P.wait("v", f"b_r1_{tt}")
            P.drain("v")
            sl4 = tt % BQ
            emit_ln(P, f"bln{tt}", r1t[bi][:, :],
                    ln2a[(tt // BQ) % 2][:, sl4 * 1024:(sl4 + 1) * 1024],
                    bnsB[:, :], mvB[:, :], stdB[:, :], invB[:, :], negmB[:, :], consts[:, 0:1],
                    f"bln{tt - 1}_ln" if tt > 0 else None)
            P.mark(f"b_ln_{tt}", "v")

            if sl4 == BQ - 1:
                kq = tt // BQ
                P.wait("v", f"b_ln_{tt}")
                stB = emit_quant_multi_a(P, [qstream(
                    pp, f"bq{tt}_{si}", ln2a[kq % 2][:, si * BH:(si + 1) * BH],
                    xq_tm[:, si * BH:(si + 1) * BH], BH, 3,
                    tA[:, si * BH:(si + 1) * BH], tC[:, si * BH:(si + 1) * BH],
                    tD[:, si * BH:(si + 1) * BH], tE[:, si * BH:(si + 1) * BH],
                    smlB[kq % 2][:, si * 8 * BQ * 32:(si + 1) * 8 * BQ * 32],
                    s8B[kq % 2][:, si * 2 * BQ * 32:(si + 1) * 2 * BQ * 32])
                    for si in range(2)])
                flushb()
                pendB[0] = (stB, tt)
    flushb()
    P.mark("b_done", "xqT")
    if DEBUG:
        P.wait("g", "b_done")
        P.inc("g", lambda g: g.dma_start(out=T["dbg2_d"][:, :], in_=xq_res[:, :]), "g", 16)
        P.wait_now("g", "g")
    P.fence_nop("t", "t")
    P.mark("b_end_t", "t")
    P.fence_nop("v", "v")
    P.mark("b_end_v", "v")
    P.fence_nop("s", "s")
    P.mark("b_end_s", "s")


def build_fa(nc, P, fstk, T):
    pp, xq_res = T["pp"], T["xq_res"]

    def fsb(name, shape, dt, **kw):
        return fstk.enter_context(nc.sbuf_tensor(name, shape, dt, **kw))

    wqg = [fsb(f"wqg{i}", [128, 8, 512], BF16) for i in range(2)]
    wqu = [fsb(f"wqu{i}", [128, 8, 512], BF16) for i in range(2)]
    h_b = [fsb(f"h_b{i}", [128, FTB * 512], BF16) for i in range(4)]
    hq_b = [fsb(f"hq_b{i}", [128, FTB * 512], BF16) for i in range(4)]
    sg_t = [fsb(f"sg_t{i}", [128, 512], BF16) for i in range(2)]
    up_t = [fsb(f"up_t{i}", [128, 512], BF16) for i in range(2)]
    tA = [fsb(f"tA_f{i}", [128, FTB * 512], BF16) for i in range(2)]
    if NSL * 512 >= 6 * FTB * 512:
        o_my = T["o_my"]
        tD = [o_my[:, i * FTB * 512:(i + 1) * FTB * 512] for i in range(2)]
        tE = [o_my[:, (2 + i) * FTB * 512:(3 + i) * FTB * 512] for i in range(2)]
        tC = [o_my[:, (4 + i) * FTB * 512:(5 + i) * FTB * 512] for i in range(2)]
    else:
        tD = [fsb(f"tD_f{i}", [128, FTB * 512], BF16)[:, :] for i in range(2)]
        tE = [fsb(f"tE_f{i}", [128, FTB * 512], BF16)[:, :] for i in range(2)]
        tC = [fsb(f"tC_fx{i}", [128, FTB * 512], BF16)[:, :] for i in range(2)]
    smlF = [fsb(f"smlF{i}", [128, 8 * FTB * 32], F32) for i in range(4)]
    s8F = [fsb(f"s8F{i}", [128, 2 * FTB * 32], F8) for i in range(4)]
    hq_fm = [fsb(f"hq_fm{i}", [128, 4, 128], BF16, is_transpose_dest=True) for i in range(2)]
    psG = [fstk.enter_context(nc.psum_tensor(f"psG{i}", [128, 512], F32)) for i in range(2)]
    psU = [fstk.enter_context(nc.psum_tensor(f"psU{i}", [128, 512], F32)) for i in range(2)]

    P.wait("sy", "w_ag")
    P.wait("sy", "b_done")
    P.wait("t", "b_done")
    for eng in ("sy", "g", "s"):
        P.wait(eng, "b_end_t")
        P.wait(eng, "b_end_v")
        P.wait(eng, "b_end_s")
        for sem in ("xld0", "xld1", "og0", "og1", "oT0", "oT1", "r1s0", "r1s1",
                    "xqT", "g"):
            P.wait_now(eng, sem)
    tr_g = [0]
    pend = [None]

    def flushq():
        if pend[0] is None:
            return
        stq, ic0, pb0 = pend[0]
        pend[0] = None
        for tb0 in (pb0 * 2, pb0 * 2 + 1):
            gb0 = ic0 * NTB + tb0
            if gb0 >= 4:
                gq = gb0 - 4
                P.wait("v", f"f_hqT_{gq // NTB}_{gq % NTB}")
        emit_quant_multi_b(P, stq)
        P.fence_inc("v", "v")
        P.mark(f"f_q_{ic0}_{pb0}", "v")
        P.wait("sy", f"f_q_{ic0}_{pb0}")
        for tb0 in (pb0 * 2, pb0 * 2 + 1):
            qb = (ic0 * NTB + tb0) % 4
            for ti in range(FTB):
                tt = tb0 * FTB + ti
                fp = tr_g[0] % 2
                if tr_g[0] >= 2:
                    P.wait("sy", f"f_hqst_g{tr_g[0] - 2}")
                P.inc("sy", lambda h, qb=qb, ti=ti, fp=fp: h.dma_start_transpose(
                    hq_fm[fp][:, :, :], hq_b[qb][:, ti * 512:(ti + 1) * 512]), f"hqT{fp}", 16)
                P.mark(f"f_hqTr_g{tr_g[0]}", f"hqT{fp}")
                P.wait("g", f"f_hqTr_g{tr_g[0]}")
                def mk_st(g, tt=tt, ic0=ic0, fp=fp):
                    dst = T["hq_sp"][tt, :, ic0 * 4:(ic0 + 1) * 4, :]
                    return g.dma_start(out=dst, in_=hq_fm[fp][:, :, :])
                P.inc("g", mk_st, f"hqs{fp}", 16)
                P.mark(f"f_hqst_g{tr_g[0]}", f"hqs{fp}")
                tr_g[0] += 1
            P.mark(f"f_hqT_{ic0}_{tb0}", f"hqs{(tr_g[0] - 1) % 2}")

    def stream_w(ic):
        if ic >= NIC:
            return
        wb = ic % 2
        if ic >= 2:
            P.wait("sy", f"f_gu_{ic - 2}")
        for wsb_t, woff in ((wqg[wb], 0), (wqu[wb], SHW)):
            def mk_ld(h, wsb_t=wsb_t, woff=woff, ic=ic):
                src = T["cc_wq_out"][ic * 3 * SHW + woff: ic * 3 * SHW + woff + SHW].rearrange(
                    "(m p r) -> p m r", p=128, r=512)
                return h.dma_start(out=wsb_t[:, :, :], in_=src)
            P.inc("sy", mk_ld, f"fw{ic % 2}", 16)
        P.mark(f"f_wld_{ic}", f"fw{ic % 2}")

    stream_w(0)
    for ic in range(NIC):
        wb = ic % 2
        stream_w(ic + 1)

        for pb in range(NTB // 2):
            streams = []
            gp2 = (ic * (NTB // 2) + pb) % 2
            for si in range(2):
                tb = pb * 2 + si
                gb = ic * NTB + tb
                hb = gb % 4
                for ti in range(FTB):
                    tt = tb * FTB + ti
                    tp = tt % 2
                    P.wait("t", f"f_wld_{ic}")
                    if not (ic == 0 and tt < 2):
                        prev = (ic, tt - 2) if tt >= 2 else (ic - 1, NTT - 2 + tt)
                        P.wait("t", f"f_h_{prev[0]}_{prev[1]}")
                    for m in range(8):
                        P.bare("t", lambda t, tt=tt, m=m, wb=wb, tp=tp: t.matmul(
                            psG[tp][:, :], xq_res[:, tt * 1024 + m * 128: tt * 1024 + (m + 1) * 128],
                            wqg[wb][:, m, :], start=(m == 0), stop=(m == 7)))
                    for m in range(8):
                        P.bare("t", lambda t, tt=tt, m=m, wb=wb, tp=tp: t.matmul(
                            psU[tp][:, :], xq_res[:, tt * 1024 + m * 128: tt * 1024 + (m + 1) * 128],
                            wqu[wb][:, m, :], start=(m == 0), stop=(m == 7)))
                    P.fence_inc("t", "t")
                    P.mark(f"f_gu_t_{ic}_{tt}", "t")
                    if tt == NTT - 1:
                        P.mark(f"f_gu_{ic}", "t")
                    P.wait("s", f"f_gu_t_{ic}_{tt}")
                    if not (ic == 0 and tt < 2):
                        prev = (ic, tt - 2) if tt >= 2 else (ic - 1, NTT - 2 + tt)
                        P.wait("s", f"f_h_{prev[0]}_{prev[1]}")
                    P.bare("s", lambda s, tp=tp: s.activation(sg_t[tp][:, :], psG[tp][:, :], AF.Silu))
                    P.drain("s")
                    P.inc("s", lambda s, tp=tp: s.copy(up_t[tp][:, :], psU[tp][:, :]), "s", 1)
                    P.mark(f"f_sg_{ic}_{tt}", "s")
                    P.wait("v", f"f_sg_{ic}_{tt}")
                    P.bare("v", lambda v, tp=tp, hb=hb, ti=ti: v.tensor_tensor(
                        h_b[hb][:, ti * 512:(ti + 1) * 512], sg_t[tp][:, :], up_t[tp][:, :], ALU.mult))
                    P.fence_inc("v", "v")
                    P.mark(f"f_h_{ic}_{tt}", "v")
                    if ti > 0:
                        am_sl = smlF[2 * gp2 + si][:, (ti - 1) * 32:ti * 32]
                        h3 = h_b[hb][:, (ti - 1) * 512:ti * 512].rearrange(
                            "p (b s) -> p b s", s=16)
                        P.bare("v", lambda v, am_sl=am_sl, h3=h3: v.tensor_reduce(
                            am_sl, h3, AX.X, ALU.max, apply_absolute_value=True))
                sd = qstream(pp, f"fq{ic}_{tb}", h_b[hb][:, :],
                             hq_b[gb % 4][:, :], FTB * 512, 4,
                             tA[si][:, :], tC[si], tD[si], tE[si],
                             smlF[2 * gp2 + si][:, :], s8F[2 * gp2 + si][:, :])
                sd["pre_amax"] = ((FTB - 1) * 32, FTB * 32,
                                  h_b[hb][:, (FTB - 1) * 512:FTB * 512].rearrange(
                                      "p (b s) -> p b s", s=16))
                streams.append(sd)
            stq = emit_quant_multi_a(P, streams)
            flushq()
            pend[0] = (stq, ic, pb)
    flushq()
    P.fence_nop("g", "gc")
    P.mark("fa_done", "gc")
    P.fence_nop("t", "t")
    P.mark("fa_end_t", "t")
    P.fence_nop("v", "v")
    P.mark("fa_end_v", "v")
    P.fence_nop("s", "s")
    P.mark("fa_end_s", "s")


def build_fb(nc, P, f2stk, T):
    def fsb(name, shape, dt, **kw):
        return f2stk.enter_context(nc.sbuf_tensor(name, shape, dt, **kw))

    wqd = fsb("wqd", [128, 4, NIC, 1024], BF16)
    hq_i = [fsb(f"hq_i{i}", [128, 32, 128], BF16) for i in range(2)]
    r1t = [fsb(f"r1tf{i}", [128, D], F32) for i in range(2)]
    o_t = [fsb(f"o_tf{i}", [128, D], F32) for i in range(2)]
    psD = [f2stk.enter_context(nc.psum_tensor(f"psD{i}", [128, 1024], F32)) for i in range(2)]

    P.wait("sy", "fa_done")
    P.wait_now("sy", "hqs0")
    P.wait_now("sy", "hqs1")
    for eng in ("sy", "g", "s"):
        P.wait(eng, "fa_end_t")
        P.wait(eng, "fa_end_v")
        P.wait(eng, "fa_end_s")
        for sem in ("fw0", "fw1", "hqT0", "hqT1", "hqs0", "hqs1", "g"):
            P.wait_now(eng, sem)
    for ic in range(NIC):
        def mk_ld(h, ic=ic):
            src = T["cc_wq_out"][ic * 3 * SHW + 2 * SHW: ic * 3 * SHW + 3 * SHW].rearrange(
                "(su p r) -> p su r", p=128, r=1024)
            return h.dma_start(out=wqd[:, :, ic, :], in_=src)
        P.inc("sy", mk_ld, "sy", 16)
    P.mark("fb_wld", "sy")

    for tt in range(NTT):
        bi = tt % 2
        if tt >= 2:
            P.wait("sy", f"fb_mm_{tt - 2}")
        P.inc("sy", lambda h, tt=tt, bi=bi: h.dma_start(
            out=hq_i[bi][:, :, :], in_=T["hq_sp"][tt, :, :, :]), f"fbl{bi}", 16)
        P.inc("sy", lambda h, tt=tt, bi=bi: h.dma_start(
            out=r1t[bi][:, :], in_=T["r1_sp"][tt * 128:(tt + 1) * 128, :]), f"fbl{bi}", 16)
        P.mark(f"fb_ld_{tt}", f"fbl{bi}")

        P.wait("t", f"fb_ld_{tt}")
        if tt == 0:
            P.wait("t", "fb_wld")
        if tt >= 2:
            P.wait("t", f"fb_oev_{tt - 2}")
        for dh in range(2):
            n = 0
            for ic in range(NIC):
                for su in range(4):
                    n += 1
                    P.bare("t", lambda t, bi=bi, ic=ic, su=su, dh=dh, n=n: t.matmul(
                        psD[bi][:, dh * 512:(dh + 1) * 512], hq_i[bi][:, ic * 4 + su, :],
                        wqd[:, su, ic, dh * 512:(dh + 1) * 512],
                        start=(n == 1), stop=(n == 32)))
        P.fence_inc("t", "t")
        P.mark(f"fb_mm_{tt}", "t")

        P.wait("v", f"fb_mm_{tt}")
        P.bare("v", lambda v, bi=bi: v.tensor_tensor(o_t[bi][:, 0:512], psD[bi][:, 0:512],
                                                     r1t[bi][:, 0:512], ALU.add))
        P.bare("v", lambda v, bi=bi: v.tensor_tensor(o_t[bi][:, 512:1024], psD[bi][:, 512:1024],
                                                     r1t[bi][:, 512:1024], ALU.add))
        P.fence_inc("v", "v")
        P.mark(f"fb_oev_{tt}", "v")

        P.wait("sy", f"fb_oev_{tt}")
        P.inc("sy", lambda h, tt=tt, bi=bi: h.dma_start(
            out=T["out_half"][tt * 128:(tt + 1) * 128, :], in_=o_t[bi][:, :]), "sy", 16)
        P.mark(f"fb_out_{tt}", "sy")

    P.wait_now("sy", "sy")


def _host_prepare(inputs):
    x = np.ascontiguousarray(inputs["L0_attn_x"])
    Wq, Wk, Wv, Wo = (inputs["L0_attn_W_q"], inputs["L0_attn_W_k"],
                      inputs["L0_attn_W_v"], inputs["L0_attn_W_o"])
    gw, uw, dw = inputs["L0_ffn_gate_w"], inputs["L0_ffn_up_w"], inputs["L0_ffn_down_w"]
    s_in = float(np.asarray(inputs["L0_ffn_s_in"]).reshape(-1)[0])
    s_ind = float(np.asarray(inputs["L0_ffn_s_in_down"]).reshape(-1)[0])

    triu = np.triu(np.ones((128, 128), np.float32))
    mask512 = np.tile(triu, (1, 4)).astype(np.float32)

    in_maps = []
    for c in range(N_CORES):
        b, p = c // 2, c % 2
        hs = 512 * p
        scal = np.zeros((128, 8), np.float32)
        scal[:, 0] = 1e-5
        scal[:, 1] = s_in
        scal[:, 2] = s_ind
        scal[:, 3] = 1.0 - p
        scal[:, 4] = float(p)
        scal[:, 5] = float(p)
        scal[:, 6] = 1.0 - p
        partner = c ^ 1
        ntt = T_HALF // 128
        qr = T_HALF // NQ
        s_idx = np.arange(T_HALF)
        local = s_idx % qr
        gidx_v = (partner * qr + local).astype(np.int32).reshape(ntt, 128).T
        gidx_v = np.ascontiguousarray(gidx_v)
        wo_arr = np.concatenate([Wo[hs:hs + 512, :], Wo[512 - hs:512 - hs + 512, :]], axis=0)
        in_maps.append({
            "x_full": np.ascontiguousarray(x[b]),
            "x_half": np.ascontiguousarray(x[b, p * T_HALF:(p + 1) * T_HALF]),
            "wqi": np.ascontiguousarray(Wq[:, hs:hs + 512]),
            "wki": np.ascontiguousarray(Wk[:, hs:hs + 512]),
            "wvi": np.ascontiguousarray(Wv[:, hs:hs + 512]),
            "woi": np.ascontiguousarray(wo_arr),
            "gshi": np.ascontiguousarray(gw[c * 512:(c + 1) * 512, :]),
            "ushi": np.ascontiguousarray(uw[c * 512:(c + 1) * 512, :]),
            "dshi": np.ascontiguousarray(dw[:, c * 512:(c + 1) * 512]),
            "maski": mask512,
            "scali": scal,
            "gidxi": gidx_v,
        })
    return in_maps


def kernel(**inputs):
    in_maps = _host_prepare(inputs)
    nc = build_nc()
    res = run_bass_kernel_spmd(nc, in_maps, core_ids=list(range(N_CORES)))
    out = np.zeros((4, T_FULL, D), np.float32)
    for c in range(N_CORES):
        b, p = c // 2, c % 2
        out[b, p * T_HALF:(p + 1) * T_HALF, :] = np.asarray(res.results[c]["out_half"])
    return out


# version 28 (final)
# speedup vs baseline: 1.0393x; 1.0129x over previous
import contextlib
import numpy as np
import ml_dtypes
import concourse.bass as bass
import concourse.mybir as mybir
from concourse.bass_utils import run_bass_kernel_spmd
from concourse.bass import IndirectOffsetOnAxis

F32 = mybir.dt.float32
F8 = mybir.dt.float8e4
BF16 = mybir.dt.bfloat16
I32 = mybir.dt.int32
AF = mybir.ActivationFunctionType
ALU = mybir.AluOpType
AX = mybir.AxisListType

N_CORES = 8
D = 1024
NH = 8
C = 128
I_FFN = 4096
IC = 512
NIC = I_FFN // IC
SHW = D * 512
LN2C = float(np.log(2.0))
GSW_CONST = 448.0 * 6.0

T_FULL = 8192
T_HALF = 4096
NCH = 64
NSL = 32
NTT = 32
BQ = 4
NQ = 4
FTB = 4
NTB = 8


def set_dims(t_full):
    global T_FULL, T_HALF, NCH, NSL, NTT, BQ, FTB, NTB
    T_FULL = t_full
    T_HALF = t_full // 2
    NCH = T_FULL // C
    NSL = NCH // 2
    NTT = T_HALF // C
    BQ = min(4, NTT)
    FTB = max(1, min(4, NTT // 2))
    NTB = NTT // FTB
    assert NTB % 2 == 0
    global NQ
    NQ = 4 if (T_HALF // 4) % 128 == 0 and NSL % 4 == 0 else 2


set_dims(8192)
DEBUG = False
DBG_CI = 0

ENGINES = ("t", "v", "s", "sy", "g")


class Prog:

    def __init__(self):
        self.streams = {e: [] for e in ENGINES}
        self.cnt = {}
        self.marks = {}
        self.sems = {}

    def sem(self, name, handle):
        self.sems[name] = handle
        self.cnt[name] = 0

    def bare(self, eng, fn):
        self.streams[eng].append(fn)

    def inc(self, eng, fn, sem, n):
        s = self.sems[sem]
        self.cnt[sem] += n
        self.streams[eng].append(lambda h: fn(h).then_inc(s, n))

    def mark(self, name, sem):
        self.marks[name] = (sem, self.cnt[sem])

    def wait(self, eng, name):
        sem_name, val = self.marks[name]
        s = self.sems[sem_name]
        if val > 0:
            self.streams[eng].append(lambda h: h.wait_ge(s, val))

    def drain(self, eng):
        self.streams[eng].append(lambda h: h.drain())

    def wait_now(self, eng, sem):
        s = self.sems[sem]
        val = self.cnt[sem]
        if val > 0:
            self.streams[eng].append(lambda h: h.wait_ge(s, val))

    def fence_inc(self, eng, sem, n=1):
        s = self.sems[sem]
        self.cnt[sem] += n
        fn = self.streams[eng].pop()
        self.streams[eng].append(lambda h, fn=fn: fn(h).then_inc(s, n))

    def fence_nop(self, eng, sem, n=1):
        s = self.sems[sem]
        self.cnt[sem] += n
        self.streams[eng].append(lambda h: h.drain())
        self.streams[eng].append(lambda h: h.nop().then_inc(s, n))


def emit_quant_multi_a(P, streams):
    par = getattr(P, "q8c", 0) % 2
    P.q8c = getattr(P, "q8c", 0) + 1
    st = []
    for s_ in streams:
        nblk = s_["nel"] // 16
        sml = s_["sml"]
        d = dict(s_)
        d["nblk"] = nblk
        d["am"] = sml[:, 0 * nblk:1 * nblk]
        d["scb"] = sml[:, 1 * nblk:2 * nblk]
        d["sch"] = sml[:, 2 * nblk:3 * nblk]
        d["sa"] = sml[:, 3 * nblk:4 * nblk]
        d["sb"] = sml[:, 4 * nblk:5 * nblk]
        d["sel"] = sml[:, 5 * nblk:6 * nblk]
        d["u"] = sml[:, 6 * nblk:7 * nblk]
        d["dq"] = sml[:, 7 * nblk:8 * nblk]
        d["c8a"] = s_["s8"][:, 0:nblk]
        d["c8b"] = s_["s8"][:, nblk:2 * nblk]
        d["x3"] = s_["xin"].rearrange("p (b s) -> p b s", s=16)
        st.append(d)

    def each(fn):
        for d in st:
            fn(d)

    tag0 = st[0]["tag"]
    P.drain("v")
    for d in st:
        if "pre_amax" in d:
            lt = d["pre_amax"]
            P.bare("v", lambda v, d=d, lt=lt: v.tensor_reduce(
                d["am"][:, lt[0]:lt[1]], lt[2], AX.X, ALU.max, apply_absolute_value=True))
        else:
            P.bare("v", lambda v, d=d: v.tensor_reduce(
                d["am"], d["x3"], AX.X, ALU.max, apply_absolute_value=True))
    P.drain("v")
    each(lambda d: P.bare("v", lambda v, d=d: v.tensor_scalar(
        d["scb"], d["am"], d["gs6"], 192.0, ALU.mult, ALU.min)))
    each(lambda d: P.bare("v", lambda v, d=d: v.tensor_scalar(
        d["sch"], d["am"], d["gs6h"], 224.0, ALU.mult, ALU.min)))
    P.fence_inc("v", "v")
    P.mark(f"{tag0}_sc", "v")
    P.wait("g", f"{tag0}_sc")
    each(lambda d: P.inc("g", lambda g, d=d: g.dma_start(out=d["c8a"], in_=d["sch"]),
                         f"f8d{par}", 16))
    each(lambda d: P.inc("g", lambda g, d=d: g.dma_start(out=d["c8b"], in_=d["scb"]),
                         f"f8d{par}", 16))
    P.mark(f"{tag0}_8d", f"f8d{par}")
    P.wait("g", f"{tag0}_8d")
    each(lambda d: P.inc("g", lambda g, d=d: g.dma_start(out=d["sa"], in_=d["c8a"]),
                         f"f8u{par}", 16))
    each(lambda d: P.inc("g", lambda g, d=d: g.dma_start(out=d["sb"], in_=d["c8b"]),
                         f"f8u{par}", 16))
    P.mark(f"{tag0}_8u", f"f8u{par}")
    return st


def emit_quant_multi_b(P, st):
    def each(fn):
        for d in st:
            fn(d)

    tag0 = st[0]["tag"]
    P.wait("v", f"{tag0}_8u")
    P.drain("v")
    each(lambda d: P.bare("v", lambda v, d=d: v.tensor_scalar(
        d["sa"], d["sa"], d["invgs2"], None, ALU.mult)))
    each(lambda d: P.bare("v", lambda v, d=d: v.tensor_scalar(
        d["sb"], d["sb"], d["invgs"], None, ALU.mult)))
    each(lambda d: P.bare("v", lambda v, d=d: v.tensor_scalar(
        d["sel"], d["scb"], 1.0, None, ALU.is_ge)))
    P.drain("v")
    each(lambda d: P.bare("v", lambda v, d=d: v.tensor_tensor(d["u"], d["sa"], d["sb"],
                                                              ALU.subtract)))
    P.drain("v")
    each(lambda d: P.bare("v", lambda v, d=d: v.tensor_tensor(d["u"], d["u"], d["sel"],
                                                              ALU.mult)))
    P.drain("v")
    each(lambda d: P.bare("v", lambda v, d=d: v.tensor_tensor(d["dq"], d["u"], d["sb"],
                                                              ALU.add)))
    P.drain("v")
    each(lambda d: P.bare("v", lambda v, d=d: v.tensor_scalar(
        d["am"], d["dq"], 1e-30, None, ALU.max)))
    P.drain("v")
    each(lambda d: P.bare("v", lambda v, d=d: v.reciprocal(d["sel"], d["am"])))
    P.drain("v")
    each(lambda d: P.bare("v", lambda v, d=d: v.tensor_tensor(
        d["x3"], d["x3"],
        d["sel"].unsqueeze(2).broadcast_to([128, d["nblk"], 16]), ALU.mult)))
    P.fence_inc("v", "v")
    P.mark(f"{tag0}_t", "v")
    P.wait("s", f"{tag0}_t")
    each(lambda d: P.bare("s", lambda s, d=d: s.activation(d["tA"], d["xin"], AF.Abs)))
    P.fence_inc("s", "s")
    P.mark(f"{tag0}_abs", "s")
    P.wait("v", f"{tag0}_abs")
    each(lambda d: P.bare("v", lambda v, d=d: v.tensor_scalar(
        d["tC"], d["tA"], 0.25, None, ALU.is_ge)))
    each(lambda d: P.bare("v", lambda v, d=d: v.tensor_scalar(
        d["xq_out"], d["tA"], 0.75, None, ALU.is_ge)))
    each(lambda d: P.bare("v", lambda v, d=d: v.tensor_scalar(
        d["tD"], d["tA"], 5.0, 2.0, ALU.is_ge, ALU.mult)))
    each(lambda d: P.bare("v", lambda v, d=d: v.tensor_scalar(
        d["tE"], d["tA"], 3.5, None, ALU.is_ge)))
    P.drain("v")
    each(lambda d: P.bare("v", lambda v, d=d: v.scalar_tensor_tensor(
        d["tC"], d["tA"], 1.25, d["tC"], ALU.is_ge, ALU.add)))
    each(lambda d: P.bare("v", lambda v, d=d: v.scalar_tensor_tensor(
        d["xq_out"], d["tA"], 1.75, d["xq_out"], ALU.is_ge, ALU.add)))
    each(lambda d: P.bare("v", lambda v, d=d: v.scalar_tensor_tensor(
        d["tD"], d["tA"], 2.5, d["tD"], ALU.is_ge, ALU.add)))
    P.fence_inc("v", "v")
    P.mark(f"{tag0}_ch", "v")
    P.wait("s", f"{tag0}_ch")
    each(lambda d: P.bare("s", lambda s, d=d: s.activation(d["tA"], d["xin"], AF.Sign)))
    P.fence_inc("s", "s")
    P.mark(f"{tag0}_sgn", "s")
    P.drain("v")
    each(lambda d: P.bare("v", lambda v, d=d: v.tensor_tensor(d["tC"], d["tC"], d["xq_out"],
                                                              ALU.add)))
    each(lambda d: P.bare("v", lambda v, d=d: v.tensor_tensor(d["tD"], d["tD"], d["tE"],
                                                              ALU.add)))
    P.wait("v", f"{tag0}_sgn")
    P.drain("v")
    each(lambda d: P.bare("v", lambda v, d=d: v.scalar_tensor_tensor(
        d["tC"], d["tC"], 0.5, d["tD"], ALU.mult, ALU.add)))
    P.drain("v")
    each(lambda d: P.bare("v", lambda v, d=d: v.tensor_tensor(d["tC"], d["tC"], d["tA"],
                                                              ALU.mult)))
    P.drain("v")
    each(lambda d: P.bare("v", lambda v, d=d: v.tensor_tensor(
        d["xq_out"].rearrange("p (b s) -> p b s", s=16),
        d["tC"].rearrange("p (b s) -> p b s", s=16),
        d["dq"].unsqueeze(2).broadcast_to([128, d["nblk"], 16]), ALU.mult)))
    P.drain("v")


def emit_quant_multi(P, streams):
    emit_quant_multi_b(P, emit_quant_multi_a(P, streams))


def qstream(pp, tag, xin, xq_out, nel, ui, tA, tC, tD, tE, sml, s8):
    return dict(tag=tag, xin=xin, xq_out=xq_out, nel=nel,
                gs6=pp[:, 2 * ui:2 * ui + 1], invgs=pp[:, 2 * ui + 1:2 * ui + 2],
                gs6h=pp[:, 18 + 2 * ui:19 + 2 * ui], invgs2=pp[:, 19 + 2 * ui:20 + 2 * ui],
                tA=tA, tC=tC, tD=tD, tE=tE, sml=sml, s8=s8)


def emit_quant(P, pp, tag, xin, xq_out, nel, ui, tA, tC, tD, tE, sml, s8):
    emit_quant_multi(P, [qstream(pp, tag, xin, xq_out, nel, ui, tA, tC, tD, tE, sml, s8)])


def emit_ln(P, tag, x_ap, out_ap, bns, mv, stdv, inv, negm, eps_pp, prev_tag):
    if prev_tag is not None:
        P.wait("v", prev_tag)
    P.drain("v")
    P.bare("v", lambda v: v.bn_stats(bns[:, 0:6], x_ap[:, 0:512]))
    P.drain("v")
    P.bare("v", lambda v: v.bn_stats(bns[:, 6:12], x_ap[:, 512:1024]))
    P.drain("v")
    P.bare("v", lambda v: v.bn_aggr(mv, bns))
    P.fence_inc("v", "v")
    P.mark(f"{tag}_bns", "v")
    P.wait("s", f"{tag}_bns")
    P.inc("s", lambda s: s.activation(stdv, mv[:, 1:2], AF.Sqrt, bias=eps_pp, scale=1.0), "s", 1)
    P.mark(f"{tag}_std", "s")
    P.wait("v", f"{tag}_std")
    P.bare("v", lambda v: v.reciprocal(inv, stdv))
    P.drain("v")
    P.bare("v", lambda v: v.tensor_scalar(negm, mv[:, 0:1], inv[:, 0:1], -1.0, ALU.mult, ALU.mult))
    P.drain("v")
    P.bare("v", lambda v: v.tensor_scalar(out_ap, x_ap, inv[:, 0:1], negm[:, 0:1],
                                          ALU.mult, ALU.add))
    P.fence_inc("v", "v")
    P.mark(f"{tag}_ln", "v")


def emit_ln2(P, tagA, xA, outA, tagB, xB, outB, st2, eps_pp, prev_tag):
    if prev_tag is not None:
        P.wait("v", prev_tag)
    P.drain("v")
    P.bare("v", lambda v: v.bn_stats(st2["bnsA"][:, 0:6], xA[:, 0:512]))
    P.bare("v", lambda v: v.bn_stats(st2["bnsB"][:, 0:6], xB[:, 0:512]))
    P.drain("v")
    P.bare("v", lambda v: v.bn_stats(st2["bnsA"][:, 6:12], xA[:, 512:1024]))
    P.bare("v", lambda v: v.bn_stats(st2["bnsB"][:, 6:12], xB[:, 512:1024]))
    P.drain("v")
    P.bare("v", lambda v: v.bn_aggr(st2["mvA"][:, :], st2["bnsA"][:, :]))
    P.bare("v", lambda v: v.bn_aggr(st2["mvB"][:, :], st2["bnsB"][:, :]))
    P.fence_inc("v", "v")
    P.mark(f"{tagA}_bns", "v")
    P.wait("s", f"{tagA}_bns")
    P.bare("s", lambda s: s.activation(st2["stdA"][:, :], st2["mvA"][:, 1:2], AF.Sqrt,
                                       bias=eps_pp, scale=1.0))
    P.drain("s")
    P.inc("s", lambda s: s.activation(st2["stdB"][:, :], st2["mvB"][:, 1:2], AF.Sqrt,
                                      bias=eps_pp, scale=1.0), "s", 1)
    P.mark(f"{tagA}_std", "s")
    P.wait("v", f"{tagA}_std")
    P.bare("v", lambda v: v.reciprocal(st2["invA"][:, :], st2["stdA"][:, :]))
    P.bare("v", lambda v: v.reciprocal(st2["invB"][:, :], st2["stdB"][:, :]))
    P.drain("v")
    P.bare("v", lambda v: v.tensor_scalar(st2["nmA"][:, :], st2["mvA"][:, 0:1], st2["invA"][:, 0:1],
                                          -1.0, ALU.mult, ALU.mult))
    P.bare("v", lambda v: v.tensor_scalar(st2["nmB"][:, :], st2["mvB"][:, 0:1], st2["invB"][:, 0:1],
                                          -1.0, ALU.mult, ALU.mult))
    P.drain("v")
    P.bare("v", lambda v: v.tensor_scalar(outA, xA, st2["invA"][:, 0:1], st2["nmA"][:, 0:1],
                                          ALU.mult, ALU.add))
    P.bare("v", lambda v: v.tensor_scalar(outB, xB, st2["invB"][:, 0:1], st2["nmB"][:, 0:1],
                                          ALU.mult, ALU.add))
    P.fence_inc("v", "v")
    P.mark(f"{tagA}_ln", "v")
    P.mark(f"{tagB}_ln", "v")


def build_nc():
    nc = bass.Bass("TRN2", target_bir_lowering=False)
    P = Prog()

    x_full = nc.dram_tensor("x_full", [T_FULL, D], F32, kind="ExternalInput").ap()
    x_half = nc.dram_tensor("x_half", [T_HALF, D], F32, kind="ExternalInput").ap()
    wqi = nc.dram_tensor("wqi", [D, 512], F32, kind="ExternalInput").ap()
    wki = nc.dram_tensor("wki", [D, 512], F32, kind="ExternalInput").ap()
    wvi = nc.dram_tensor("wvi", [D, 512], F32, kind="ExternalInput").ap()
    woi = nc.dram_tensor("woi", [D, D], F32, kind="ExternalInput").ap()
    gshi = nc.dram_tensor("gshi", [512, D], F32, kind="ExternalInput").ap()
    ushi = nc.dram_tensor("ushi", [512, D], F32, kind="ExternalInput").ap()
    dshi = nc.dram_tensor("dshi", [D, 512], F32, kind="ExternalInput").ap()
    maski = nc.dram_tensor("maski", [128, 512], F32, kind="ExternalInput").ap()
    scali = nc.dram_tensor("scali", [128, 8], F32, kind="ExternalInput").ap()
    gidxi = nc.dram_tensor("gidxi", [128, NTT], I32, kind="ExternalInput").ap()
    out_half = nc.dram_tensor("out_half", [T_HALF, D], F32, kind="ExternalOutput").ap()

    cc_am_in = nc.dram_tensor("cc_am_in", [1, 3], F32).ap()
    cc_am_out = nc.dram_tensor("cc_am_out", [1, 3], F32).ap()
    cc_wq_in = nc.dram_tensor("cc_wq_in", [3 * SHW], BF16).ap()
    cc_wq_out = nc.dram_tensor("cc_wq_out", [N_CORES * 3 * SHW], BF16).ap()
    cc_o_in = nc.dram_tensor("cc_o_in", [T_HALF, 512], BF16).ap()
    cc_o_out_q = [nc.dram_tensor(f"cc_o_out{k}", [N_CORES * T_HALF // NQ, 512], BF16).ap()
                  for k in range(NQ)]
    r1_sp = nc.dram_tensor("r1_sp", [T_HALF, D], F32).ap()
    dbg_d = nc.dram_tensor("dbg_d", [128, 7680], F32).ap()
    dbg2_d = nc.dram_tensor("dbg2_d", [128, NTT * 1024], BF16).ap()
    dbg3_d = nc.dram_tensor("dbg3_d", [128, BQ * 1024 + 8 * BQ * 64], F32).ap()
    hq_sp = nc.dram_tensor("hq_sp", [NTT, 128, 32, 128], BF16).ap()

    T = dict(dbg_d=dbg_d, dbg2_d=dbg2_d, dbg3_d=dbg3_d, x_full=x_full, x_half=x_half, wqi=wqi, wki=wki, wvi=wvi, woi=woi,
             gshi=gshi, ushi=ushi, dshi=dshi, maski=maski, scali=scali, gidxi=gidxi,
             out_half=out_half, cc_am_in=cc_am_in, cc_am_out=cc_am_out,
             cc_wq_in=cc_wq_in, cc_wq_out=cc_wq_out, cc_o_in=cc_o_in,
             cc_o_out_q=cc_o_out_q, r1_sp=r1_sp, hq_sp=hq_sp)

    with contextlib.ExitStack() as stk:
        def sbp(name, shape, dt, **kw):
            return stk.enter_context(nc.sbuf_tensor(name, shape, dt, **kw))

        consts = sbp("consts", [128, 8], F32)
        maskt = sbp("maskt", [128, 512], F32)
        sk_sb = [sbp(f"sk_sb{i}", [64, 512], BF16) for i in range(2)]
        sk_f32 = sbp("sk_f32", [64, 512], F32)
        z_sb = sbp("z_sb", [64, 512], BF16)
        z_f32 = sbp("z_f32", [64, 512], F32)
        pp = sbp("pp", [128, 28], F32)
        gidx = sbp("gidx", [128, NTT], I32)
        T.update(consts=consts, maskt=maskt, sk_sb=sk_sb, sk_f32=sk_f32, z_sb=z_sb,
                 z_f32=z_f32, pp=pp, gidx=gidx)

        semnames = (["sy", "sct", "g", "gc", "cc", "t", "v", "s", "wtr", "wst", "xqT"]
                    + [f"{k}{i}" for k in ("xld", "lnT", "qkT", "wT", "ost", "og", "oT",
                                           "r1s", "fw", "hqT", "hqs", "fbl") for i in (0, 1)]
                    + ["qkT2", "f8d0", "f8d1", "f8u0", "f8u1"])
        for nm in semnames:
            h = stk.enter_context(nc.semaphore("s_" + nm))
            P.sem(nm, h)

        with nc.sbuf_tensor("o_my", [128, NSL * 512], BF16) as o_my:
            T["o_my"] = o_my
            qstk = contextlib.ExitStack()
            for nm in ("wq_sb", "wk_sb", "wv_sb"):
                T[nm] = qstk.enter_context(nc.sbuf_tensor(nm, [128, 8, 512], BF16))
            wstk = contextlib.ExitStack()
            build_w_head(nc, P, wstk, T)
            with contextlib.ExitStack() as astk:
                build_attn(nc, P, astk, T)
            wstk.close()
            qstk.close()
            with nc.sbuf_tensor("xq_res", [128, NTT * 8 * 128], BF16,
                                is_transpose_dest=True) as xq_res:
                T["xq_res"] = xq_res
                with contextlib.ExitStack() as bstk:
                    build_b(nc, P, bstk, T)
                with contextlib.ExitStack() as fstk:
                    build_fa(nc, P, fstk, T)
        with contextlib.ExitStack() as f2stk:
            build_fb(nc, P, f2stk, T)

        with nc.Block() as blk:
            @blk.tensor
            def _(t):
                for fn in P.streams["t"]:
                    fn(t)

            @blk.vector
            def _(v):
                for fn in P.streams["v"]:
                    fn(v)

            @blk.scalar
            def _(s):
                for fn in P.streams["s"]:
                    fn(s)

            @blk.sync
            def _(sy):
                for fn in P.streams["sy"]:
                    fn(sy)

            @blk.gpsimd
            def _(g):
                for fn in P.streams["g"]:
                    fn(g)

    return nc


def build_w_head(nc, P, wstk, T):
    consts, pp = T["consts"], T["pp"]
    o_my = T["o_my"]
    wsh = [wstk.enter_context(nc.sbuf_tensor(f"wsh{i}", [128, 4096], F32)) for i in range(3)]
    wqt = wstk.enter_context(nc.sbuf_tensor("wqt", [128, 4096], BF16))
    scr1 = wstk.enter_context(nc.sbuf_tensor("scr1", [128, 2048], BF16))
    scr2 = wstk.enter_context(nc.sbuf_tensor("scr2", [128, 2048], BF16))
    scr3 = wstk.enter_context(nc.sbuf_tensor("scr3", [128, 2048], BF16))
    if NSL * 512 - 23 * 512 >= 4096:
        scr4 = o_my[:, 23 * 512:23 * 512 + 4096]
    else:
        scr4 = wstk.enter_context(nc.sbuf_tensor("scr4", [128, 4096], BF16))[:, :]
    sml = wstk.enter_context(nc.sbuf_tensor("smlW", [128, 8 * 128], F32))
    s8W = wstk.enter_context(nc.sbuf_tensor("s8W", [128, 256], F8))
    wfm = wstk.enter_context(nc.sbuf_tensor("wfm", [128, 8, 128], BF16, is_transpose_dest=True))

    P.inc("sy", lambda h: h.dma_start(
        out=wsh[0][:, :].rearrange("p (j f) -> p j f", f=D),
        in_=T["gshi"].rearrange("(j p) f -> p j f", p=128)), "sy", 16)
    P.inc("sy", lambda h: h.dma_start(
        out=wsh[1][:, :].rearrange("p (j f) -> p j f", f=D),
        in_=T["ushi"].rearrange("(j p) f -> p j f", p=128)), "sy", 16)
    P.inc("sy", lambda h: h.dma_start(
        out=wsh[2][:, :].rearrange("p (j f) -> p j f", f=512),
        in_=T["dshi"].rearrange("(j p) f -> p j f", p=128)), "sy", 16)
    P.inc("sy", lambda h: h.dma_start(out=T["consts"][:, :], in_=T["scali"][:, :]), "sy", 16)
    P.inc("sy", lambda h: h.dma_start(out=T["maskt"][:, :], in_=T["maski"][:, :]), "sy", 16)
    P.inc("sy", lambda h: h.dma_start(out=T["gidx"][:, :], in_=T["gidxi"][:, :]), "sy", 16)
    P.mark("w_ld", "sy")
    for wsrc, wdst in ((T["wqi"], T["wq_sb"]), (T["wki"], T["wk_sb"]),
                       (T["wvi"], T["wv_sb"])):
        P.inc("g", lambda g, wsrc=wsrc, wdst=wdst: g.dma_start(
            out=wdst[:, :, :], in_=wsrc.rearrange("(m p) c -> p m c", p=128)), "sct", 16)
    P.mark("a_wld", "sct")

    P.bare("v", lambda v: v.memset(pp[:, 16:17], -6.0 * LN2C))
    P.bare("v", lambda v: v.memset(pp[:, 17:18], 6.0 * LN2C))
    P.drain("v")
    P.wait("v", "w_ld")
    for i in range(3):
        P.bare("v", lambda v, i=i: v.tensor_reduce(
            pp[:, 10 + i:11 + i], wsh[i][:, :], AX.X, ALU.max,
            apply_absolute_value=True))
    P.fence_inc("v", "v")
    P.mark("w_amax", "v")
    P.wait("g", "w_amax")
    P.inc("g", lambda g: g.tensor_reduce(pp[0:1, 13:16], pp[:, 10:13], AX.C, ALU.max), "gc", 1)
    P.wait_now("g", "gc")
    P.inc("g", lambda g: g.dma_start(out=T["cc_am_in"][0:1, 0:3], in_=pp[0:1, 13:16]), "g", 16)
    P.wait_now("g", "g")
    P.inc("g", lambda g: g.collective_compute(
        "AllReduce", ALU.max, replica_groups=[list(range(N_CORES))],
        ins=[T["cc_am_in"].opt()], outs=[T["cc_am_out"].opt()]), "cc", 1)
    P.wait_now("g", "cc")
    P.inc("g", lambda g: g.dma_start(
        out=pp[:, 10:13], in_=T["cc_am_out"][0:1, :].broadcast_to([128, 3])), "g", 16)
    P.mark("w_gam", "g")

    tr_i = [0]

    def tail(i):
        if i == 0:
            P.wait("v", "w_gam")
            for j in range(3):
                P.bare("v", lambda v, j=j: v.tensor_scalar(
                    pp[:, 1 + 2 * j:2 + 2 * j], pp[:, 10 + j:11 + j], 1.0 / GSW_CONST,
                    None, ALU.mult))
                P.bare("v", lambda v, j=j: v.reciprocal(pp[:, 13:14], pp[:, 10 + j:11 + j]))
                P.drain("v")
                P.bare("v", lambda v, j=j: v.tensor_scalar(
                    pp[:, 0 + 2 * j:1 + 2 * j], pp[:, 13:14], 448.0, None, ALU.mult))
                P.drain("v")
            P.bare("v", lambda v: v.tensor_scalar(pp[:, 6:7], consts[:, 1:2], 1.0 / 6.0,
                                                  None, ALU.mult))
            P.bare("v", lambda v: v.reciprocal(pp[:, 7:8], consts[:, 1:2]))
            P.bare("v", lambda v: v.tensor_scalar(pp[:, 8:9], consts[:, 2:3], 1.0 / 6.0,
                                                  None, ALU.mult))
            P.bare("v", lambda v: v.reciprocal(pp[:, 9:10], consts[:, 2:3]))
            P.drain("v")
            for j in range(5):
                P.bare("v", lambda v, j=j: v.tensor_scalar(
                    pp[:, 18 + 2 * j:19 + 2 * j], pp[:, 2 * j:2 * j + 1], 0.5, None, ALU.mult))
                P.bare("v", lambda v, j=j: v.tensor_scalar(
                    pp[:, 19 + 2 * j:20 + 2 * j], pp[:, 2 * j + 1:2 * j + 2], 2.0, None, ALU.mult))
            P.drain("v")
        if i < 3:
            if i > 0:
                P.wait("v", f"w_tr_{tr_i[0] - 1}")
            for jh in range(2):
                emit_quant(P, pp, f"wq{i}_{jh}", wsh[i][:, jh * 2048:(jh + 1) * 2048],
                           wqt[:, jh * 2048:(jh + 1) * 2048], 2048, i,
                           scr1[:, :], scr2[:, :], scr3[:, :],
                           scr4[:, jh * 2048:(jh + 1) * 2048], sml[:, :],
                           s8W[:, :])
            P.fence_inc("v", "v")
            P.mark(f"w_q_{i}", "v")
            P.wait("sy", f"w_q_{i}")
            njt, freew, rowtot = [(4, 1024, 512), (4, 1024, 512), (8, 512, 1024)][i]
            off_dr = i * SHW
            nmid = freew // 128
            for j in range(njt):
                if tr_i[0] > 0:
                    P.wait("sy", f"w_st_{tr_i[0] - 1}")
                srcw = wqt[:, j * freew:(j + 1) * freew]
                P.inc("sy", lambda h, srcw=srcw, nmid=nmid: h.dma_start_transpose(
                    wfm[:, 0:nmid, :], srcw), "wtr", 16)
                P.mark(f"w_tr_{tr_i[0]}", "wtr")
                P.wait("g", f"w_tr_{tr_i[0]}")

                def mk_store(h, nmid=nmid, off_dr=off_dr, j=j, rowtot=rowtot):
                    dst3 = T["cc_wq_in"][off_dr:off_dr + SHW].rearrange(
                        "(m p r) -> p m r", p=128, r=rowtot)
                    return h.dma_start(out=dst3[:, 0:nmid, j * 128:(j + 1) * 128],
                                       in_=wfm[:, 0:nmid, :])
                P.inc("g", mk_store, "wst", 16)
                P.mark(f"w_st_{tr_i[0]}", "wst")
                tr_i[0] += 1
        else:
            P.wait_now("g", "g")
            P.wait_now("g", "wst")
            P.fence_nop("g", "gc")
            P.mark("w_g_done", "gc")
            P.inc("g", lambda g: g.collective_compute(
                "AllGather", ALU.bypass, replica_groups=[list(range(N_CORES))],
                ins=[T["cc_wq_in"].opt()], outs=[T["cc_wq_out"].opt()]), "cc", 1)
            P.mark("w_ag", "cc")

    T["w_tails"] = [lambda i=i: tail(i) for i in range(4)]


def build_attn(nc, P, astk, T):
    consts, maskt, o_my = T["consts"], T["maskt"], T["o_my"]
    sk_sb, z_sb, z_f32 = T["sk_sb"], T["z_sb"], T["z_f32"]
    sk_f32 = T["sk_f32"]
    NB = 2

    def asb(name, shape, dt, **kw):
        return astk.enter_context(nc.sbuf_tensor(name, shape, dt, **kw))

    def aps(name, shape):
        return astk.enter_context(nc.psum_tensor(name, shape, F32))

    x_t = [asb(f"x_t{i}", [128, D], F32) for i in range(NB)]
    st2 = dict(bnsA=asb("a_bnsA", [128, 12], F32), bnsB=asb("a_bnsB", [128, 12], F32),
               mvA=asb("a_mvA", [128, 2], F32), mvB=asb("a_mvB", [128, 2], F32),
               stdA=asb("a_stdA", [128, 1], F32), stdB=asb("a_stdB", [128, 1], F32),
               invA=asb("a_invA", [128, 1], F32), invB=asb("a_invB", [128, 1], F32),
               nmA=asb("a_nmA", [128, 1], F32), nmB=asb("a_nmB", [128, 1], F32))
    ln_tm = [asb(f"ln_tm{i}", [128, D], BF16) for i in range(NB)]
    ln_fm = [asb(f"ln_fm{i}", [128, 8, 128], BF16, is_transpose_dest=True) for i in range(NB)]
    NBQ = 3
    q_sp = [asb(f"q_sp{i}", [128, 1024], BF16) for i in range(NBQ)]
    k_sp = [asb(f"k_sp{i}", [128, 1024], BF16) for i in range(NBQ)]
    q_fm = [asb(f"q_fm{i}", [128, 8, 128], BF16, is_transpose_dest=True) for i in range(NBQ)]
    k_fm = [asb(f"k_fm{i}", [128, 8, 128], BF16, is_transpose_dest=True) for i in range(NBQ)]
    k_tm = [asb(f"k_tm{i}", [128, 512], BF16) for i in range(NBQ)]
    v_tm = [asb(f"v_tm{i}", [128, 512], BF16) for i in range(NBQ)]
    a_sb = [asb(f"a_sb{i}", [128, 1024], BF16) for i in range(NB)]
    w_all = [asb(f"w_all{i}", [64, 1024], BF16) for i in range(NB)]
    w_tm = [asb(f"w_tm{i}", [128, 8, 64], BF16, is_transpose_dest=True) for i in range(NB)]
    b_sb = [asb(f"b_sb{i}", [128, 1024], BF16) for i in range(NB)]
    NBO = 6
    o_st = [asb(f"o_st{i}", [128, 512], BF16) for i in range(NBO)]
    wq_sb, wk_sb, wv_sb = T["wq_sb"], T["wk_sb"], T["wv_sb"]
    dbg_sb = asb("dbg_sb", [128, 7680], F32) if DEBUG else None

    psQ = aps("psQ", [128, 512])
    psK = aps("psK", [128, 512])
    psVO = aps("psVO", [128, 512])
    psAB = aps("psAB", [128, 1024])
    psW = aps("psW", [64, 1024])
    psSK = aps("psSK", [64, 512])

    sel1 = consts[:, 3:4]
    sel2 = consts[:, 4:5]
    selx1 = consts[:, 5:6]
    selx2 = consts[:, 6:7]

    if DEBUG:
        P.bare("v", lambda v: v.memset(dbg_sb[:, :], 0.0))
    P.bare("v", lambda v: v.memset(z_sb[:, :], 0.0))
    P.bare("v", lambda v: v.memset(sk_sb[0][:, :], 0.0))
    P.bare("v", lambda v: v.memset(sk_sb[1][:, :], 0.0))
    P.bare("v", lambda v: v.memset(sk_f32[:, :], 0.0))
    P.bare("v", lambda v: v.memset(z_f32[:, :], 0.0))
    for i in range(NBQ):
        P.bare("v", lambda v, i=i: v.memset(q_sp[i][:, :], 0.0))
        P.bare("v", lambda v, i=i: v.memset(k_sp[i][:, :], 0.0))
    P.drain("v")
    P.fence_inc("v", "v")
    P.mark("a_zsb_-1", "v")
    P.mark("a_sk0_0", "v")
    P.mark("a_state0", "v")

    def ok(ci):
        return 0 <= ci < NCH

    def st_front2(c0):
        pair = [c for c in (c0, c0 + 1) if ok(c)]
        if not pair:
            return
        for ci in pair:
            bi = ci % NB
            if ci >= NB:
                P.wait("sy", f"a_ln_{ci - NB}")
            P.inc("sy", lambda h, ci=ci, bi=bi: h.dma_start(
                out=x_t[bi][:, :], in_=T["x_full"][ci * 128:(ci + 1) * 128, :]), f"xld{bi}", 16)
            P.mark(f"a_xld_{ci}", f"xld{bi}")
        for ci in pair:
            P.wait("v", f"a_xld_{ci}")
        prev = f"aln{c0 - 1}_ln" if c0 > 0 else None
        if len(pair) == 2:
            emit_ln2(P, f"aln{c0}", x_t[c0 % NB][:, :], ln_tm[c0 % NB][:, :],
                     f"aln{c0 + 1}", x_t[(c0 + 1) % NB][:, :], ln_tm[(c0 + 1) % NB][:, :],
                     st2, consts[:, 0:1], prev)
        else:
            ci = pair[0]
            emit_ln(P, f"aln{ci}", x_t[ci % NB][:, :], ln_tm[ci % NB][:, :],
                    st2["bnsA"][:, :], st2["mvA"][:, :], st2["stdA"][:, :],
                    st2["invA"][:, :], st2["nmA"][:, :], consts[:, 0:1], prev)
        for ci in pair:
            P.mark(f"a_ln_{ci}", "v")
        for ci in pair:
            bi = ci % NB
            P.wait("sy", f"a_ln_{ci}")
            if ci >= NB:
                P.wait("sy", f"a_proj_{ci - NB}")
            P.inc("sy", lambda h, bi=bi: h.dma_start_transpose(
                ln_fm[bi][:, :, :], ln_tm[bi][:, :]), f"lnT{bi}", 16)
            P.mark(f"a_lnT_{ci}", f"lnT{bi}")

    def st_B1(ci):
        if not ok(ci):
            return
        bi = ci % NB
        P.wait("t", f"a_lnT_{ci}")
        if ci == 0:
            P.wait("t", "a_wld")
        if ci > 0:
            P.wait("t", f"a_qkvev_{ci - 1}")
        if ci >= 3:
            P.wait("t", f"a_oev_{ci - 3}")
        for m in range(8):
            P.bare("t", lambda t, bi=bi, m=m: t.matmul(
                psQ[:, :], ln_fm[bi][:, m, :], wq_sb[:, m, :], start=(m == 0), stop=(m == 7)))
        for m in range(8):
            P.bare("t", lambda t, bi=bi, m=m: t.matmul(
                psK[:, :], ln_fm[bi][:, m, :], wk_sb[:, m, :], start=(m == 0), stop=(m == 7)))
        for m in range(8):
            P.bare("t", lambda t, bi=bi, m=m: t.matmul(
                psVO[:, :], ln_fm[bi][:, m, :], wv_sb[:, m, :], start=(m == 0), stop=(m == 7)))
        P.fence_inc("t", "t")
        P.mark(f"a_proj_{ci}", "t")

    def st_qkvev(ci):
        if not ok(ci):
            return
        bq = ci % NBQ
        P.wait("v", f"a_proj_{ci}")
        if ci >= NBQ:
            P.wait("v", f"a_qkT_{ci - NBQ}")
            P.wait("v", f"a_skz_{ci - NBQ}")
        P.bare("v", lambda v, bq=bq: v.tensor_copy(
            q_sp[bq][:, :].rearrange("p (h r) -> p h r", r=128)[:, :, 0:64],
            psQ[:, :].rearrange("p (h r) -> p h r", r=64)))
        P.bare("v", lambda v, bq=bq: v.tensor_copy(
            k_sp[bq][:, :].rearrange("p (h r) -> p h r", r=128)[:, :, 0:64],
            psK[:, :].rearrange("p (h r) -> p h r", r=64)))
        P.bare("v", lambda v, bq=bq: v.tensor_copy(k_tm[bq][:, :], psK[:, :]))
        P.bare("v", lambda v, bq=bq: v.tensor_copy(v_tm[bq][:, :], psVO[:, :]))
        P.fence_inc("v", "v")
        P.mark(f"a_qkvev_{ci}", "v")

    def st_qkT(ci):
        if not ok(ci):
            return
        bq = ci % NBQ
        sem = f"qkT{bq}" if bq < 2 else "qkT2"
        P.wait("sy", f"a_qkvev_{ci}")
        if ci >= NBQ:
            P.wait("sy", f"a_skz_{ci - NBQ}")
        P.inc("sy", lambda h, bq=bq: h.dma_start_transpose(q_fm[bq][:, :, :], q_sp[bq][:, :]),
              sem, 16)
        P.inc("sy", lambda h, bq=bq: h.dma_start_transpose(k_fm[bq][:, :, :], k_sp[bq][:, :]),
              sem, 16)
        P.mark(f"a_qkT_{ci}", sem)

    def st_B2(ci):
        if not ok(ci):
            return
        bq = ci % NBQ
        P.wait("t", f"a_qkT_{ci}")
        if ci > 0:
            P.wait("t", f"a_bmask_{ci - 1}")
            P.wait("t", f"a_skadd_{ci - 1}")
        for h_ in range(8):
            P.bare("t", lambda t, bq=bq, h_=h_: t.matmul(
                psAB[:, h_ * 128:(h_ + 1) * 128], k_fm[bq][0:64, h_, :], q_fm[bq][0:64, h_, :],
                start=True, stop=True))
        for h_ in range(8):
            P.bare("t", lambda t, bq=bq, h_=h_: t.matmul(
                psSK[0:64, h_ * 64:(h_ + 1) * 64], k_tm[bq][:, h_ * 64:(h_ + 1) * 64],
                k_tm[bq][:, h_ * 64:(h_ + 1) * 64], start=True, stop=True))
        P.fence_inc("t", "t")
        P.mark(f"a_A_{ci}", "t")

    def st_amask_skadd(ci):
        if not ok(ci):
            return
        bi = ci % NB
        P.wait("v", f"a_A_{ci}")
        if ci >= NB:
            P.wait("v", f"a_W_{ci - NB}")
        P.bare("v", lambda v, bi=bi: v.tensor_tensor(a_sb[bi][:, 0:512], psAB[:, 0:512],
                                                     maskt[:, :], ALU.mult))
        P.bare("v", lambda v, bi=bi: v.tensor_tensor(a_sb[bi][:, 512:1024], psAB[:, 512:1024],
                                                     maskt[:, :], ALU.mult))
        P.fence_inc("v", "v")
        P.mark(f"a_amask_{ci}", "v")
        P.drain("v")
        P.bare("v", lambda v: v.tensor_tensor(sk_f32[:, :], sk_f32[:, :], psSK[:, :], ALU.add))
        P.drain("v")
        P.fence_inc("v", "v")
        P.mark(f"a_skadd_{ci}", "v")
        if ci + 1 < NCH:
            P.wait("v", f"a_W_{ci - 1}") if ci >= 1 else None
            P.bare("v", lambda v, ci=ci: v.tensor_copy(sk_sb[(ci + 1) % 2][:, :], sk_f32[:, :]))
            P.drain("v")
            P.fence_inc("v", "v")
            P.mark(f"a_sk0_{ci + 1}", "v")

    def st_B3(ci):
        if not ok(ci):
            return
        bi = ci % NB
        P.wait("t", f"a_amask_{ci}")
        P.wait("t", f"a_sk0_{ci}")
        if ci == 0:
            P.wait("t", "a_state0")
        if ci > 0:
            P.wait("t", f"a_zsb_{ci - 1}")
        bq = ci % NBQ
        for h_ in range(8):
            P.bare("t", lambda t, bi=bi, bq=bq, h_=h_: t.matmul(
                psW[0:64, h_ * 128:(h_ + 1) * 128], k_tm[bq][:, h_ * 64:(h_ + 1) * 64],
                a_sb[bi][:, h_ * 128:(h_ + 1) * 128], start=True, stop=False))
            P.bare("t", lambda t, bq=bq, h_=h_, ci=ci: t.matmul(
                psW[0:64, h_ * 128:(h_ + 1) * 128], sk_sb[ci % 2][0:64, h_ * 64:(h_ + 1) * 64],
                q_fm[bq][0:64, h_, :], start=False, stop=True))
        P.fence_inc("t", "t")
        P.mark(f"a_W_{ci}", "t")

    def st_wev(ci):
        if not ok(ci):
            return
        bi = ci % NB
        P.wait("v", f"a_W_{ci}")
        if ci >= NB:
            P.wait("v", f"a_wT_{ci - NB}")
            P.wait("v", f"a_B_{ci - NB}")
        P.bare("v", lambda v, bi=bi: v.tensor_copy(w_all[bi][:, 0:512], psW[:, 0:512]))
        P.bare("v", lambda v, bi=bi: v.tensor_copy(w_all[bi][:, 512:1024], psW[:, 512:1024]))
        if DEBUG and ci == DBG_CI:
            P.drain("v")
            P.bare("v", lambda v, bi=bi: v.tensor_copy(dbg_sb[0:64, 3584:4608], w_all[bi][:, :]))
        P.fence_inc("v", "v")
        P.mark(f"a_wev_{ci}", "v")

    def st_wT(ci):
        if not ok(ci):
            return
        bi = ci % NB
        P.wait("sy", f"a_wev_{ci}")
        if ci >= NB:
            P.wait("sy", f"a_skz_{ci - NB}")
        P.inc("sy", lambda h, bi=bi: h.dma_start_transpose(w_tm[bi][:, :, :], w_all[bi][:, :]),
              f"wT{bi}", 16)
        P.mark(f"a_wT_{ci}", f"wT{bi}")

    def st_B4(ci):
        if not ok(ci):
            return
        bi = ci % NB
        P.wait("t", f"a_wev_{ci}")
        bq = ci % NBQ
        for h_ in range(8):
            P.bare("t", lambda t, bi=bi, bq=bq, h_=h_: t.matmul(
                psAB[:, h_ * 128:(h_ + 1) * 128], w_all[bi][0:64, h_ * 128:(h_ + 1) * 128],
                q_fm[bq][0:64, h_, :], start=True, stop=True))
        P.fence_inc("t", "t")
        P.mark(f"a_B_{ci}", "t")

    def st_bmask(ci):
        if not ok(ci):
            return
        bi = ci % NB
        P.wait("v", f"a_B_{ci}")
        if ci >= NB:
            P.wait("v", f"a_O_{ci - NB}")
        P.bare("v", lambda v, bi=bi: v.tensor_tensor(b_sb[bi][:, 0:512], psAB[:, 0:512],
                                                     maskt[:, :], ALU.mult))
        P.bare("v", lambda v, bi=bi: v.tensor_tensor(b_sb[bi][:, 512:1024], psAB[:, 512:1024],
                                                     maskt[:, :], ALU.mult))
        P.fence_inc("v", "v")
        P.mark(f"a_bmask_{ci}", "v")

    def st_B5(ci):
        if not ok(ci):
            return
        bi = ci % NB
        P.wait("t", f"a_bmask_{ci}")
        P.wait("t", f"a_zsb_{ci - 1}")
        P.wait("t", f"a_wT_{ci}")
        if ci + 2 < NCH:
            P.wait("t", f"a_qkvev_{ci + 2}")
        bq = ci % NBQ
        for h_ in range(8):
            P.bare("t", lambda t, bi=bi, bq=bq, h_=h_: t.matmul(
                psVO[:, h_ * 64:(h_ + 1) * 64], b_sb[bi][:, h_ * 128:(h_ + 1) * 128],
                v_tm[bq][:, h_ * 64:(h_ + 1) * 64], start=True, stop=False))
            P.bare("t", lambda t, bq=bq, h_=h_: t.matmul(
                psVO[:, h_ * 64:(h_ + 1) * 64], q_fm[bq][0:64, h_, :],
                z_sb[0:64, h_ * 64:(h_ + 1) * 64], start=False, stop=True))
        for h_ in range(8):
            P.bare("t", lambda t, bi=bi, bq=bq, h_=h_: t.matmul(
                psW[0:64, h_ * 64:(h_ + 1) * 64], w_tm[bi][:, h_, :],
                v_tm[bq][:, h_ * 64:(h_ + 1) * 64], start=True, stop=True))
        P.fence_inc("t", "t")
        P.mark(f"a_O_{ci}", "t")
        P.mark(f"a_skz_{ci}", "t")

    def st_oev_z(ci):
        if not ok(ci):
            return
        bi = ci % NBO
        sp = ci % 2
        sl = ci % NSL
        first_half = ci < NSL
        P.wait("v", f"a_O_{ci}")
        if ci >= NBO:
            P.wait("v", f"a_ost_{ci - NBO}")
        if first_half:
            P.bare("v", lambda v, sl=sl: v.tensor_scalar(
                o_my[:, sl * 512:(sl + 1) * 512], psVO[:, :], sel1, None, ALU.mult))
            P.bare("v", lambda v, bi=bi: v.tensor_scalar(
                o_st[bi][:, :], psVO[:, :], selx1, None, ALU.mult))
        else:
            P.bare("v", lambda v, sl=sl: v.scalar_tensor_tensor(
                o_my[:, sl * 512:(sl + 1) * 512], psVO[:, :], sel2,
                o_my[:, sl * 512:(sl + 1) * 512], ALU.mult, ALU.add))
            P.bare("v", lambda v, bi=bi: v.tensor_scalar(
                o_st[bi][:, :], psVO[:, :], selx2, None, ALU.mult))
        if DEBUG and ci == DBG_CI:
            P.drain("v")
            P.bare("v", lambda v: v.tensor_copy(dbg_sb[:, 4608:5120], psVO[:, :]))
            if DBG_CI > 0:
                P.bare("v", lambda v, ci=ci: v.tensor_copy(dbg_sb[0:64, 5120:5632], sk_sb[ci % 2][:, :]))
                P.bare("v", lambda v: v.tensor_copy(dbg_sb[0:64, 6144:6656], z_sb[:, :]))
                P.bare("v", lambda v: v.tensor_copy(dbg_sb[0:64, 6656:7168], z_f32[:, :]))
        P.fence_inc("v", "v")
        P.mark(f"a_oev_{ci}", "v")
        P.wait("g", f"a_oev_{ci}")
        if first_half:
            P.inc("g", lambda g, bi=bi, sl=sl: g.dma_start(
                out=T["cc_o_in"][sl * 128:(sl + 1) * 128, :], in_=o_st[bi][:, :]),
                f"ost{sp}", 16)
        else:
            P.wait("g", f"a_ost_{ci - NSL}")
            P.inc("g", lambda g, bi=bi, sl=sl: g.dma_start(
                out=T["cc_o_in"][sl * 128:(sl + 1) * 128, :], in_=o_st[bi][:, :],
                accum_op=ALU.add), f"ost{sp}", 16)
        P.mark(f"a_ost_{ci}", f"ost{sp}")
        P.wait("v", f"a_skz_{ci}")
        if ci == 0:
            P.bare("v", lambda v: v.tensor_copy(z_f32[:, :], psW[0:64, 0:512]))
        else:
            P.bare("v", lambda v: v.tensor_tensor(z_f32[:, :], z_f32[:, :],
                                                  psW[0:64, 0:512], ALU.add))
        P.drain("v")
        P.bare("v", lambda v: v.tensor_copy(z_sb[:, :], z_f32[:, :]))
        P.fence_inc("v", "v")
        P.mark(f"a_zsb_{ci}", "v")

    st_front2(0)
    st_B1(0)
    st_qkvev(0)
    st_B1(1)
    st_qkvev(1)
    st_qkT(0)
    st_B2(0)
    st_amask_skadd(0)
    st_qkT(1)
    w_ip = [min(10 + 4 * j, NCH - 1) for j in range(4)]
    w_tp = [0]
    for i in range(NCH):
        if (i + 2) % 2 == 0:
            st_front2(i + 2)
        st_B3(i)
        st_wev(i)
        st_B1(i + 2)
        st_qkvev(i + 2)
        st_B4(i)
        st_bmask(i)
        st_B2(i + 1)
        st_amask_skadd(i + 1)
        st_wT(i)
        st_B5(i)
        st_oev_z(i)
        while w_tp[0] < 4 and i >= w_ip[w_tp[0]]:
            T["w_tails"][w_tp[0]]()
            w_tp[0] += 1
        for k in range(NQ - 1):
            if i == NSL + (NSL // NQ) * (k + 1) - 1:
                P.wait("g", f"a_ost_{i - 1}")
                P.wait("g", f"a_ost_{i}")
                qr = T_HALF // NQ
                P.inc("g", lambda g, k=k, qr=qr: g.collective_compute(
                    "AllGather", ALU.bypass, replica_groups=[list(range(N_CORES))],
                    ins=[T["cc_o_in"][k * qr:(k + 1) * qr, :].opt()],
                    outs=[T["cc_o_out_q"][k].opt()]), "cc", 1)
                P.mark(f"o_ag{k}", "cc")
        st_qkT(i + 2)

    if DEBUG:
        P.wait_now("g", "ost0")
        P.wait_now("g", "ost1")
        P.inc("g", lambda g: g.dma_start(out=T["dbg_d"][:, :], in_=dbg_sb[:, :]), "g", 16)
        P.wait_now("g", "g")
    P.fence_nop("t", "t")
    P.mark("a_end_t", "t")
    P.fence_nop("v", "v")
    P.mark("a_end_v", "v")
    P.fence_nop("s", "s")
    P.mark("a_end_s", "s")

    P.wait_now("g", "ost0")
    P.wait_now("g", "ost1")
    qr = T_HALF // NQ
    k = NQ - 1
    P.inc("g", lambda g, k=k: g.collective_compute(
        "AllGather", ALU.bypass, replica_groups=[list(range(N_CORES))],
        ins=[T["cc_o_in"][k * qr:(k + 1) * qr, :].opt()],
        outs=[T["cc_o_out_q"][k].opt()]), "cc", 1)
    P.mark(f"o_ag{k}", "cc")


def build_b(nc, P, bstk, T):
    consts, o_my, xq_res = T["consts"], T["o_my"], T["xq_res"]
    pp = T["pp"]
    NB = 2

    def bsb(name, shape, dt, **kw):
        return bstk.enter_context(nc.sbuf_tensor(name, shape, dt, **kw))

    wo_sb = bsb("wo_sb", [128, 8, 1024], BF16)
    o_pt = [bsb(f"o_pt{i}", [128, 512], BF16) for i in range(NB)]
    om_fm = [bsb(f"om_fm{i}", [128, 4, 128], BF16, is_transpose_dest=True) for i in range(NB)]
    op_fm = [bsb(f"op_fm{i}", [128, 4, 128], BF16, is_transpose_dest=True) for i in range(NB)]
    r1t = [bsb(f"r1t{i}", [128, D], F32) for i in range(NB)]
    ln2a = [bsb(f"ln2a{i}", [128, BQ * 1024], BF16) for i in range(2)]
    xq_tm = bsb("xq_tm", [128, BQ * 1024], BF16)
    BH = BQ * 512
    if NSL * 512 >= 2 * BH * 4:
        tA = o_my[:, 0:2 * BH]
        tA_in_omy = True
    else:
        tA = bsb("tA_b", [128, 2 * BH], BF16)[:, :]
        tA_in_omy = False
    tC = bsb("tC_b", [128, 2 * BH], BF16)
    tD = bsb("tD_b", [128, 2 * BH], BF16)
    tE = bsb("tE_b", [128, 2 * BH], BF16)
    smlB = [bsb(f"smlB{i}", [128, 8 * BQ * 64], F32) for i in range(2)]
    s8B = [bsb(f"s8B{i}", [128, 2 * BQ * 64], F8) for i in range(2)]
    bnsB = bsb("bnsB", [128, 12], F32)
    mvB = bsb("mvB", [128, 2], F32)
    stdB = bsb("stdB", [128, 1], F32)
    invB = bsb("invB", [128, 1], F32)
    negmB = bsb("negmB", [128, 1], F32)
    psWo = [bstk.enter_context(nc.psum_tensor(f"psWo{i}", [128, 1024], F32)) for i in range(2)]

    for eng in ("sy", "g", "s"):
        P.wait(eng, "a_end_t")
        P.wait(eng, "a_end_v")
        P.wait(eng, "a_end_s")
        for sem in ("xld0", "xld1", "lnT0", "lnT1", "qkT0", "qkT1", "qkT2", "wT0", "wT1",
                    "ost0", "ost1", "g"):
            P.wait_now(eng, sem)
    P.inc("g", lambda g: g.dma_start(
        out=wo_sb[:, :, :], in_=T["woi"].rearrange("(m p) c -> p m c", p=128)), "g", 16)
    P.mark("b_wold", "g")

    pendB = [None]

    def flushb():
        if pendB[0] is None:
            return
        stB, tt0 = pendB[0]
        pendB[0] = None
        if tA_in_omy:
            nsl_used = (2 * BH) // 512
            gt = min(nsl_used - 1, tt0)
            P.wait("v", f"b_oT_{gt}")
            if gt >= 1:
                P.wait("v", f"b_oT_{gt - 1}")
        if tt0 >= BQ:
            P.wait("v", f"b_xqT_{tt0 - BQ}")
        emit_quant_multi_b(P, stB)
        P.fence_inc("v", "v")
        P.mark(f"b_q_{tt0}", "v")
        P.wait("sy", f"b_q_{tt0}")
        for i4 in range(BQ):
            tq = tt0 - (BQ - 1) + i4
            P.inc("sy", lambda h, i4=i4, tq=tq: h.dma_start_transpose(
                xq_res[:, tq * 1024:(tq + 1) * 1024].rearrange(
                    "p (m r) -> p m r", r=128),
                xq_tm[:, i4 * 1024:(i4 + 1) * 1024]), "xqT", 16)
        P.mark(f"b_xqT_{tt0}", "xqT")

    for pt in range(NTT // 2):
        tpair = (2 * pt, 2 * pt + 1)
        for tt in tpair:
            bi = tt % NB
            if tt >= NB:
                P.wait("sy", f"b_r1sp_{tt - NB}")
                P.wait("sy", f"bln{tt - NB}_ln")
            P.inc("sy", lambda h, tt=tt, bi=bi: h.dma_start(
                out=r1t[bi][:, :], in_=T["x_half"][tt * 128:(tt + 1) * 128, :]), f"xld{bi}", 16)
            P.mark(f"b_xld_{tt}", f"xld{bi}")

        for tt in tpair:
            bi = tt % NB
            qk = tt * NQ // NTT
            P.wait("g", f"o_ag{qk}")
            if tt >= NB:
                P.wait("g", f"b_oT_{tt - NB}")
            P.inc("g", lambda g, tt=tt, bi=bi, qk=qk: g.indirect_dma_start(
                o_pt[bi][:, :], None, T["cc_o_out_q"][qk][:, :],
                IndirectOffsetOnAxis(T["gidx"][:, tt:tt + 1], 0)), f"og{bi}", 16)
            P.mark(f"b_og_{tt}", f"og{bi}")

        for tt in tpair:
            bi = tt % NB
            P.wait("sy", f"b_og_{tt}")
            if tt >= NB:
                P.wait("sy", f"b_wo_{tt - NB}")
            P.inc("sy", lambda h, tt=tt, bi=bi: h.dma_start_transpose(
                om_fm[bi][:, :, :], o_my[:, tt * 512:(tt + 1) * 512]), f"oT{bi}", 16)
            P.inc("sy", lambda h, bi=bi: h.dma_start_transpose(op_fm[bi][:, :, :], o_pt[bi][:, :]),
                  f"oT{bi}", 16)
            P.mark(f"b_oT_{tt}", f"oT{bi}")

        for tt in tpair:
            bi = tt % NB
            P.wait("t", f"b_oT_{tt}")
            if tt == 0:
                P.wait("t", "b_wold")
            if tt >= 2:
                P.wait("t", f"b_r1_{tt - 2}")
            for dh in range(2):
                for j in range(4):
                    P.bare("t", lambda t, bi=bi, j=j, dh=dh: t.matmul(
                        psWo[bi][:, dh * 512:(dh + 1) * 512], om_fm[bi][:, j, :],
                        wo_sb[:, j, dh * 512:(dh + 1) * 512], start=(j == 0), stop=False))
                for j in range(4):
                    P.bare("t", lambda t, bi=bi, j=j, dh=dh: t.matmul(
                        psWo[bi][:, dh * 512:(dh + 1) * 512], op_fm[bi][:, j, :],
                        wo_sb[:, 4 + j, dh * 512:(dh + 1) * 512], start=False, stop=(j == 3)))
            P.fence_inc("t", "t")
            P.mark(f"b_wo_{tt}", "t")

        for tt in tpair:
            bi = tt % NB
            P.wait("v", f"b_wo_{tt}")
            P.wait("v", f"b_xld_{tt}")
            P.bare("v", lambda v, bi=bi: v.tensor_tensor(r1t[bi][:, 0:512], psWo[bi][:, 0:512],
                                                         r1t[bi][:, 0:512], ALU.add))
            P.bare("v", lambda v, bi=bi: v.tensor_tensor(r1t[bi][:, 512:1024], psWo[bi][:, 512:1024],
                                                         r1t[bi][:, 512:1024], ALU.add))
            P.fence_inc("v", "v")
            P.mark(f"b_r1_{tt}", "v")

        for tt in tpair:
            bi = tt % NB
            P.wait("g", f"b_r1_{tt}")
            P.inc("g", lambda g, tt=tt, bi=bi: g.dma_start(
                out=T["r1_sp"][tt * 128:(tt + 1) * 128, :], in_=r1t[bi][:, :]), f"r1s{bi}", 16)
            P.mark(f"b_r1sp_{tt}", f"r1s{bi}")

        for tt in tpair:
            bi = tt % NB
            P.wait("v", f"b_r1_{tt}")
            P.drain("v")
            sl4 = tt % BQ
            emit_ln(P, f"bln{tt}", r1t[bi][:, :],
                    ln2a[(tt // BQ) % 2][:, sl4 * 1024:(sl4 + 1) * 1024],
                    bnsB[:, :], mvB[:, :], stdB[:, :], invB[:, :], negmB[:, :], consts[:, 0:1],
                    f"bln{tt - 1}_ln" if tt > 0 else None)
            P.mark(f"b_ln_{tt}", "v")

            if sl4 == BQ - 1:
                kq = tt // BQ
                P.wait("v", f"b_ln_{tt}")
                stB = emit_quant_multi_a(P, [qstream(
                    pp, f"bq{tt}_{si}", ln2a[kq % 2][:, si * BH:(si + 1) * BH],
                    xq_tm[:, si * BH:(si + 1) * BH], BH, 3,
                    tA[:, si * BH:(si + 1) * BH], tC[:, si * BH:(si + 1) * BH],
                    tD[:, si * BH:(si + 1) * BH], tE[:, si * BH:(si + 1) * BH],
                    smlB[kq % 2][:, si * 8 * BQ * 32:(si + 1) * 8 * BQ * 32],
                    s8B[kq % 2][:, si * 2 * BQ * 32:(si + 1) * 2 * BQ * 32])
                    for si in range(2)])
                flushb()
                pendB[0] = (stB, tt)
    flushb()
    P.mark("b_done", "xqT")
    if DEBUG:
        P.wait("g", "b_done")
        P.inc("g", lambda g: g.dma_start(out=T["dbg2_d"][:, :], in_=xq_res[:, :]), "g", 16)
        P.wait_now("g", "g")
    P.fence_nop("t", "t")
    P.mark("b_end_t", "t")
    P.fence_nop("v", "v")
    P.mark("b_end_v", "v")
    P.fence_nop("s", "s")
    P.mark("b_end_s", "s")


def build_fa(nc, P, fstk, T):
    pp, xq_res = T["pp"], T["xq_res"]

    def fsb(name, shape, dt, **kw):
        return fstk.enter_context(nc.sbuf_tensor(name, shape, dt, **kw))

    wqg = [fsb(f"wqg{i}", [128, 8, 512], BF16) for i in range(2)]
    wqu = [fsb(f"wqu{i}", [128, 8, 512], BF16) for i in range(2)]
    h_b = [fsb(f"h_b{i}", [128, FTB * 512], BF16) for i in range(4)]
    hq_b = [fsb(f"hq_b{i}", [128, FTB * 512], BF16) for i in range(4)]
    sg_t = [fsb(f"sg_t{i}", [128, 512], BF16) for i in range(2)]
    up_t = [fsb(f"up_t{i}", [128, 512], BF16) for i in range(2)]
    tA = [fsb(f"tA_f{i}", [128, FTB * 512], BF16) for i in range(2)]
    if NSL * 512 >= 6 * FTB * 512:
        o_my = T["o_my"]
        tD = [o_my[:, i * FTB * 512:(i + 1) * FTB * 512] for i in range(2)]
        tE = [o_my[:, (2 + i) * FTB * 512:(3 + i) * FTB * 512] for i in range(2)]
        tC = [o_my[:, (4 + i) * FTB * 512:(5 + i) * FTB * 512] for i in range(2)]
    else:
        tD = [fsb(f"tD_f{i}", [128, FTB * 512], BF16)[:, :] for i in range(2)]
        tE = [fsb(f"tE_f{i}", [128, FTB * 512], BF16)[:, :] for i in range(2)]
        tC = [fsb(f"tC_fx{i}", [128, FTB * 512], BF16)[:, :] for i in range(2)]
    smlF = [fsb(f"smlF{i}", [128, 8 * FTB * 32], F32) for i in range(4)]
    s8F = [fsb(f"s8F{i}", [128, 2 * FTB * 32], F8) for i in range(4)]
    hq_fm = [fsb(f"hq_fm{i}", [128, 4, 128], BF16, is_transpose_dest=True) for i in range(2)]
    psG = [fstk.enter_context(nc.psum_tensor(f"psG{i}", [128, 512], F32)) for i in range(2)]
    psU = [fstk.enter_context(nc.psum_tensor(f"psU{i}", [128, 512], F32)) for i in range(2)]

    P.wait("sy", "w_ag")
    P.wait("sy", "b_done")
    P.wait("t", "b_done")
    for eng in ("sy", "g", "s"):
        P.wait(eng, "b_end_t")
        P.wait(eng, "b_end_v")
        P.wait(eng, "b_end_s")
        for sem in ("xld0", "xld1", "og0", "og1", "oT0", "oT1", "r1s0", "r1s1",
                    "xqT", "g"):
            P.wait_now(eng, sem)
    tr_g = [0]
    pend = [None]

    def flushq():
        if pend[0] is None:
            return
        stq, ic0, pb0 = pend[0]
        pend[0] = None
        for tb0 in (pb0 * 2, pb0 * 2 + 1):
            gb0 = ic0 * NTB + tb0
            if gb0 >= 4:
                gq = gb0 - 4
                P.wait("v", f"f_hqT_{gq // NTB}_{gq % NTB}")
        emit_quant_multi_b(P, stq)
        P.fence_inc("v", "v")
        P.mark(f"f_q_{ic0}_{pb0}", "v")
        P.wait("sy", f"f_q_{ic0}_{pb0}")
        for tb0 in (pb0 * 2, pb0 * 2 + 1):
            qb = (ic0 * NTB + tb0) % 4
            for ti in range(FTB):
                tt = tb0 * FTB + ti
                fp = tr_g[0] % 2
                if tr_g[0] >= 2:
                    P.wait("sy", f"f_hqst_g{tr_g[0] - 2}")
                P.inc("sy", lambda h, qb=qb, ti=ti, fp=fp: h.dma_start_transpose(
                    hq_fm[fp][:, :, :], hq_b[qb][:, ti * 512:(ti + 1) * 512]), f"hqT{fp}", 16)
                P.mark(f"f_hqTr_g{tr_g[0]}", f"hqT{fp}")
                P.wait("g", f"f_hqTr_g{tr_g[0]}")
                def mk_st(g, tt=tt, ic0=ic0, fp=fp):
                    dst = T["hq_sp"][tt, :, ic0 * 4:(ic0 + 1) * 4, :]
                    return g.dma_start(out=dst, in_=hq_fm[fp][:, :, :])
                P.inc("g", mk_st, f"hqs{fp}", 16)
                P.mark(f"f_hqst_g{tr_g[0]}", f"hqs{fp}")
                tr_g[0] += 1
            P.mark(f"f_hqT_{ic0}_{tb0}", f"hqs{(tr_g[0] - 1) % 2}")

    def stream_w(ic):
        if ic >= NIC:
            return
        wb = ic % 2
        if ic >= 2:
            P.wait("s", f"f_gu_{ic - 2}")
        for wsb_t, woff in ((wqg[wb], 0), (wqu[wb], SHW)):
            def mk_ld(h, wsb_t=wsb_t, woff=woff, ic=ic):
                src = T["cc_wq_out"][ic * 3 * SHW + woff: ic * 3 * SHW + woff + SHW].rearrange(
                    "(m p r) -> p m r", p=128, r=512)
                return h.dma_start(out=wsb_t[:, :, :], in_=src)
            P.inc("s", mk_ld, f"fw{ic % 2}", 16)
        P.mark(f"f_wld_{ic}", f"fw{ic % 2}")

    stream_w(0)
    for ic in range(NIC):
        wb = ic % 2
        stream_w(ic + 1)

        for pb in range(NTB // 2):
            streams = []
            gp2 = (ic * (NTB // 2) + pb) % 2
            for si in range(2):
                tb = pb * 2 + si
                gb = ic * NTB + tb
                hb = gb % 4
                for ti in range(FTB):
                    tt = tb * FTB + ti
                    tp = tt % 2
                    P.wait("t", f"f_wld_{ic}")
                    if not (ic == 0 and tt < 2):
                        prev = (ic, tt - 2) if tt >= 2 else (ic - 1, NTT - 2 + tt)
                        P.wait("t", f"f_h_{prev[0]}_{prev[1]}")
                    for m in range(8):
                        P.bare("t", lambda t, tt=tt, m=m, wb=wb, tp=tp: t.matmul(
                            psG[tp][:, :], xq_res[:, tt * 1024 + m * 128: tt * 1024 + (m + 1) * 128],
                            wqg[wb][:, m, :], start=(m == 0), stop=(m == 7)))
                    for m in range(8):
                        P.bare("t", lambda t, tt=tt, m=m, wb=wb, tp=tp: t.matmul(
                            psU[tp][:, :], xq_res[:, tt * 1024 + m * 128: tt * 1024 + (m + 1) * 128],
                            wqu[wb][:, m, :], start=(m == 0), stop=(m == 7)))
                    P.fence_inc("t", "t")
                    P.mark(f"f_gu_t_{ic}_{tt}", "t")
                    if tt == NTT - 1:
                        P.mark(f"f_gu_{ic}", "t")
                    P.wait("s", f"f_gu_t_{ic}_{tt}")
                    if not (ic == 0 and tt < 2):
                        prev = (ic, tt - 2) if tt >= 2 else (ic - 1, NTT - 2 + tt)
                        P.wait("s", f"f_h_{prev[0]}_{prev[1]}")
                    P.bare("s", lambda s, tp=tp: s.activation(sg_t[tp][:, :], psG[tp][:, :], AF.Silu))
                    P.drain("s")
                    P.inc("s", lambda s, tp=tp: s.copy(up_t[tp][:, :], psU[tp][:, :]), "s", 1)
                    P.mark(f"f_sg_{ic}_{tt}", "s")
                    P.wait("v", f"f_sg_{ic}_{tt}")
                    P.bare("v", lambda v, tp=tp, hb=hb, ti=ti: v.tensor_tensor(
                        h_b[hb][:, ti * 512:(ti + 1) * 512], sg_t[tp][:, :], up_t[tp][:, :], ALU.mult))
                    P.fence_inc("v", "v")
                    P.mark(f"f_h_{ic}_{tt}", "v")
                    if ti > 0:
                        am_sl = smlF[2 * gp2 + si][:, (ti - 1) * 32:ti * 32]
                        h3 = h_b[hb][:, (ti - 1) * 512:ti * 512].rearrange(
                            "p (b s) -> p b s", s=16)
                        P.bare("v", lambda v, am_sl=am_sl, h3=h3: v.tensor_reduce(
                            am_sl, h3, AX.X, ALU.max, apply_absolute_value=True))
                sd = qstream(pp, f"fq{ic}_{tb}", h_b[hb][:, :],
                             hq_b[gb % 4][:, :], FTB * 512, 4,
                             tA[si][:, :], tC[si], tD[si], tE[si],
                             smlF[2 * gp2 + si][:, :], s8F[2 * gp2 + si][:, :])
                sd["pre_amax"] = ((FTB - 1) * 32, FTB * 32,
                                  h_b[hb][:, (FTB - 1) * 512:FTB * 512].rearrange(
                                      "p (b s) -> p b s", s=16))
                streams.append(sd)
            stq = emit_quant_multi_a(P, streams)
            flushq()
            pend[0] = (stq, ic, pb)
    flushq()
    P.fence_nop("g", "gc")
    P.mark("fa_done", "gc")
    P.fence_nop("t", "t")
    P.mark("fa_end_t", "t")
    P.fence_nop("v", "v")
    P.mark("fa_end_v", "v")
    P.fence_nop("s", "s")
    P.mark("fa_end_s", "s")


def build_fb(nc, P, f2stk, T):
    def fsb(name, shape, dt, **kw):
        return f2stk.enter_context(nc.sbuf_tensor(name, shape, dt, **kw))

    wqd = fsb("wqd", [128, 4, NIC, 1024], BF16)
    hq_i = [fsb(f"hq_i{i}", [128, 32, 128], BF16) for i in range(2)]
    r1t = [fsb(f"r1tf{i}", [128, D], F32) for i in range(2)]
    o_t = [fsb(f"o_tf{i}", [128, D], F32) for i in range(2)]
    psD = [f2stk.enter_context(nc.psum_tensor(f"psD{i}", [128, 1024], F32)) for i in range(2)]

    P.wait("sy", "fa_done")
    P.wait_now("sy", "hqs0")
    P.wait_now("sy", "hqs1")
    for eng in ("sy", "g", "s"):
        P.wait(eng, "fa_end_t")
        P.wait(eng, "fa_end_v")
        P.wait(eng, "fa_end_s")
        for sem in ("fw0", "fw1", "hqT0", "hqT1", "hqs0", "hqs1", "g"):
            P.wait_now(eng, sem)
    for ic in range(NIC):
        def mk_ld(h, ic=ic):
            src = T["cc_wq_out"][ic * 3 * SHW + 2 * SHW: ic * 3 * SHW + 3 * SHW].rearrange(
                "(su p r) -> p su r", p=128, r=1024)
            return h.dma_start(out=wqd[:, :, ic, :], in_=src)
        P.inc("s", mk_ld, "fw0", 16)
    P.mark("fb_wld", "fw0")

    for tt in range(NTT):
        bi = tt % 2
        if tt >= 2:
            P.wait("sy", f"fb_mm_{tt - 2}")
        P.inc("sy", lambda h, tt=tt, bi=bi: h.dma_start(
            out=hq_i[bi][:, :, :], in_=T["hq_sp"][tt, :, :, :]), f"fbl{bi}", 16)
        P.inc("sy", lambda h, tt=tt, bi=bi: h.dma_start(
            out=r1t[bi][:, :], in_=T["r1_sp"][tt * 128:(tt + 1) * 128, :]), f"fbl{bi}", 16)
        P.mark(f"fb_ld_{tt}", f"fbl{bi}")

        P.wait("t", f"fb_ld_{tt}")
        if tt == 0:
            P.wait("t", "fb_wld")
        if tt >= 2:
            P.wait("t", f"fb_oev_{tt - 2}")
        for dh in range(2):
            n = 0
            for ic in range(NIC):
                for su in range(4):
                    n += 1
                    P.bare("t", lambda t, bi=bi, ic=ic, su=su, dh=dh, n=n: t.matmul(
                        psD[bi][:, dh * 512:(dh + 1) * 512], hq_i[bi][:, ic * 4 + su, :],
                        wqd[:, su, ic, dh * 512:(dh + 1) * 512],
                        start=(n == 1), stop=(n == 32)))
        P.fence_inc("t", "t")
        P.mark(f"fb_mm_{tt}", "t")

        P.wait("v", f"fb_mm_{tt}")
        P.bare("v", lambda v, bi=bi: v.tensor_tensor(o_t[bi][:, 0:512], psD[bi][:, 0:512],
                                                     r1t[bi][:, 0:512], ALU.add))
        P.bare("v", lambda v, bi=bi: v.tensor_tensor(o_t[bi][:, 512:1024], psD[bi][:, 512:1024],
                                                     r1t[bi][:, 512:1024], ALU.add))
        P.fence_inc("v", "v")
        P.mark(f"fb_oev_{tt}", "v")

        P.wait("sy", f"fb_oev_{tt}")
        P.inc("sy", lambda h, tt=tt, bi=bi: h.dma_start(
            out=T["out_half"][tt * 128:(tt + 1) * 128, :], in_=o_t[bi][:, :]), "sy", 16)
        P.mark(f"fb_out_{tt}", "sy")

    P.wait_now("sy", "sy")


def _host_prepare(inputs):
    x = np.ascontiguousarray(inputs["L0_attn_x"])
    Wq, Wk, Wv, Wo = (inputs["L0_attn_W_q"], inputs["L0_attn_W_k"],
                      inputs["L0_attn_W_v"], inputs["L0_attn_W_o"])
    gw, uw, dw = inputs["L0_ffn_gate_w"], inputs["L0_ffn_up_w"], inputs["L0_ffn_down_w"]
    s_in = float(np.asarray(inputs["L0_ffn_s_in"]).reshape(-1)[0])
    s_ind = float(np.asarray(inputs["L0_ffn_s_in_down"]).reshape(-1)[0])

    triu = np.triu(np.ones((128, 128), np.float32))
    mask512 = np.tile(triu, (1, 4)).astype(np.float32)

    in_maps = []
    for c in range(N_CORES):
        b, p = c // 2, c % 2
        hs = 512 * p
        scal = np.zeros((128, 8), np.float32)
        scal[:, 0] = 1e-5
        scal[:, 1] = s_in
        scal[:, 2] = s_ind
        scal[:, 3] = 1.0 - p
        scal[:, 4] = float(p)
        scal[:, 5] = float(p)
        scal[:, 6] = 1.0 - p
        partner = c ^ 1
        ntt = T_HALF // 128
        qr = T_HALF // NQ
        s_idx = np.arange(T_HALF)
        local = s_idx % qr
        gidx_v = (partner * qr + local).astype(np.int32).reshape(ntt, 128).T
        gidx_v = np.ascontiguousarray(gidx_v)
        wo_arr = np.concatenate([Wo[hs:hs + 512, :], Wo[512 - hs:512 - hs + 512, :]], axis=0)
        in_maps.append({
            "x_full": np.ascontiguousarray(x[b]),
            "x_half": np.ascontiguousarray(x[b, p * T_HALF:(p + 1) * T_HALF]),
            "wqi": np.ascontiguousarray(Wq[:, hs:hs + 512]),
            "wki": np.ascontiguousarray(Wk[:, hs:hs + 512]),
            "wvi": np.ascontiguousarray(Wv[:, hs:hs + 512]),
            "woi": np.ascontiguousarray(wo_arr),
            "gshi": np.ascontiguousarray(gw[c * 512:(c + 1) * 512, :]),
            "ushi": np.ascontiguousarray(uw[c * 512:(c + 1) * 512, :]),
            "dshi": np.ascontiguousarray(dw[:, c * 512:(c + 1) * 512]),
            "maski": mask512,
            "scali": scal,
            "gidxi": gidx_v,
        })
    return in_maps


def kernel(**inputs):
    in_maps = _host_prepare(inputs)
    nc = build_nc()
    res = run_bass_kernel_spmd(nc, in_maps, core_ids=list(range(N_CORES)))
    out = np.zeros((4, T_FULL, D), np.float32)
    for c in range(N_CORES):
        b, p = c // 2, c % 2
        out[b, p * T_HALF:(p + 1) * T_HALF, :] = np.asarray(res.results[c]["out_half"])
    return out
